# version 18
# baseline (speedup 1.0000x reference)
import contextlib
import numpy as np
import ml_dtypes
import concourse.bass as bass
import concourse.bacc as bacc
import concourse.mybir as mybir
from concourse.bass_utils import run_bass_kernel_spmd

F32 = mybir.dt.float32
BF16 = mybir.dt.bfloat16
I32 = mybir.dt.int32
U32 = mybir.dt.uint32
AF = mybir.ActivationFunctionType
ALU = mybir.AluOpType
AX = mybir.AxisListType

NCORES = 8
TOK = 2048
D = 1024
DI = 2048
DSTATE = 16
DTRANK = 64
DCONV = 4
IFF = 4096
NEXP = 4096
NKEY = 64
HEADS_P = 4
AH = 16
AG = 4
AE = 64
LN_EPS = 1e-5
RMS_EPS = 1e-6
ACT_EPS = 1e-5
BIG = 1e30

bfd = ml_dtypes.bfloat16

_ENGINES = ("sync", "vector", "scalar", "tensor", "gpsimd")


class Buf:
    __slots__ = ("t", "wtok", "rtoks", "name")

    def __init__(self, t, name):
        self.t = t
        self.name = name
        self.wtok = None
        self.rtoks = []

    def __getitem__(self, key):
        return self.t[key]

    @property
    def ap(self):
        return self.t[:] if hasattr(self.t, "__enter__") or not hasattr(self.t, "ap") else self.t


class BufView:
    def __init__(self, parent, ap):
        self.parent = parent
        self.t = ap
        self.name = parent.name + "_view"

    def __getitem__(self, key):
        return self.t[key]

    @property
    def wtok(self):
        return self.parent.wtok

    @wtok.setter
    def wtok(self, v):
        self.parent.wtok = v

    @property
    def rtoks(self):
        return self.parent.rtoks

    @rtoks.setter
    def rtoks(self, v):
        self.parent.rtoks = v


class Em:

    def __init__(self, name, rd=True):
        self.nc = bacc.Bacc("TRN2", target_bir_lowering=False, debug=False,
                            detect_race_conditions=rd)
        self.ctx = contextlib.ExitStack()
        self.name = name
        self.out_names = []
        self.sems = {}
        self.cnt = {}
        for e in _ENGINES:
            self.sems[e] = self.ctx.enter_context(self.nc.semaphore("sem_" + e))
            self.cnt[e] = 0
        self.ndma = 20
        for i in range(self.ndma):
            self.sems[f"dma{i}"] = self.ctx.enter_context(self.nc.semaphore(f"sem_dma{i}"))
            self.cnt[f"dma{i}"] = 0
        self._dma_rr = 0
        self.ngdma = 8
        for i in range(self.ngdma):
            self.sems[f"gdma{i}"] = self.ctx.enter_context(self.nc.semaphore(f"sem_gdma{i}"))
            self.cnt[f"gdma{i}"] = 0
        self._gdma_rr = 0
        self._waited = {e: {} for e in _ENGINES}
        self._cbuf = None
        self.const_vals = []
        self._scopes = []
        cd = self.din("consts_arr", (1, 32), F32)
        self._cbuf_pending = cd
        self._bar = None
        self.nbar = 0

    def din(self, name, shape, dtype):
        return Buf(self.nc.dram_tensor(name, list(shape), dtype, kind="ExternalInput").ap(), name)

    def dout(self, name, shape, dtype):
        self.out_names.append(name)
        return Buf(self.nc.dram_tensor(name, list(shape), dtype, kind="ExternalOutput").ap(), name)

    def dtmp(self, name, shape, dtype):
        return Buf(self.nc.dram_tensor(name, list(shape), dtype, kind="Internal").ap(), name)

    def sb(self, name, shape, dtype):
        st = self._scopes[-1] if self._scopes else self.ctx
        self._uctr = getattr(self, "_uctr", 0) + 1
        name = f"{name}_u{self._uctr}"
        return Buf(st.enter_context(self.nc.sbuf_tensor(name, list(shape), dtype)), name)

    def ps(self, name, shape, dtype=F32):
        st = self._scopes[-1] if self._scopes else self.ctx
        self._uctr = getattr(self, "_uctr", 0) + 1
        name = f"{name}_u{self._uctr}"
        return Buf(st.enter_context(self.nc.psum_tensor(name, list(shape), dtype)), name)

    def push(self):
        self._scopes.append(contextlib.ExitStack())

    def pop(self):
        self.fence()
        self._scopes.pop().close()

    def _wait(self, engine, tok):
        if tok is None:
            return
        sem, n = tok
        key = id(sem)
        w = self._waited[engine]
        if w.get(key, -1) >= n:
            return
        getattr(self.nc, engine).wait_ge(sem, n)
        w[key] = n

    SKIP_SAME_ENGINE = ("tensor",)

    def _skip(self, engine, tok):
        if tok[0] is self.sems.get(engine):
            return engine in self.SKIP_SAME_ENGINE
        return False

    def _pre(self, engine, reads, writes, war=True):
        for b in reads:
            if b is not None and b.wtok is not None:
                tok = b.wtok
                if self._skip(engine, tok):
                    continue
                self._wait(engine, tok)
        for b in writes:
            if b is None:
                continue
            if b.wtok is not None:
                tok = b.wtok
                if not self._skip(engine, b.wtok):
                    self._wait(engine, b.wtok)
            if war:
                for tok in b.rtoks:
                    if self._skip(engine, tok):
                        continue
                    self._wait(engine, tok)

    def _post(self, engine, inst, reads, writes, sem_name=None, inc=1):
        sem_name = sem_name or engine
        sem = self.sems[sem_name]
        inst.then_inc(sem, inc)
        self.cnt[sem_name] += inc
        tok = (sem, self.cnt[sem_name])
        for b in writes:
            if b is not None:
                b.wtok = tok
                b.rtoks = []
        for b in reads:
            if b is not None:
                b.rtoks.append(tok)
        return tok

    def cconst(self, val, rows=128):
        val = float(val)
        if self._cbuf is None:
            assert not self._scopes, "first cconst must happen in base scope; pre-warm with em.cconst(0.0)"
            self._cbuf = self.sb("consts_buf", (128, 32), F32)
            self.dma(self._cbuf, self._cbuf[:], None, self._cbuf_pending[:].to_broadcast([128, 32]))
        if val not in self.const_vals:
            assert len(self.const_vals) < 32
            self.const_vals.append(val)
        i = self.const_vals.index(val)
        return self._cbuf[0:rows, i:i + 1]

    def consts_input(self):
        a = np.zeros((1, 32), np.float32)
        for i, v in enumerate(self.const_vals):
            a[0, i] = v
        return a

    def op(self, engine, fn, reads=(), writes=()):
        reads = list(reads)
        if self._cbuf is not None:
            reads.append(self._cbuf)
        self._pre(engine, reads, writes)
        inst = fn(getattr(self.nc, engine))
        return self._post(engine, inst, reads, writes)

    def dma(self, dst, dst_ap, src, src_ap, engine="sync"):
        reads = [src] if src is not None else []
        writes = [dst] if dst is not None else []
        self._pre(engine, reads, writes)
        if engine == "gpsimd":
            sname = f"gdma{self._gdma_rr}"
            self._gdma_rr = (self._gdma_rr + 1) % self.ngdma
        else:
            sname = f"dma{self._dma_rr}"
            self._dma_rr = (self._dma_rr + 1) % self.ndma
        if self.cnt[sname] > 0:
            self._wait(engine, (self.sems[sname], self.cnt[sname]))
        inst = getattr(self.nc, engine).dma_start(out=dst_ap, in_=src_ap)
        return self._post(engine, inst, reads, writes, sem_name=sname, inc=16)

    def dshared(self, name, shape, dtype):
        return Buf(self.nc.dram_tensor(name, list(shape), dtype, kind="Internal",
                                       addr_space="Shared").ap(), name)

    def _bar_init(self):
        if self._bar is not None:
            return self._bar
        par = self.din("par", (1, 2), I32)
        bseq = self.din("barseq", (1, 16), I32)
        pb = self.sb("bar_pb", (1, 2), I32)
        bs = self.sb("bar_bs", (1, 16), I32)
        fl = self.sb("bar_fl", (1, 1), I32)
        flags = self.dshared("bar_flags", (2, 16), I32)
        self.dma(pb, pb[:], None, par[:])
        self.dma(bs, bs[:], None, bseq[:])
        self.op("vector", lambda v: v.memset(fl[:], 0), reads=[], writes=[fl])
        self._wait("sync", pb.wtok)
        self._wait("sync", bs.wtok)
        self._wait("sync", fl.wtok)
        p = self.nc.sync.value_load(pb[0:1, 0:1], min_val=0, max_val=1)
        q = self.nc.sync.value_load(pb[0:1, 1:2], min_val=0, max_val=1)
        spin_sem = self.ctx.enter_context(self.nc.semaphore("sem_spin"))
        spin_tgt = self.nc.sync.to_reg(0)
        self._bar = dict(pb=pb, bs=bs, fl=fl, flags=flags, p=p, q=q,
                         spin_sem=spin_sem, spin_tgt=spin_tgt)
        return self._bar

    @property
    def preg(self):
        return self._bar_init()["p"]

    def barrier(self, protects=()):
        b = self._bar_init()
        k = self.nbar
        self.nbar += 1
        assert k < 16
        nc = self.nc
        for buf in protects:
            if buf.wtok is not None:
                self._wait("sync", buf.wtok)
        inst = nc.sync.dma_start(out=b["flags"][bass.ts(b["p"], 1), k:k + 1]
                                 .squeeze(0).unsqueeze(0), in_=b["bs"][0:1, k:k + 1])
        sname = f"dma{self._dma_rr}"
        self._dma_rr = (self._dma_rr + 1) % self.ndma
        if self.cnt[sname] > 0:
            self._wait("sync", (self.sems[sname], self.cnt[sname]))
        inst.then_inc(self.sems[sname], 16)
        self.cnt[sname] += 16
        self._wait("sync", (self.sems[sname], self.cnt[sname]))
        tgt = nc.sync.value_load(b["bs"][0:1, k:k + 1])

        def cond():
            v = nc.sync.value_load(b["fl"][0:1, 0:1])
            return v != tgt
        with nc.sync.While(cond):
            nc.sync.dma_start(out=b["fl"][:], in_=b["flags"][bass.ts(b["q"], 1), k:k + 1]
                              .squeeze(0).unsqueeze(0)).then_inc(b["spin_sem"], 16)
            nc.sync.reg_add(b["spin_tgt"], b["spin_tgt"], 16)
            nc.sync.wait_ge(b["spin_sem"], b["spin_tgt"])
        tok_inst = nc.sync.nop()
        tok = self._post("sync", tok_inst, [], list(protects))
        return tok

    def fence(self, engines=_ENGINES):
        for e in engines:
            for s_name, sem in self.sems.items():
                if self.cnt[s_name] > 0:
                    self._wait(e, (sem, self.cnt[s_name]))

    def finish(self, out_dma_engine="sync"):
        self.fence(engines=(out_dma_engine,))
        self.nc.compile()

    def close(self):
        self.ctx.close()




def run_em(em, in_map, core_ids=None, n_cores=1, trace=False):
    core_ids = core_ids if core_ids is not None else list(range(n_cores))
    maps = in_map if isinstance(in_map, list) else [in_map] * len(core_ids)
    res = run_bass_kernel_spmd(em.nc, maps, core_ids=core_ids, trace=trace)
    return res



def emit_ln(em, x_t, xn_t, scr, stats, width=D, eps=LN_EPS, x_slice=None, xn_slice=None):
    xap = x_t[:] if x_slice is None else x_slice
    oap = xn_t[:] if xn_slice is None else xn_slice
    s_sum = stats[:, 0:1]
    s_sq = stats[:, 1:2]
    s_mu = stats[:, 2:3]
    s_var = stats[:, 3:4]
    s_rstd = stats[:, 4:5]
    s_b = stats[:, 5:6]
    em.op("vector", lambda v: v.reduce_sum(out=s_sum, in_=xap, axis=AX.X),
          reads=[x_t], writes=[stats])
    em.op("scalar", lambda s: s.activation(out=scr[:, 0:width], in_=xap, func=AF.Square,
                                           accum_out=s_sq),
          reads=[x_t], writes=[scr, stats])
    em.op("vector", lambda v: v.tensor_scalar_mul(s_mu, s_sum, 1.0 / width),
          reads=[stats], writes=[stats])
    em.op("vector", lambda v: v.tensor_mul(s_var, s_mu, s_mu), reads=[stats], writes=[stats])
    em.op("vector", lambda v: v.scalar_tensor_tensor(out=s_var, in0=s_sq, scalar=1.0 / width,
                                                     in1=s_var, op0=ALU.mult, op1=ALU.subtract),
          reads=[stats], writes=[stats])
    parts = stats.t.shape[0] if hasattr(stats.t, 'shape') else 128
    epsc = em.cconst(eps, rows=parts)
    em.op("scalar", lambda s: s.activation(out=s_rstd, in_=s_var, func=AF.Ln, bias=epsc),
          reads=[stats], writes=[stats])
    em.op("scalar", lambda s: s.activation(out=s_rstd, in_=s_rstd, func=AF.Exp, scale=-0.5),
          reads=[stats], writes=[stats])
    em.op("vector", lambda v: v.scalar_tensor_tensor(out=s_b, in0=s_mu, scalar=-1.0,
                                                     in1=s_rstd, op0=ALU.mult, op1=ALU.mult),
          reads=[stats], writes=[stats])
    em.op("scalar", lambda s: s.activation(out=oap, in_=xap, func=AF.Identity,
                                           scale=s_rstd, bias=s_b),
          reads=[x_t, stats], writes=[xn_t])


def emit_transpose128(em, src_t, src_ap, ps_t, ps_ap, dst_t, dst_ap, ident_t):
    em.op("tensor", lambda t: t.transpose(ps_ap, src_ap, ident_t[:]),
          reads=[src_t, ident_t], writes=[ps_t])
    em.op("vector", lambda v: v.tensor_copy(out=dst_ap, in_=ps_ap),
          reads=[ps_t], writes=[dst_t])


def _weng(b):
    return "sync" if b.t.dtype == BF16 else "gpsimd"


def emit_precast(em, plan):
    em.push()
    CW = 4096
    nbuf = 4
    pin = [em.sb(f"pci{j}", (128, CW), F32) for j in range(nbuf)]
    pot = [em.sb(f"pco{j}", (128, CW), BF16) for j in range(nbuf)]
    k = 0
    for srcb, dstb, rows, cols in plan:
        for r in range(0, rows, 128):
            for c0 in range(0, cols, CW):
                w = min(CW, cols - c0)
                ib, ob = pin[k % nbuf], pot[k % nbuf]
                em.dma(ib, ib[:, 0:w], None, srcb[r:r + 128, c0:c0 + w])
                if k % 2 == 0:
                    em.op("vector", lambda v, ib=ib, ob=ob, w=w: v.tensor_copy(
                        out=ob[:, 0:w], in_=ib[:, 0:w]), reads=[ib], writes=[ob])
                else:
                    em.op("scalar", lambda s, ib=ib, ob=ob, w=w: s.activation(
                        out=ob[:, 0:w], in_=ib[:, 0:w], func=AF.Identity),
                        reads=[ib], writes=[ob])
                em.dma(dstb, dstb[r:r + 128, c0:c0 + w], ob, ob[:, 0:w])
                k += 1
    em.pop()


def emit_qffn(em, t, ntok=TOK):
    nt = ntok // 128
    x = t["x"]; wT = t["w_int8T"]; wsc = t["wscale_col"]; bias = t["bias_row"]
    ident = t["ident"]; y = t["y"]
    em.push()
    wt_i32 = em.sb("wt_i32", (128, 8 * D), I32)
    wt_bf = em.sb("wt_bf", (128, 8 * D), BF16)
    bias_bc = em.sb("bias_bc", (128, D), F32)
    wsc_t = em.sb("wsc_t", (128, 1), F32)
    id_t = em.sb("id_t", (128, 128), BF16)

    xt = [em.sb(f"xt{i}", (128, D), F32) for i in range(2)]
    scr_a = [em.sb(f"scr{j}", (128, D), F32) for j in range(2)]
    xn_a = [em.sb(f"xn{j}", (128, D), F32) for j in range(2)]
    qbf_a = [em.sb(f"qbf{j}", (128, D), BF16) for j in range(2)]
    qi_a = [em.sb(f"qi{j}", (128, D), I32) for j in range(2)]
    qfm_a = [em.sb(f"qfm{j}", (128, D), BF16) for j in range(2)]
    ot_a = [em.sb(f"ot{j}", (128, D), F32) for j in range(2)]
    yt = [em.sb(f"yt{i}", (128, D), F32) for i in range(2)]
    stats_a = [em.sb(f"stats{j}", (128, 8), F32) for j in range(2)]
    st2_a = [em.sb(f"st2{j}", (128, 8), F32) for j in range(2)]

    psM_a = [em.ps(f"psM{j}", (128, 1024), F32) for j in range(2)]
    psT_a = [em.ps(f"psT{j}", (128, 512), BF16) for j in range(2)]

    em.dma(id_t, id_t[:], None, ident[:])
    em.dma(wsc_t, wsc_t[:], None, wsc[:])
    em.dma(bias_bc, bias_bc[:], None, bias[:].to_broadcast([128, D]))
    for c in range(8):
        em.dma(wt_i32, wt_i32[:, c * D:(c + 1) * D], None, wT[c * 128:(c + 1) * 128, :])
    em.op("vector", lambda v: v.tensor_copy(out=wt_bf[:], in_=wt_i32[:]),
          reads=[wt_i32], writes=[wt_bf])

    for i in range(nt):
        xb = xt[i % 2]
        yb = yt[i % 2]
        scr, xn, qbf, qi, qfm, ot = (scr_a[i % 2], xn_a[i % 2], qbf_a[i % 2],
                                     qi_a[i % 2], qfm_a[i % 2], ot_a[i % 2])
        stats, st2, psT, psM = stats_a[i % 2], st2_a[i % 2], psT_a[i % 2], psM_a[i % 2]
        em.dma(xb, xb[:], None, x[i * 128:(i + 1) * 128, :])
        emit_ln(em, xb, xn, scr, stats)
        s_ms = st2[:, 0:1]
        s_r2 = st2[:, 1:2]
        s_am = st2[:, 2:3]
        s_scale = st2[:, 3:4]
        s_comb = st2[:, 4:5]
        s_fact = st2[:, 5:6]
        em.op("scalar", lambda s: s.activation(out=scr[:], in_=xn[:], func=AF.Square,
                                               accum_out=s_ms), reads=[xn], writes=[scr, st2])
        em.op("scalar", lambda s: s.activation(out=s_r2, in_=s_ms, func=AF.Ln,
                                               scale=1.0 / D, bias=em.cconst(RMS_EPS)),
              reads=[st2], writes=[st2])
        em.op("scalar", lambda s: s.activation(out=s_r2, in_=s_r2, func=AF.Exp, scale=-0.5),
              reads=[st2], writes=[st2])
        em.op("vector", lambda v: v.reduce_max(out=s_am, in_=xn[:], axis=AX.X,
                                               apply_absolute_value=True),
              reads=[xn], writes=[st2])
        em.op("vector", lambda v: v.tensor_mul(s_am, s_am, s_r2), reads=[st2], writes=[st2])
        em.op("vector", lambda v: v.tensor_scalar_max(s_am, s_am, float(ACT_EPS)),
              reads=[st2], writes=[st2])
        em.op("vector", lambda v: v.reciprocal(out=s_scale, in_=s_am), reads=[st2], writes=[st2])
        em.op("vector", lambda v: v.scalar_tensor_tensor(out=s_comb, in0=s_scale, scalar=127.0,
                                                         in1=s_r2, op0=ALU.mult, op1=ALU.mult),
              reads=[st2], writes=[st2])
        em.op("vector", lambda v: v.scalar_tensor_tensor(out=s_fact, in0=s_am,
                                                         scalar=1.0 / 127.0, in1=wsc_t[:],
                                                         op0=ALU.mult, op1=ALU.mult),
              reads=[st2, wsc_t], writes=[st2])
        em.op("scalar", lambda s: s.activation(out=scr[:], in_=xn[:], func=AF.Identity,
                                               scale=s_comb), reads=[xn, st2], writes=[scr])
        em.op("vector", lambda v: v.tensor_scalar(out=scr[:], in0=scr[:], scalar1=127.0,
                                                  scalar2=-128.0, op0=ALU.min, op1=ALU.max),
              reads=[scr], writes=[scr])
        em.op("vector", lambda v: v.tensor_copy(out=qi[:], in_=scr[:]), reads=[scr], writes=[qi])
        em.op("vector", lambda v: v.tensor_copy(out=xn[:], in_=qi[:]), reads=[qi], writes=[xn])
        em.op("vector", lambda v: v.tensor_sub(scr[:], scr[:], xn[:]), reads=[scr, xn], writes=[scr])
        em.op("vector", lambda v: v.tensor_scalar(out=qbf[:], in0=scr[:], scalar1=0.5,
                                                  scalar2=None, op0=ALU.is_gt),
              reads=[scr], writes=[qbf])
        em.op("vector", lambda v: v.tensor_add(xn[:], xn[:], qbf[:]), reads=[xn, qbf], writes=[xn])
        em.op("vector", lambda v: v.tensor_scalar(out=qbf[:], in0=scr[:], scalar1=-0.5,
                                                  scalar2=None, op0=ALU.is_lt),
              reads=[scr], writes=[qbf])
        em.op("vector", lambda v: v.tensor_sub(xn[:], xn[:], qbf[:]), reads=[xn, qbf], writes=[xn])
        em.op("vector", lambda v: v.tensor_copy(out=qbf[:], in_=xn[:]), reads=[xn], writes=[qbf])
        for c in range(8):
            emit_transpose128(em, qbf, qbf[:, c * 128:(c + 1) * 128], psT, psT[:, 0:128],
                              qfm, qfm[:, c * 128:(c + 1) * 128], id_t)
        for h in range(2):
            for c in range(8):
                em.op("tensor",
                      lambda t, h=h, c=c: t.matmul(
                          psM[:, h * 512:(h + 1) * 512],
                          qfm[:, c * 128:(c + 1) * 128],
                          wt_bf[:, c * D + h * 512: c * D + (h + 1) * 512],
                          start=(c == 0), stop=(c == 7)),
                      reads=[qfm, wt_bf], writes=[psM])
        em.op("scalar", lambda s: s.activation(out=ot[:], in_=psM[:], func=AF.Identity,
                                               scale=s_fact), reads=[psM, st2], writes=[ot])
        em.op("vector", lambda v: v.tensor_add(ot[:], ot[:], bias_bc[:]),
              reads=[ot, bias_bc], writes=[ot])
        em.op("vector", lambda v: v.scalar_tensor_tensor(out=yb[:], in0=ot[:], scalar=0.5,
                                                         in1=xb[:], op0=ALU.mult, op1=ALU.add),
              reads=[ot, xb], writes=[yb])
        em.dma(y, y[i * 128:(i + 1) * 128, :], yb, yb[:])
    em.pop()


def build_qffn(ntok=TOK):
    em = Em("qffn")
    em.cconst(0.0)
    t = {
        "x": em.din("x", (ntok, D), F32),
        "w_int8T": em.din("w_int8T", (D, D), I32),
        "wscale_col": em.din("wscale_col", (128, 1), F32),
        "bias_row": em.din("bias_row", (1, D), F32),
        "ident": em.din("ident", (128, 128), BF16),
        "y": em.dout("y", (ntok, D), F32),
    }
    emit_qffn(em, t, ntok)
    em.finish()
    return em


def qffn_host_inputs(x_shard, w_int8, weight_scale, bias, em=None):
    return {
        "x": np.ascontiguousarray(x_shard, dtype=np.float32),
        "w_int8T": np.ascontiguousarray(w_int8.T).astype(np.int32, copy=False),
        "wscale_col": np.ascontiguousarray(np.broadcast_to(np.asarray(weight_scale, np.float32).reshape(1, 1), (128, 1))),
        "bias_row": np.ascontiguousarray(bias, dtype=np.float32).reshape(1, D),
        "ident": np.eye(128, dtype=bfd),
    }




def emit_peer(em, t, ntok=TOK, dbg=False, hook=None):
    nt = ntok // 128
    x = t["x"]; wqT = t["wqT"]; keysP = t["keysP"]; deT = t["deT"]; ue = t["ue"]
    wupT = t["wupT"]; wdnT = t["wdnT"]; y = t["y"]
    wt_dram = t["wt_dram"]; u_dram = t["u_dram"]; id_t = t["id_t"]
    if dbg:
        dbg_xn = em.dout("dbg_xn", (128, D), BF16)
        dbg_sim = em.dout("dbg_sim", (128, 512), F32)
        dbg_sxm = em.dout("dbg_sxm", (128, 256), BF16)
        dbg_sym = em.dout("dbg_sym", (128, 256), BF16)
        dbg_small = em.dout("dbg_small", (128, 12), F32)
        dbg_proj = em.dout("dbg_proj", (128, NEXP), BF16)
        dbg_w = em.dout("dbg_w", (128, NEXP), BF16)

    xt = [em.sb(f"xt{i}", (128, D), F32) for i in range(2)]
    scr = em.sb("scr", (128, D), F32)
    stats = em.sb("stats", (128, 8), F32)
    em.push()
    xn_fm = [em.sb(f"xnfm{c}", (128, ntok), BF16) for c in range(8)]
    xn_tm = em.sb("xn_tm", (128, D), BF16)
    psT = em.ps("psT", (128, 512), BF16)
    for i in range(nt):
        xb = xt[i % 2]
        em.dma(xb, xb[:], None, x[i * 128:(i + 1) * 128, :])
        emit_ln(em, xb, xn_tm, scr, stats)
        if dbg and i == 0:
            em.dma(dbg_xn, dbg_xn[:], xn_tm, xn_tm[:])
        for c in range(8):
            emit_transpose128(em, xn_tm, xn_tm[:, c * 128:(c + 1) * 128],
                              psT, psT[:, 0:128],
                              xn_fm[c], xn_fm[c][:, i * 128:(i + 1) * 128], id_t)

    sxm_all = em.sb("sxm_all", (128, nt * 256), BF16)
    sym_all = em.sb("sym_all", (128, nt * 256), BF16)
    thr_all = em.sb("thr_all", (128, nt * 4), F32)
    ngm_all = em.sb("ngm_all", (128, nt * 4), F32)
    rcp_all = em.sb("rcp_all", (128, nt * 4), F32)

    em.push()
    wq_bf = em.sb("wq_bf", (128, 8 * 512), BF16)
    for c in range(8):
        em.dma(wq_bf, wq_bf[:, c * 512:(c + 1) * 512], None,
               wqT[c * 128:(c + 1) * 128, :], engine=_weng(wqT))
    keys_bf = em.sb("keys_bf", (128, 8 * 64), BF16)
    for p in range(2):
        for h in range(HEADS_P):
            em.dma(keys_bf, keys_bf[0:64, (p * 4 + h) * 64:(p * 4 + h + 1) * 64], None,
                   keysP[p, h], engine="gpsimd")
            em.dma(keys_bf, keys_bf[64:128, (p * 4 + h) * 64:(p * 4 + h + 1) * 64], None,
                   keysP[p, h], engine="gpsimd")
    q_fm = [em.sb(f"qfm{c}", (128, ntok), BF16) for c in range(4)]
    psQ = em.ps("psQ", (128, 512), F32)
    tqs = min(512, ntok)
    for qc in range(4):
        for th in range(ntok // tqs):
            for c in range(8):
                em.op("tensor", lambda t, qc=qc, th=th, c=c: t.matmul(
                    psQ[:, 0:tqs], wq_bf[:, c * 512 + qc * 128: c * 512 + (qc + 1) * 128],
                    xn_fm[c][:, th * tqs:(th + 1) * tqs],
                    start=(c == 0), stop=(c == 7)),
                    reads=[wq_bf, xn_fm[c]], writes=[psQ])
            em.op("scalar", lambda s, qc=qc, th=th: s.activation(
                out=q_fm[qc][:, th * tqs:(th + 1) * tqs], in_=psQ[:, 0:tqs], func=AF.Identity),
                reads=[psQ], writes=[q_fm[qc]])

    simb = em.sb("simb", (128, 512), F32)
    mrb = em.sb("mrb", (128, 512), F32)
    eqb = em.sb("eqb", (128, 512), F32)
    d1b = em.sb("d1b", (128, 512), F32)
    v8x = em.sb("v8x", (128, 32), BF16)
    v8y = em.sb("v8y", (128, 32), BF16)
    v8f = em.sb("v8f", (128, 8), F32)
    p64 = em.sb("p64", (128, 256), BF16)
    v8p = em.sb("v8p", (128, 32), BF16)
    pexp = em.sb("pexp", (128, 32), F32)
    psS = em.ps("psS", (128, 64), F32)
    for i in range(nt):
        for p in range(2):
            for h in range(HEADS_P):
                qc = p * 2 + h // 2
                base = (h % 2) * 64
                em.op("tensor", lambda t, p=p, h=h, qc=qc, base=base, i=i: t.matmul(
                    psS[:], q_fm[qc][base:base + 64, i * 128:(i + 1) * 128],
                    keys_bf[base:base + 64, (p * 4 + h) * 64:(p * 4 + h + 1) * 64],
                    start=True, stop=True),
                    reads=[q_fm[qc], keys_bf], writes=[psS])
                em.op("scalar", lambda s, p=p, h=h: s.activation(
                    out=simb[:, (p * 4 + h) * 64:(p * 4 + h + 1) * 64], in_=psS[:],
                    func=AF.Identity), reads=[psS], writes=[simb])
        for p in range(2):
            for h in range(HEADS_P):
                seg = simb[:, (p * 4 + h) * 64:(p * 4 + h + 1) * 64]
                mseg = mrb[:, (p * 4 + h) * 64:(p * 4 + h + 1) * 64]
                em.op("vector", lambda v, seg=seg: v.max(out=v8f[:], in_=seg),
                      reads=[simb], writes=[v8f])
                em.op("vector", lambda v, seg=seg, mseg=mseg: v.match_replace(
                    out=mseg, in_to_replace=v8f[:], in_values=seg, imm_value=-BIG),
                      reads=[simb, v8f], writes=[mrb])
        em.op("vector", lambda v: v.tensor_tensor(out=eqb[:], in0=simb[:], in1=mrb[:],
                                                  op=ALU.is_equal),
              reads=[simb, mrb], writes=[eqb])
        em.op("vector", lambda v: v.tensor_scalar_add(d1b[:], simb[:], BIG),
              reads=[simb], writes=[d1b])
        em.op("vector", lambda v: v.tensor_mul(d1b[:], d1b[:], eqb[:]),
              reads=[d1b, eqb], writes=[d1b])
        em.op("vector", lambda v, i=i: v.tensor_sub(sxm_all[:, i * 256:(i + 1) * 256],
                                                    simb[:, 0:256], d1b[:, 0:256]),
              reads=[simb, d1b], writes=[sxm_all])
        em.op("vector", lambda v, i=i: v.tensor_sub(sym_all[:, i * 256:(i + 1) * 256],
                                                    simb[:, 256:512], d1b[:, 256:512]),
              reads=[simb, d1b], writes=[sym_all])
        for h in range(HEADS_P):
            em.op("vector", lambda v, i=i, h=h: v.max(
                out=v8x[:, h * 8:(h + 1) * 8],
                in_=sxm_all[:, i * 256 + h * 64: i * 256 + (h + 1) * 64]),
                reads=[sxm_all], writes=[v8x])
            em.op("vector", lambda v, i=i, h=h: v.max(
                out=v8y[:, h * 8:(h + 1) * 8],
                in_=sym_all[:, i * 256 + h * 64: i * 256 + (h + 1) * 64]),
                reads=[sym_all], writes=[v8y])
        em.op("vector", lambda v: v.tensor_tensor(
            out=p64[:].rearrange("p (h i j) -> p h i j", h=4, i=8),
            in0=v8x[:].rearrange("p (h i) -> p h i", h=4).unsqueeze(3).to_broadcast([128, 4, 8, 8]),
            in1=v8y[:].rearrange("p (h j) -> p h j", h=4).unsqueeze(2).to_broadcast([128, 4, 8, 8]),
            op=ALU.add), reads=[v8x, v8y], writes=[p64])
        for h in range(HEADS_P):
            em.op("vector", lambda v, h=h: v.max(out=v8p[:, h * 8:(h + 1) * 8],
                                                 in_=p64[:, h * 64:(h + 1) * 64]),
                  reads=[p64], writes=[v8p])
        for h in range(HEADS_P):
            em.op("vector", lambda v, i=i, h=h: v.tensor_copy(
                out=thr_all[:, i * 4 + h: i * 4 + h + 1], in_=v8p[:, h * 8 + 7: h * 8 + 8]),
                reads=[v8p], writes=[thr_all])
            em.op("vector", lambda v, i=i, h=h: v.tensor_scalar_mul(
                ngm_all[:, i * 4 + h: i * 4 + h + 1], v8p[:, h * 8: h * 8 + 1], -1.0),
                reads=[v8p], writes=[ngm_all])
            em.op("scalar", lambda s, i=i, h=h: s.activation(
                out=pexp[:, h * 8:(h + 1) * 8], in_=v8p[:, h * 8:(h + 1) * 8], func=AF.Exp,
                bias=ngm_all[:, i * 4 + h: i * 4 + h + 1]),
                reads=[v8p, ngm_all], writes=[pexp])
        em.op("vector", lambda v, i=i: v.reduce_sum(
            out=rcp_all[:, i * 4:(i + 1) * 4],
            in_=pexp[:].rearrange("p (h k) -> p h k", h=4), axis=AX.X),
            reads=[pexp], writes=[rcp_all])
        em.op("vector", lambda v, i=i: v.reciprocal(
            out=rcp_all[:, i * 4:(i + 1) * 4], in_=rcp_all[:, i * 4:(i + 1) * 4]),
            reads=[rcp_all], writes=[rcp_all])
        if dbg and i == 0:
            em.dma(dbg_sim, dbg_sim[:], simb, simb[:])
            em.dma(dbg_sxm, dbg_sxm[:], sxm_all, sxm_all[:, 0:256])
            em.dma(dbg_sym, dbg_sym[:], sym_all, sym_all[:, 0:256])
            em.dma(dbg_small, dbg_small[:, 0:4], thr_all, thr_all[:, 0:4])
            em.dma(dbg_small, dbg_small[:, 4:8], ngm_all, ngm_all[:, 0:4])
            em.dma(dbg_small, dbg_small[:, 8:12], rcp_all, rcp_all[:, 0:4])

    em.pop()
    if hook is not None:
        hook()

    em.push()
    deT_bf = [em.sb(f"deT{c}", (128, NEXP), BF16) for c in range(8)]
    for c in range(8):
        for half in range(2):
            em.dma(deT_bf[c], deT_bf[c][:, half * 2048:(half + 1) * 2048], None,
                   deT[c * 128:(c + 1) * 128, half * 2048:(half + 1) * 2048], engine=_weng(deT))
    proj_b = [em.sb(f"proj_b{j}", (128, NEXP), BF16) for j in range(2)]
    zb = [em.sb("zb0", (128, NEXP), BF16)]
    eb = [em.sb("eb0", (128, NEXP), BF16)]
    mb_ = [em.sb("mb_0", (128, NEXP), BF16)]
    sb_ = eb
    wacc = [em.sb(f"wacc{j}", (128, NEXP), BF16) for j in range(2)]
    wts_1 = em.sb("wts", (128, 32 * 128), BF16)
    wts = [wts_1, wts_1]
    psP = em.ps("psP", (128, 512), F32)
    psP2 = em.ps("psP2", (128, 512), F32)
    psW = em.ps("psW", (128, 512), BF16)
    psW2 = em.ps("psW2", (128, 512), BF16)
    for i in range(nt):
        ti = i % 2
        prj, wa, wt_ = proj_b[ti], wacc[ti], wts[ti]
        for ec in range(8):
            pp = psP if ec % 2 == 0 else psP2
            for c in range(8):
                em.op("tensor", lambda t, i=i, ec=ec, c=c, pp=pp: t.matmul(
                    pp[:], xn_fm[c][:, i * 128:(i + 1) * 128],
                    deT_bf[c][:, ec * 512:(ec + 1) * 512],
                    start=(c == 0), stop=(c == 7)),
                    reads=[xn_fm[c], deT_bf[c]], writes=[pp])
            em.op("scalar", lambda s, ec=ec, pp=pp, prj=prj: s.activation(
                out=prj[:, ec * 512:(ec + 1) * 512], in_=pp[:], func=AF.Identity),
                reads=[pp], writes=[prj])
        for h in range(HEADS_P):
            zz, ee, mm = zb[0], eb[0], mb_[0]
            ss = sb_[0]
            em.op("vector", lambda v, i=i, h=h, zz=zz: v.tensor_tensor(
                out=zz[:].rearrange("p (i j) -> p i j", i=64),
                in0=sxm_all[:, i * 256 + h * 64:i * 256 + (h + 1) * 64]
                    .unsqueeze(2).to_broadcast([128, 64, 64]),
                in1=sym_all[:, i * 256 + h * 64:i * 256 + (h + 1) * 64]
                    .unsqueeze(1).to_broadcast([128, 64, 64]),
                op=ALU.add), reads=[sxm_all, sym_all], writes=[zz])
            em.op("scalar", lambda s, i=i, h=h, zz=zz, ee=ee: s.activation(
                out=ee[:], in_=zz[:], func=AF.Exp,
                bias=ngm_all[:, i * 4 + h:i * 4 + h + 1]),
                reads=[zz, ngm_all], writes=[ee])
            em.op("vector", lambda v, i=i, h=h, zz=zz, ee=ee, mm=mm: v.scalar_tensor_tensor(
                out=mm[:], in0=zz[:], scalar=thr_all[:, i * 4 + h:i * 4 + h + 1],
                in1=ee[:], op0=ALU.is_ge, op1=ALU.mult),
                reads=[zz, thr_all, ee], writes=[mm])
            em.op("vector", lambda v, mm=mm, prj=prj: v.tensor_mul(mm[:], mm[:], prj[:]),
                  reads=[mm, prj], writes=[mm])
            if h == 0:
                em.op("scalar", lambda s, i=i, h=h, mm=mm, wa=wa: s.activation(
                    out=wa[:], in_=mm[:], func=AF.Silu,
                    scale=rcp_all[:, i * 4 + h:i * 4 + h + 1]),
                    reads=[mm, rcp_all], writes=[wa])
            else:
                em.op("scalar", lambda s, i=i, h=h, mm=mm, ss=ss: s.activation(
                    out=ss[:], in_=mm[:], func=AF.Silu,
                    scale=rcp_all[:, i * 4 + h:i * 4 + h + 1]),
                    reads=[mm, rcp_all], writes=[ss])
                em.op("vector", lambda v, ss=ss, wa=wa: v.tensor_add(wa[:], wa[:], ss[:]),
                      reads=[wa, ss], writes=[wa])
        if dbg and i == 0:
            em.dma(dbg_proj, dbg_proj[:], prj, prj[:])
            em.dma(dbg_w, dbg_w[:], wa, wa[:])
        pw = psW if ti == 0 else psW2
        for c in range(32):
            emit_transpose128(em, wa, wa[:, c * 128:(c + 1) * 128], pw, pw[:, 0:128],
                              wt_, wt_[:, c * 128:(c + 1) * 128], id_t)
        em.dma(wt_dram, wt_dram[i].rearrange("c p t -> p c t"), wt_,
               wt_[:].rearrange("p (c t) -> p c t", c=32), engine="sync")

    em.pop()

    em.push()
    wup_bf = [em.sb(f"wup{c}", (128, IFF), BF16) for c in range(8)]
    for c in range(8):
        for half in range(2):
            em.dma(wup_bf[c], wup_bf[c][:, half * 2048:(half + 1) * 2048], None,
                   wupT[c * 128:(c + 1) * 128, half * 2048:(half + 1) * 2048], engine=_weng(wupT))
    ub = [em.sb(f"ub{j}", (128, ntok), BF16) for j in range(2)]
    psU = em.ps("psU", (128, 512), F32)
    psU2 = em.ps("psU2", (128, 512), F32)
    tqs2 = min(512, ntok)
    for ic in range(32):
        u_t = ub[ic % 2]
        for tq in range(ntok // tqs2):
            pp = psU if tq % 2 == 0 else psU2
            for c in range(8):
                em.op("tensor", lambda t, ic=ic, tq=tq, c=c, pp=pp: t.matmul(
                    pp[:, 0:tqs2], wup_bf[c][:, ic * 128:(ic + 1) * 128],
                    xn_fm[c][:, tq * tqs2:(tq + 1) * tqs2],
                    start=(c == 0), stop=(c == 7)),
                    reads=[wup_bf[c], xn_fm[c]], writes=[pp])
            em.op("scalar", lambda s, tq=tq, pp=pp, u_t=u_t: s.activation(
                out=u_t[:, tq * tqs2:(tq + 1) * tqs2], in_=pp[:, 0:tqs2], func=AF.Silu),
                reads=[pp], writes=[u_t])
        em.dma(u_dram, u_dram[ic], u_t, u_t[:], engine="sync")

    em.pop()
    em.pop()

    em.push()
    wdn_bf = [em.sb(f"wdn{c}", (128, 32 * 128), BF16) for c in range(8)]
    for ic in range(32):
        em.dma(wdn_bf[ic // 4], wdn_bf[ic // 4][:, (ic % 4) * 1024:(ic % 4 + 1) * 1024],
               None, wdnT[ic * 128:(ic + 1) * 128, :], engine=_weng(wdnT))
    ue_bf = [em.sb(f"ue{c}", (128, 32 * 128), BF16) for c in range(8)]
    for ec in range(32):
        em.dma(ue_bf[ec // 4], ue_bf[ec // 4][:, (ec % 4) * 1024:(ec % 4 + 1) * 1024],
               None, ue[ec * 128:(ec + 1) * 128, :], engine=_weng(ue))
    ustrip = [em.sb(f"ustrip{j}", (128, 4096), BF16) for j in range(2)]
    wtstrip = [em.sb(f"wtstrip{j}", (128, 4096), BF16) for j in range(2)]
    yb = [em.sb(f"yb{j}", (128, D), F32) for j in range(2)]
    psO = em.ps("psO", (128, 1024), F32)
    for i in range(nt):
        us = ustrip[i % 2]
        ws_ = wtstrip[i % 2]
        xb = xt[i % 2]
        ybuf = yb[i % 2]
        em.dma(us, us[:].rearrange("p (c t) -> p c t", c=32), u_dram,
               u_dram[:, :, i * 128:(i + 1) * 128].rearrange("c p t -> p c t"), engine="sync")
        em.dma(ws_, ws_[:].rearrange("p (c t) -> p c t", c=32), wt_dram,
               wt_dram[i].rearrange("c p t -> p c t"), engine="sync")
        em.dma(xb, xb[:], None, x[i * 128:(i + 1) * 128, :])
        for dh in range(2):
            for ic in range(32):
                em.op("tensor", lambda t, i=i, dh=dh, ic=ic, us=us: t.matmul(
                    psO[:, dh * 512:(dh + 1) * 512], us[:, ic * 128:(ic + 1) * 128],
                    wdn_bf[ic // 4][:, (ic % 4) * 1024 + dh * 512:(ic % 4) * 1024 + (dh + 1) * 512],
                    start=(ic == 0), stop=False),
                    reads=[us, wdn_bf[ic // 4]], writes=[psO])
            for ec in range(32):
                em.op("tensor", lambda t, i=i, dh=dh, ec=ec, ws_=ws_: t.matmul(
                    psO[:, dh * 512:(dh + 1) * 512], ws_[:, ec * 128:(ec + 1) * 128],
                    ue_bf[ec // 4][:, (ec % 4) * 1024 + dh * 512:(ec % 4) * 1024 + (dh + 1) * 512],
                    start=False, stop=(ec == 31)),
                    reads=[ws_, ue_bf[ec // 4]], writes=[psO])
        em.op("vector", lambda v, ybuf=ybuf, xb=xb: v.scalar_tensor_tensor(
            out=ybuf[:], in0=psO[:], scalar=0.5, in1=xb[:], op0=ALU.mult, op1=ALU.add),
            reads=[psO, xb], writes=[ybuf])
        em.dma(y, y[i * 128:(i + 1) * 128, :], ybuf, ybuf[:])

    em.pop()


def peer_dins(em, ntok=TOK):
    t = {
        "wqT": em.din("wqT", (D, 512), F32),
        "keysP": em.din("keysP", (2, HEADS_P, 64, 64), F32),
        "deT": em.din("deT", (D, NEXP), F32),
        "ue": em.din("ue", (NEXP, D), F32),
        "wupT": em.din("wupT", (D, IFF), F32),
        "wdnT": em.din("wdnT", (IFF, D), F32),
        "wt_dram": em.dtmp("wt_dram", (ntok // 128, 32, 128, 128), BF16),
        "u_dram": em.dtmp("u_dram", (32, 128, ntok), BF16),
    }
    return t


def build_peer(ntok=TOK, dbg=False):
    em = Em("peer")
    em.cconst(0.0)
    t = peer_dins(em, ntok)
    t["x"] = em.din("x", (ntok, D), F32)
    t["ident"] = em.din("ident", (128, 128), BF16)
    t["y"] = em.dout("y", (ntok, D), F32)
    id_t = em.sb("id_t", (128, 128), BF16)
    em.dma(id_t, id_t[:], None, t["ident"][:])
    t["id_t"] = id_t
    emit_peer(em, t, ntok, dbg=dbg)
    em.finish()
    return em


def peer_host_inputs(x_shard, W_up, W_down, W_q, keys, down_embed, up_embed):
    return {
        "x": np.ascontiguousarray(x_shard, dtype=np.float32),
        "wqT": np.ascontiguousarray(W_q.T, dtype=np.float32),
        "keysP": np.ascontiguousarray(keys.transpose(2, 0, 3, 1), dtype=np.float32),
        "deT": np.ascontiguousarray(down_embed.T, dtype=np.float32),
        "ue": np.ascontiguousarray(up_embed, dtype=np.float32),
        "wupT": np.ascontiguousarray(W_up.T, dtype=np.float32),
        "wdnT": np.ascontiguousarray(W_down.T, dtype=np.float32),
        "ident": np.eye(128, dtype=bfd),
    }


def emit_attn1(em, t, ntok=TOK):
    nt = ntok // 128
    x = t["x"]; wq = t["wq"]; wk = t["wk"]; wv = t["wv"]; id_t = t["id_t"]
    xn_o = t["xn"]; q_o = t["q"]; k_o = t["k"]; v_o = t["v"]
    em.push()
    xt = [em.sb(f"xt{i}", (128, D), F32) for i in range(2)]
    scr = em.sb("scr", (128, D), F32)
    stats = em.sb("stats", (128, 8), F32)
    xn_f32 = em.sb("xn_f32", (128, D), F32)
    xn_tm = em.sb("xn_tm", (128, D), BF16)
    xn_fm = [em.sb(f"xnfm{c}", (128, ntok), BF16) for c in range(8)]
    psT = em.ps("psT", (128, 512), BF16)
    for i in range(nt):
        xb = xt[i % 2]
        em.dma(xb, xb[:], None, x[i * 128:(i + 1) * 128, :])
        emit_ln(em, xb, xn_f32, scr, stats)
        em.dma(xn_o, xn_o[i * 128:(i + 1) * 128, :], xn_f32, xn_f32[:])
        em.op("vector", lambda v: v.tensor_copy(out=xn_tm[:], in_=xn_f32[:]),
              reads=[xn_f32], writes=[xn_tm])
        for c in range(8):
            emit_transpose128(em, xn_tm, xn_tm[:, c * 128:(c + 1) * 128],
                              psT, psT[:, 0:128],
                              xn_fm[c], xn_fm[c][:, i * 128:(i + 1) * 128], id_t)

    wq_bf = [em.sb(f"wqb{c}", (128, D), BF16) for c in range(8)]
    wk_bf = [em.sb(f"wkb{c}", (128, 256), BF16) for c in range(8)]
    wv_bf = [em.sb(f"wvb{c}", (128, 256), BF16) for c in range(8)]
    for c in range(8):
        em.dma(wq_bf[c], wq_bf[c][:], None, wq[c * 128:(c + 1) * 128, :], engine=_weng(wq))
        em.dma(wk_bf[c], wk_bf[c][:], None, wk[c * 128:(c + 1) * 128, :], engine=_weng(wk))
        em.dma(wv_bf[c], wv_bf[c][:], None, wv[c * 128:(c + 1) * 128, :], engine=_weng(wv))
    ob_a = [em.sb(f"ob{j}", (128, 512), BF16) for j in range(2)]
    psM = em.ps("psM", (128, 512), F32)
    psM2 = em.ps("psM2", (128, 512), F32)
    tqs = min(512, ntok)
    for qc in range(8):
        for th in range(ntok // tqs):
            pp = psM if th % 2 == 0 else psM2
            ob = ob_a[th % 2]
            for c in range(8):
                em.op("tensor", lambda t, qc=qc, th=th, c=c, pp=pp: t.matmul(
                    pp[:, 0:tqs], wq_bf[c][:, qc * 128:(qc + 1) * 128],
                    xn_fm[c][:, th * tqs:(th + 1) * tqs],
                    start=(c == 0), stop=(c == 7)),
                    reads=[wq_bf[c], xn_fm[c]], writes=[pp])
            em.op("scalar", lambda s, pp=pp, ob=ob: s.activation(out=ob[:, 0:tqs], in_=pp[:, 0:tqs],
                                                                 func=AF.Identity),
                  reads=[pp], writes=[ob])
            em.dma(q_o, q_o[qc * 128:(qc + 1) * 128, th * tqs:(th + 1) * tqs], ob, ob[:, 0:tqs])
    for kc in range(2):
        for th in range(ntok // tqs):
            pp = psM if th % 2 == 0 else psM2
            ob = ob_a[th % 2]
            for c in range(8):
                em.op("tensor", lambda t, kc=kc, th=th, c=c, pp=pp: t.matmul(
                    pp[:, 0:tqs], wk_bf[c][:, kc * 128:(kc + 1) * 128],
                    xn_fm[c][:, th * tqs:(th + 1) * tqs],
                    start=(c == 0), stop=(c == 7)),
                    reads=[wk_bf[c], xn_fm[c]], writes=[pp])
            em.op("scalar", lambda s, pp=pp, ob=ob: s.activation(out=ob[:, 0:tqs], in_=pp[:, 0:tqs],
                                                                 func=AF.Identity, scale=0.125),
                  reads=[pp], writes=[ob])
            em.dma(k_o, k_o[kc * 128:(kc + 1) * 128, th * tqs:(th + 1) * tqs], ob, ob[:, 0:tqs])
    for i in range(nt):
        pp = psM if i % 2 == 0 else psM2
        ob = ob_a[i % 2]
        for c in range(8):
            em.op("tensor", lambda t, i=i, c=c, pp=pp: t.matmul(
                pp[:, 0:256], xn_fm[c][:, i * 128:(i + 1) * 128], wv_bf[c][:],
                start=(c == 0), stop=(c == 7)),
                reads=[xn_fm[c], wv_bf[c]], writes=[pp])
        em.op("scalar", lambda s, pp=pp, ob=ob: s.activation(out=ob[:, 0:256], in_=pp[:, 0:256],
                                                             func=AF.Identity),
              reads=[pp], writes=[ob])
        em.dma(v_o, v_o[i * 128:(i + 1) * 128, :], ob, ob[:, 0:256])
    em.pop()


def build_attn1(ntok=TOK):
    em = Em("attn1")
    em.cconst(0.0)
    t = {
        "x": em.din("x", (ntok, D), F32),
        "wq": em.din("wq", (D, D), F32),
        "wk": em.din("wk", (D, 256), F32),
        "wv": em.din("wv", (D, 256), F32),
        "ident": em.din("ident", (128, 128), BF16),
        "xn": em.dout("xn", (ntok, D), F32),
        "q": em.dout("q", (D, ntok), BF16),
        "k": em.dout("k", (256, ntok), BF16),
        "v": em.dout("v", (ntok, 256), BF16),
    }
    id_t = em.sb("id_t", (128, 128), BF16)
    em.dma(id_t, id_t[:], None, t["ident"][:])
    t["id_t"] = id_t
    emit_attn1(em, t, ntok)
    em.finish()
    return em


def emit_attn2(em, t, ntok=TOK, sk=2 * TOK):
    nt = ntok // 128
    ntk = sk // 128
    nsc = max(1, ntok // 512)
    scs = min(512, ntok)
    x = t["x"]; xn_i = t["xn"]; q_i = t["q"]; k_i = t["k"]; v_i = t["v"]
    wpost = t["wpost"]; id_t = t["id_t"]; y = t["y"]
    bscr = t["bscr"]; rd_dram = t["rd_dram"]
    em.push()
    qf = [em.sb(f"qf{c}", (128, ntok), BF16) for c in range(8)]
    for c in range(8):
        em.dma(qf[c], qf[c][:], None, q_i[c * 128:(c + 1) * 128, :])
    kdup = [em.sb(f"kdup{g}", (128, sk), BF16) for g in range(AG)]
    for g in range(AG):
        em.dma(kdup[g], kdup[g][0:64, :], None, k_i[g * 64:(g + 1) * 64, :])
        em.dma(kdup[g], kdup[g][64:128, :], None, k_i[g * 64:(g + 1) * 64, :])
    vaug = em.sb("vaug", (128, ntk * AG * 65), BF16)
    for g in range(AG):
        em.dma(vaug, vaug[:].rearrange("p (tkg e) -> p tkg e", e=65)[:, g::AG, 0:64],
               None, v_i[:].rearrange("(tk p) e -> p tk e", p=128)[:, :, g * 64:(g + 1) * 64])
    em.op("vector", lambda v: v.memset(
        vaug[:].rearrange("p (tk g e) -> p tk g e", tk=ntk, g=AG)[:, :, :, 64:65], 1.0),
        reads=[], writes=[vaug])
    af = [em.sb(f"af{c}", (128, ntok), BF16) for c in range(8)]
    negB = em.sb("negB", (128, 1), F32)

    em.push()
    sq = em.sb("sq", (128, 512), F32)
    ones2 = em.sb("ones2", (128, 2), F32)
    em.op("vector", lambda v: v.memset(ones2[0:64, :], 0.0), reads=[], writes=[ones2])
    em.op("vector", lambda v: v.memset(ones2[0:64, 0:1], 1.0), reads=[], writes=[ones2])
    em.op("vector", lambda v: v.memset(ones2[64:128, 0:1], 0.0), reads=[], writes=[ones2])
    em.op("vector", lambda v: v.memset(ones2[64:128, 1:2], 1.0), reads=[], writes=[ones2])
    nmx = em.sb("nmx", (2, 64), F32)
    psN = em.ps("psN", (128, 512), F32)
    col = 0
    col_q = 0
    for (src_, nchunks) in ((qf, ntok // 512 if ntok >= 512 else 1), (kdup, sk // 512 if sk >= 512 else 1)):
        nbuf = len(src_) if src_ is qf else AG
        for c in range(nbuf):
            for th in range(nchunks):
                w_ = min(512, (ntok if src_ is qf else sk))
                em.op("scalar", lambda s, c=c, th=th, src=src_, w_=w_: s.activation(
                    out=sq[:, 0:w_], in_=src[c][:, th * w_:(th + 1) * w_], func=AF.Square),
                    reads=[src_[c]], writes=[sq])
                em.op("tensor", lambda t, w_=w_: t.matmul(
                    psN[:2, 0:w_], ones2[:], sq[:, 0:w_], start=True, stop=True),
                    reads=[ones2, sq], writes=[psN])
                em.op("vector", lambda v, col=col, w_=w_: v.reduce_max(
                    out=nmx[:, col:col + 1], in_=psN[:2, 0:w_], axis=AX.X),
                    reads=[psN], writes=[nmx])
                col += 1
        if src_ is qf:
            col_q = col
    ncq = col_q
    nred = em.sb("nred", (2, 2), F32)
    em.op("vector", lambda v: v.reduce_max(out=nred[:, 0:1], in_=nmx[:, 0:ncq], axis=AX.X),
          reads=[nmx], writes=[nred])
    em.op("vector", lambda v: v.reduce_max(out=nred[:, 1:2], in_=nmx[:, ncq:col], axis=AX.X),
          reads=[nmx], writes=[nred])
    em.dma(bscr, bscr[0:1, 0:4], nred, nred[:, 0:2])
    brow = em.sb("brow", (1, 8), F32)
    em.dma(brow, brow[0:1, 0:4], bscr, bscr[0:1, 0:4])
    em.op("vector", lambda v: v.reduce_max(out=brow[:, 4:5],
                                           in_=brow[:, 0:4].rearrange("p (a b) -> p b a", b=2)[:, 0:1, :],
                                           axis=AX.X),
          reads=[brow], writes=[brow])
    em.op("vector", lambda v: v.reduce_max(out=brow[:, 5:6],
                                           in_=brow[:, 0:4].rearrange("p (a b) -> p b a", b=2)[:, 1:2, :],
                                           axis=AX.X),
          reads=[brow], writes=[brow])
    em.op("vector", lambda v: v.tensor_mul(brow[:, 6:7], brow[:, 4:5], brow[:, 5:6]),
          reads=[brow], writes=[brow])
    em.op("scalar", lambda s: s.activation(out=brow[:, 6:7], in_=brow[:, 6:7], func=AF.Sqrt),
          reads=[brow], writes=[brow])
    em.op("vector", lambda v: v.tensor_scalar(out=brow[:, 7:8], in0=brow[:, 6:7],
                                              scalar1=-1.05, scalar2=-0.05,
                                              op0=ALU.mult, op1=ALU.add),
          reads=[brow], writes=[brow])
    em.dma(bscr, bscr[0:1, 4:5], brow, brow[:, 7:8])
    em.dma(negB, negB[:], bscr, bscr[0:1, 4:5].to_broadcast([128, 1]))
    em.pop()

    em.push()
    scs2 = min(1024, ntok)
    nsc2 = ntok // scs2
    hw = scs2 // 2
    den = em.sb("den", (1, 1024), F32)
    eb = [em.sb(f"ebb{j}", (128, scs2), BF16) for j in range(2)]
    psE = [em.ps(f"psE{j}", (128, scs2), F32) for j in range(2)]
    psA = [[em.ps(f"psA{j}h{hf}", (65, hw), F32) for hf in range(2)] for j in range(2)]
    for h in range(AH):
        g = h // 4
        base = (h % 2) * 64
        qb = qf[h // 2]
        for sc in range(nsc2):
            pa = psA[sc % 2]
            q0 = sc * scs2

            def qk(tk, half, pe, h=h, g=g, base=base, qb=qb, q0=q0):
                em.op("tensor", lambda t, tk=tk, half=half, pe=pe: t.matmul(
                    pe[:, half * hw:(half + 1) * hw],
                    kdup[g][base:base + 64, tk * 128:(tk + 1) * 128],
                    qb[base:base + 64, q0 + half * hw:q0 + (half + 1) * hw],
                    start=True, stop=True),
                    reads=[kdup[g], qb], writes=[pe])

            qk(0, 0, psE[0]); qk(0, 1, psE[0])
            for tk in range(ntk):
                pe = psE[tk % 2]
                ebb = eb[tk % 2]
                em.op("scalar", lambda s, pe=pe, ebb=ebb: s.activation(
                    out=ebb[:], in_=pe[:], func=AF.Exp, bias=negB[:]),
                    reads=[pe, negB], writes=[ebb])
                if tk + 1 < ntk:
                    qk(tk + 1, 0, psE[(tk + 1) % 2]); qk(tk + 1, 1, psE[(tk + 1) % 2])
                for half in range(2):
                    em.op("tensor", lambda t, g=g, tk=tk, half=half, ebb=ebb, pa=pa: t.matmul(
                        pa[half][:],
                        vaug[:, (tk * AG + g) * 65:(tk * AG + g + 1) * 65],
                        ebb[:, half * hw:(half + 1) * hw],
                        start=(tk == 0), stop=(tk == ntk - 1)),
                        reads=[vaug, ebb], writes=[pa[half]])
            for half in range(2):
                em.op("vector", lambda v, h=h, base=base, q0=q0, half=half, pa=pa: v.tensor_copy(
                    out=af[h // 2][base:base + 64, q0 + half * hw:q0 + (half + 1) * hw],
                    in_=pa[half][0:64, :]),
                    reads=[pa[half]], writes=[af[h // 2]])
                em.op("vector", lambda v, half=half, pa=pa: v.tensor_copy(
                    out=den[0:1, half * hw:(half + 1) * hw], in_=pa[half][64:65, :]),
                    reads=[pa[half]], writes=[den])
            em.dma(rd_dram, rd_dram[h, q0:q0 + scs2].unsqueeze(0), den, den[0:1, 0:scs2])

    em.pop()

    em.push()
    rb = em.sb("rb", (128, ntok), F32)
    for c in range(8):
        em.dma(rb, rb[0:64, :], rd_dram, rd_dram[2 * c].unsqueeze(0).to_broadcast([64, ntok]))
        em.dma(rb, rb[64:128, :], rd_dram, rd_dram[2 * c + 1].unsqueeze(0).to_broadcast([64, ntok]))
        em.op("vector", lambda v: v.reciprocal(out=rb[:], in_=rb[:]), reads=[rb], writes=[rb])
        em.op("vector", lambda v, c=c: v.tensor_mul(af[c][:], af[c][:], rb[:]),
              reads=[af[c], rb], writes=[af[c]])
    wp_bf = [em.sb(f"wpb{c}", (128, D), BF16) for c in range(8)]
    for c in range(8):
        em.dma(wp_bf[c], wp_bf[c][:], None, wpost[c * 128:(c + 1) * 128, :], engine=_weng(wpost))
    xb = [em.sb(f"xb{j}", (128, D), F32) for j in range(2)]
    xnb = [em.sb(f"xnb{j}", (128, D), F32) for j in range(2)]
    yb = [em.sb(f"yb{j}", (128, D), F32) for j in range(2)]
    psO = em.ps("psO", (128, 1024), F32)
    for i in range(nt):
        xbb = xb[i % 2]
        xnn = xnb[i % 2]
        ybb = yb[i % 2]
        em.dma(xbb, xbb[:], None, x[i * 128:(i + 1) * 128, :])
        em.dma(xnn, xnn[:], None, xn_i[i * 128:(i + 1) * 128, :])
        for dh in range(2):
            for c in range(8):
                em.op("tensor", lambda t, i=i, dh=dh, c=c: t.matmul(
                    psO[:, dh * 512:(dh + 1) * 512], af[c][:, i * 128:(i + 1) * 128],
                    wp_bf[c][:, dh * 512:(dh + 1) * 512],
                    start=(c == 0), stop=(c == 7)),
                    reads=[af[c], wp_bf[c]], writes=[psO])
        em.op("vector", lambda v, xnn=xnn: v.tensor_add(xnn[:], xnn[:], psO[:]),
              reads=[xnn, psO], writes=[xnn])
        em.op("vector", lambda v, ybb=ybb, xbb=xbb, xnn=xnn: v.tensor_add(ybb[:], xnn[:], xbb[:]),
              reads=[xnn, xbb], writes=[ybb])
        em.dma(y, y[i * 128:(i + 1) * 128, :], ybb, ybb[:])
    em.pop()
    em.pop()


def build_attn2(ntok=TOK, sk=2 * TOK):
    em = Em("attn2")
    em.cconst(0.0)
    t = {
        "x": em.din("x", (ntok, D), F32),
        "xn": em.din("xn", (ntok, D), F32),
        "q": em.din("q", (D, ntok), BF16),
        "k": em.din("k", (256, sk), BF16),
        "v": em.din("v", (sk, 256), BF16),
        "wpost": em.din("wpost", (D, D), F32),
        "ident": em.din("ident", (128, 128), BF16),
        "y": em.dout("y", (ntok, D), F32),
        "bscr": em.dtmp("bscr", (1, 16), F32),
        "rd_dram": em.dtmp("rd_dram", (AH, ntok), F32),
    }
    id_t = em.sb("id_t", (128, 128), BF16)
    em.dma(id_t, id_t[:], None, t["ident"][:])
    t["id_t"] = id_t
    emit_attn2(em, t, ntok, sk)
    em.finish()
    return em


def attn_host_inputs1(x_shard, Wq, Wk, Wv):
    return {
        "x": np.ascontiguousarray(x_shard, dtype=np.float32),
        "wq": np.ascontiguousarray(Wq, dtype=np.float32),
        "wk": np.ascontiguousarray(Wk, dtype=np.float32),
        "wv": np.ascontiguousarray(Wv, dtype=np.float32),
        "ident": np.eye(128, dtype=bfd),
    }


def emit_m1(em, t, ntok=TOK):
    nt = ntok // 128
    ntc = max(1, ntok // 512)
    tqs = min(512, ntok)
    x = t["x"]; xhalo = t["xhalo"]; hmask = t["hmask"]; ln_g = t["ln_g"]; ln_b = t["ln_b"]
    w_in = t["w_in"]; conv_w = t["conv_w"]; conv_b = t["conv_b"]; w_x = t["w_x"]
    w_dt = t["w_dt"]; b_dt = t["b_dt"]; id_t = t["id_t"]
    xn1_o = t["xn1"]; xc_o = t["xc"]; z_o = t["z"]; dt_o = t["dt"]; bc_o = t["bc"]
    em.push()
    g_bc = em.sb("g_bc", (128, D), F32)
    b_bc = em.sb("b_bc", (128, D), F32)
    em.dma(g_bc, g_bc[:], None, ln_g[:].to_broadcast([128, D]))
    em.dma(b_bc, b_bc[:], None, ln_b[:].to_broadcast([128, D]))

    xh_fm = [em.sb(f"xhfm{k}", (128, 3 + ntok), BF16) for k in range(16)]
    em.push()
    xn2_fm = [em.sb(f"xn2fm{c}", (128, ntok), BF16) for c in range(8)]
    xn2h_fm = em.sb("xn2h_fm", (128, 24), BF16)
    em.push()
    xt = [em.sb(f"xt{i}", (128, D), F32) for i in range(2)]
    scr = em.sb("scr", (128, D), F32)
    stats = em.sb("stats", (128, 8), F32)
    xnt = em.sb("xnt", (128, D), F32)
    xn2t = em.sb("xn2t", (128, D), F32)
    xn2b = em.sb("xn2b", (128, D), BF16)
    psT = em.ps("psT", (128, 512), BF16)
    for i in range(nt):
        xb = xt[i % 2]
        em.dma(xb, xb[:], None, x[i * 128:(i + 1) * 128, :])
        emit_ln(em, xb, xnt, scr, stats)
        em.op("vector", lambda v: v.tensor_mul(xn2t[:], xnt[:], g_bc[:]),
              reads=[xnt, g_bc], writes=[xn2t])
        em.op("vector", lambda v: v.tensor_add(xn2t[:], xn2t[:], b_bc[:]),
              reads=[xn2t, b_bc], writes=[xn2t])
        em.dma(xn1_o, xn1_o[i * 128:(i + 1) * 128, :], xnt, xnt[:])
        em.op("vector", lambda v: v.tensor_copy(out=xn2b[:], in_=xn2t[:]),
              reads=[xn2t], writes=[xn2b])
        for c in range(8):
            emit_transpose128(em, xn2b, xn2b[:, c * 128:(c + 1) * 128],
                              psT, psT[:, 0:128],
                              xn2_fm[c], xn2_fm[c][:, i * 128:(i + 1) * 128], id_t)
    hx = em.sb("hx", (3, D), F32)
    hm = em.sb("hm", (3, 1), F32)
    hstats = em.sb("hstats", (3, 8), F32)
    hscr = em.sb("hscr", (3, D), F32)
    hxn = em.sb("hxn", (3, D), F32)
    hxb = em.sb("hxb", (3, D), BF16)
    em.dma(hx, hx[:], None, xhalo[:])
    em.dma(hm, hm[:], None, hmask[:])
    emit_ln(em, hx, hxn, hscr, hstats, width=D)
    em.op("vector", lambda v: v.tensor_mul(hxn[:], hxn[:], g_bc[0:3, :]),
          reads=[hxn, g_bc], writes=[hxn])
    em.op("vector", lambda v: v.tensor_add(hxn[:], hxn[:], b_bc[0:3, :]),
          reads=[hxn, b_bc], writes=[hxn])
    em.op("vector", lambda v: v.tensor_scalar(out=hxb[:], in0=hxn[:], scalar1=hm[:],
                                              scalar2=None, op0=ALU.mult),
          reads=[hxn, hm], writes=[hxb])
    psH = em.ps("psH", (128, 128), BF16)
    for c in range(8):
        em.op("tensor", lambda t, c=c: t.transpose(psH[:, 0:3], hxb[:, c * 128:(c + 1) * 128],
                                                   id_t[0:3, 0:3]),
              reads=[hxb, id_t], writes=[psH])
        em.op("vector", lambda v, c=c: v.tensor_copy(out=xn2h_fm[:, c * 3:(c + 1) * 3],
                                                     in_=psH[:, 0:3]),
              reads=[psH], writes=[xn2h_fm])
    em.pop()

    em.push()
    w_in_bf = [em.sb(f"winb{c}", (128, 2 * DI), BF16) for c in range(8)]
    for c in range(8):
        for hf in range(2):
            em.dma(w_in_bf[c], w_in_bf[c][:, hf * DI:(hf + 1) * DI], None,
                   w_in[c * 128:(c + 1) * 128, hf * DI:(hf + 1) * DI], engine=_weng(w_in))
    zb = em.sb("zbm", (128, 512), F32)
    psZ = em.ps("psZ", (128, 512), F32)
    psZ2 = em.ps("psZ2", (128, 512), F32)
    for oc in range(32):
        is_xh = oc < 16
        for th in range(ntc):
            pp = psZ if th % 2 == 0 else psZ2
            for c in range(8):
                em.op("tensor", lambda t, oc=oc, th=th, c=c, pp=pp: t.matmul(
                    pp[:, 0:tqs], w_in_bf[c][:, oc * 128:(oc + 1) * 128],
                    xn2_fm[c][:, th * tqs:(th + 1) * tqs],
                    start=(c == 0), stop=(c == 7)),
                    reads=[w_in_bf[c], xn2_fm[c]], writes=[pp])
            if is_xh:
                em.op("vector", lambda v, oc=oc, th=th, pp=pp: v.tensor_copy(
                    out=xh_fm[oc][:, 3 + th * tqs: 3 + (th + 1) * tqs], in_=pp[:, 0:tqs]),
                    reads=[pp], writes=[xh_fm[oc]])
            else:
                em.op("vector", lambda v, pp=pp: v.tensor_copy(out=zb[:, 0:tqs], in_=pp[:, 0:tqs]),
                      reads=[pp], writes=[zb])
                em.dma(z_o, z_o[(oc - 16) * 128:(oc - 15) * 128, th * tqs:(th + 1) * tqs],
                       zb, zb[:, 0:tqs])
        if is_xh:
            pp = psZ
            for c in range(8):
                em.op("tensor", lambda t, oc=oc, c=c, pp=pp: t.matmul(
                    pp[:, 0:3], w_in_bf[c][:, oc * 128:(oc + 1) * 128],
                    xn2h_fm[:, c * 3:(c + 1) * 3],
                    start=(c == 0), stop=(c == 7)),
                    reads=[w_in_bf[c], xn2h_fm], writes=[pp])
            em.op("vector", lambda v, oc=oc, pp=pp: v.tensor_copy(
                out=xh_fm[oc][:, 0:3], in_=pp[:, 0:3]), reads=[pp], writes=[xh_fm[oc]])
    em.pop()
    em.pop()

    xc_bf = [em.sb(f"xcbf{k}", (128, ntok), BF16) for k in range(16)]
    em.push()
    cw = em.sb("cw", (128, 16 * DCONV), F32)
    cb = em.sb("cb", (128, 16), F32)
    for k in range(16):
        em.dma(cw, cw[:, k * DCONV:(k + 1) * DCONV], None, conv_w[k * 128:(k + 1) * 128, :])
    em.dma(cb, cb[:], None, conv_b[:])
    acc = em.sb("acc", (128, TOK), F32)
    xcf = em.sb("xcf", (128, TOK), F32)
    for k in range(16):
        em.op("vector", lambda v, k=k: v.tensor_scalar(
            out=acc[:, 0:ntok], in0=xh_fm[k][:, 0:ntok], scalar1=cw[:, k * DCONV:k * DCONV + 1],
            scalar2=None, op0=ALU.mult),
            reads=[xh_fm[k], cw], writes=[acc])
        for tap in range(1, DCONV):
            em.op("vector", lambda v, k=k, tap=tap: v.scalar_tensor_tensor(
                out=acc[:, 0:ntok], in0=xh_fm[k][:, tap:tap + ntok],
                scalar=cw[:, k * DCONV + tap:k * DCONV + tap + 1],
                in1=acc[:, 0:ntok], op0=ALU.mult, op1=ALU.add),
                reads=[xh_fm[k], cw, acc], writes=[acc])
        em.op("scalar", lambda s, k=k: s.activation(out=xcf[:, 0:ntok], in_=acc[:, 0:ntok],
                                                    func=AF.Silu, bias=cb[:, k:k + 1]),
              reads=[acc, cb], writes=[xcf])
        em.dma(xc_o, xc_o[k * 128:(k + 1) * 128, :], xcf, xcf[:, 0:ntok])
        em.op("vector", lambda v, k=k: v.tensor_copy(out=xc_bf[k][:], in_=xcf[:, 0:ntok]),
              reads=[xcf], writes=[xc_bf[k]])
    em.pop()

    em.push()
    wx_bf = em.sb("wx_bf", (128, 16 * 96), BF16)
    for k in range(16):
        em.dma(wx_bf, wx_bf[:, k * 96:(k + 1) * 96], None,
               w_x[k * 128:(k + 1) * 128, :], engine="gpsimd")
    dbc = em.sb("dbc", (96, ntok), F32)
    dbc_bf = em.sb("dbc_bf", (96, ntok), BF16)
    psD = em.ps("psD", (128, 512), F32)
    for th in range(ntc):
        for k in range(16):
            em.op("tensor", lambda t, th=th, k=k: t.matmul(
                psD[0:96, 0:tqs], wx_bf[:, k * 96:(k + 1) * 96],
                xc_bf[k][:, th * tqs:(th + 1) * tqs], start=(k == 0), stop=(k == 15)),
                reads=[wx_bf, xc_bf[k]], writes=[psD])
        em.op("vector", lambda v, th=th: v.tensor_copy(out=dbc[:, th * tqs:(th + 1) * tqs],
                                                       in_=psD[0:96, 0:tqs]),
              reads=[psD], writes=[dbc])
    em.dma(bc_o, bc_o[:], dbc, dbc[64:96, :])
    em.op("vector", lambda v: v.tensor_copy(out=dbc_bf[:], in_=dbc[:]),
          reads=[dbc], writes=[dbc_bf])
    wdt_bf = em.sb("wdt_bf", (64, DI), BF16)
    em.dma(wdt_bf, wdt_bf[:], None, w_dt[:], engine="gpsimd")
    bdt_c = em.sb("bdt_c", (128, 16), F32)
    em.dma(bdt_c, bdt_c[:], None, b_dt[:])
    ut = em.sb("ut", (128, 512), F32)
    t1 = em.sb("t1m", (128, 512), F32)
    t2 = em.sb("t2m", (128, 512), F32)
    dtt = em.sb("dtt", (128, 512), F32)
    one_c = em.cconst(1.0)
    for dc in range(16):
        for th in range(ntc):
            em.op("tensor", lambda t, dc=dc, th=th: t.matmul(
                psD[:, 0:tqs], wdt_bf[:, dc * 128:(dc + 1) * 128],
                dbc_bf[0:64, th * tqs:(th + 1) * tqs], start=True, stop=True),
                reads=[wdt_bf, dbc_bf], writes=[psD])
            em.op("scalar", lambda s, dc=dc: s.activation(out=ut[:, 0:tqs], in_=psD[:, 0:tqs],
                                                          func=AF.Identity,
                                                          bias=bdt_c[:, dc:dc + 1]),
                  reads=[psD, bdt_c], writes=[ut])
            em.op("scalar", lambda s: s.activation(out=t1[:, 0:tqs], in_=ut[:, 0:tqs], func=AF.Abs),
                  reads=[ut], writes=[t1])
            em.op("scalar", lambda s: s.activation(out=t1[:, 0:tqs], in_=t1[:, 0:tqs], func=AF.Exp,
                                                   scale=-1.0),
                  reads=[t1], writes=[t1])
            em.op("scalar", lambda s: s.activation(out=t1[:, 0:tqs], in_=t1[:, 0:tqs], func=AF.Ln,
                                                   bias=one_c),
                  reads=[t1], writes=[t1])
            em.op("scalar", lambda s: s.activation(out=t2[:, 0:tqs], in_=ut[:, 0:tqs], func=AF.Relu),
                  reads=[ut], writes=[t2])
            em.op("vector", lambda v: v.tensor_add(dtt[:, 0:tqs], t1[:, 0:tqs], t2[:, 0:tqs]),
                  reads=[t1, t2], writes=[dtt])
            em.dma(dt_o, dt_o[dc * 128:(dc + 1) * 128, th * tqs:(th + 1) * tqs], dtt, dtt[:, 0:tqs])
    em.pop()
    em.pop()


def build_m1(ntok=TOK):
    em = Em("m1")
    em.cconst(0.0)
    t = {
        "x": em.din("x", (ntok, D), F32),
        "xhalo": em.din("xhalo", (3, D), F32),
        "hmask": em.din("hmask", (3, 1), F32),
        "ln_g": em.din("ln_g", (1, D), F32),
        "ln_b": em.din("ln_b", (1, D), F32),
        "w_in": em.din("w_in", (D, 2 * DI), F32),
        "conv_w": em.din("conv_w", (DI, DCONV), F32),
        "conv_b": em.din("conv_b", (128, 16), F32),
        "w_x": em.din("w_x", (DI, 96), F32),
        "w_dt": em.din("w_dt", (DTRANK, DI), F32),
        "b_dt": em.din("b_dt", (128, 16), F32),
        "ident": em.din("ident", (128, 128), BF16),
        "xn1": em.dout("xn1", (ntok, D), F32),
        "xc": em.dout("xc", (DI, ntok), F32),
        "z": em.dout("z", (DI, ntok), F32),
        "dt": em.dout("dt", (DI, ntok), F32),
        "bc": em.dout("bc", (32, ntok), F32),
    }
    id_t = em.sb("id_t", (128, 128), BF16)
    em.dma(id_t, id_t[:], None, t["ident"][:])
    t["id_t"] = id_t
    emit_m1(em, t, ntok)
    em.finish()
    return em


def emit_m2(em, t, nch=1024, slen=2 * TOK):
    ndb = nch // 128
    CB = min(512, slen)
    CB2 = min(1024, slen)
    ncb2 = slen // CB2
    xc_i = t["xc"]; dt_i = t["dt"]; b_i = t["bm"]; c_i = t["cm"]; alog = t["alog"]
    y_o = t["ym"]; bsel = t["bsel"]
    em.push()
    ab = em.sb("ab", (128, ndb * DSTATE), F32)
    for db in range(ndb):
        em.dma(ab, ab[:, db * DSTATE:(db + 1) * DSTATE], None,
               alog[db * 128:(db + 1) * 128, :])
    em.op("scalar", lambda s: s.activation(out=ab[:], in_=ab[:], func=AF.Exp, scale=1.0),
          reads=[ab], writes=[ab])
    em.op("vector", lambda v: v.tensor_scalar_mul(ab[:], ab[:], -1.0), reads=[ab], writes=[ab])
    zcol = em.sb("zcol", (128, 1), F32)
    em.op("vector", lambda v: v.memset(zcol[:], 0.0), reads=[], writes=[zcol])

    bselb = em.sb("bselb", (48, DSTATE * 128), BF16)
    em.dma(bselb, bselb[0:16, :], None, bsel[0:16, :])
    em.dma(bselb, bselb[32:48, :], None, bsel[32:48, :])
    brow = em.sb("brow", (48, slen), BF16)
    em.dma(brow, brow[0:16, :], None, b_i[:], engine="gpsimd")
    em.dma(brow, brow[32:48, :], None, c_i[:], engine="gpsimd")

    dtb = [em.sb(f"dtb{j}", (128, slen), F32) for j in range(2)]
    wxc = em.sb("wxc", (128, slen), F32)
    wb = [em.sb(f"wb{j}", (128, slen), BF16) for j in range(2)]
    bbc = [em.sb(f"bbc{j}", (128, slen), BF16) for j in range(2)]
    cbc = [em.sb(f"cbc{j}", (128, slen), BF16) for j in range(2)]
    da = [em.sb(f"da{j}", (128, slen), BF16) for j in range(2)]
    dbu = em.sb("dbu", (128, slen), BF16)
    hbb = em.sb("hb", (128, slen), BF16)
    ynb = em.sb("yn", (128, slen), BF16)
    yacc = [em.sb(f"yacc{j}", (128, slen), F32) for j in range(2)]
    psB = [em.ps(f"psB{j}", (128, CB2), F32) for j in range(2)]
    psC = [em.ps(f"psC{j}", (128, CB2), F32) for j in range(2)]
    for db in range(ndb):
        jd = db % 2
        em.dma(dtb[jd], dtb[jd][:], None, dt_i[db * 128:(db + 1) * 128, :])
        em.dma(wxc, wxc[:], None, xc_i[db * 128:(db + 1) * 128, :])
        em.op("vector", lambda v, jd=jd: v.tensor_mul(wb[jd][:], wxc[:], dtb[jd][:]),
              reads=[wxc, dtb[jd]], writes=[wb[jd]])
        for n in range(DSTATE):
            j = n % 2
            for cc in range(ncb2):
                pb, pc = psB[cc % 2], psC[cc % 2]
                for h in range(CB2 // CB):
                    em.op("tensor", lambda t_, n=n, cc=cc, h=h, pb=pb: t_.matmul(
                        pb[:, h * CB:(h + 1) * CB], bselb[0:16, n * 128:(n + 1) * 128],
                        brow[0:16, cc * CB2 + h * CB: cc * CB2 + (h + 1) * CB],
                        start=True, stop=True),
                        reads=[bselb, brow], writes=[pb])
                    em.op("tensor", lambda t_, n=n, cc=cc, h=h, pc=pc: t_.matmul(
                        pc[:, h * CB:(h + 1) * CB], bselb[32:48, n * 128:(n + 1) * 128],
                        brow[32:48, cc * CB2 + h * CB: cc * CB2 + (h + 1) * CB],
                        start=True, stop=True),
                        reads=[bselb, brow], writes=[pc])
                em.op("scalar", lambda s, cc=cc, j=j, pb=pb: s.activation(
                    out=bbc[j][:, cc * CB2:(cc + 1) * CB2], in_=pb[:], func=AF.Identity),
                    reads=[pb], writes=[bbc[j]])
                em.op("scalar", lambda s, cc=cc, j=j, pc=pc: s.activation(
                    out=cbc[j][:, cc * CB2:(cc + 1) * CB2], in_=pc[:], func=AF.Identity),
                    reads=[pc], writes=[cbc[j]])
            em.op("scalar", lambda s, db=db, n=n, j=j, jd=jd: s.activation(
                out=da[j][:], in_=dtb[jd][:], func=AF.Exp,
                scale=ab[:, db * DSTATE + n:db * DSTATE + n + 1]),
                reads=[dtb[jd], ab], writes=[da[j]])
            em.op("vector", lambda v, j=j, jd=jd: v.tensor_mul(
                dbu[:], wb[jd][:], bbc[j][:]), reads=[wb[jd], bbc[j]], writes=[dbu])
            em.op("vector", lambda v, j=j: v.tensor_tensor_scan(
                out=hbb[:], data0=da[j][:], data1=dbu[:], initial=zcol[:],
                op0=ALU.mult, op1=ALU.add),
                reads=[da[j], dbu, zcol], writes=[hbb])
            if n == 0:
                em.op("vector", lambda v, j=j, jd=jd: v.tensor_mul(
                    yacc[jd][:], hbb[:], cbc[j][:]), reads=[hbb, cbc[j]], writes=[yacc[jd]])
            else:
                em.op("vector", lambda v, j=j: v.tensor_mul(
                    ynb[:], hbb[:], cbc[j][:]), reads=[hbb, cbc[j]], writes=[ynb])
                em.op("vector", lambda v, jd=jd: v.tensor_add(
                    yacc[jd][:], yacc[jd][:], ynb[:]), reads=[yacc[jd], ynb], writes=[yacc[jd]])
        em.dma(y_o, y_o[db * 128:(db + 1) * 128, :], yacc[jd], yacc[jd][:])
    em.pop()


def mk_bsel():
    b = np.zeros((48, DSTATE * 128), dtype=bfd)
    for n in range(DSTATE):
        b[n, n * 128:(n + 1) * 128] = 1.0
        b[32 + n, n * 128:(n + 1) * 128] = 1.0
    return b


def build_m2(nch=1024, slen=2 * TOK):
    em = Em("m2")
    em.cconst(0.0)
    t = {
        "xc": em.din("xc", (nch, slen), F32),
        "dt": em.din("dt", (nch, slen), F32),
        "bm": em.din("bm", (DSTATE, slen), F32),
        "cm": em.din("cm", (DSTATE, slen), F32),
        "alog": em.din("alog", (nch, DSTATE), F32),
        "bsel": em.din("bsel", (48, DSTATE * 128), BF16),
        "ym": em.dout("y", (nch, slen), F32),
    }
    emit_m2(em, t, nch, slen)
    em.finish()
    return em


def emit_m3(em, t, ntok=TOK):
    nt = ntok // 128
    x = t["x"]; xn1_i = t["xn1"]; y_i = t["ym"]; xc_i = t["xc"]; z_i = t["zm"]
    dvec = t["dvec"]; w_out = t["w_out"]; y_o = t["y"]
    em.push()
    dcol = em.sb("dcol", (128, 16), F32)
    em.dma(dcol, dcol[:], None, dvec[:])
    wout_bf = em.sb("wout_bf", (128, 16 * D), BF16)
    for k in range(16):
        em.dma(wout_bf, wout_bf[:, k * D:(k + 1) * D], None,
               w_out[k * 128:(k + 1) * 128, :], engine=_weng(w_out))
    ygb = [em.sb(f"ygb{k}", (128, ntok), BF16) for k in range(16)]
    ybf = em.sb("ybf", (128, TOK), F32)
    xcb = em.sb("xcb", (128, TOK), F32)
    zbf = em.sb("zbf", (128, TOK), F32)
    for k in range(16):
        em.dma(ybf, ybf[:, 0:ntok], None, y_i[k * 128:(k + 1) * 128, :])
        em.dma(xcb, xcb[:, 0:ntok], None, xc_i[k * 128:(k + 1) * 128, :])
        em.dma(zbf, zbf[:, 0:ntok], None, z_i[k * 128:(k + 1) * 128, :])
        em.op("vector", lambda v, k=k: v.scalar_tensor_tensor(
            out=ybf[:, 0:ntok], in0=xcb[:, 0:ntok], scalar=dcol[:, k:k + 1],
            in1=ybf[:, 0:ntok], op0=ALU.mult, op1=ALU.add),
            reads=[xcb, dcol, ybf], writes=[ybf])
        em.op("scalar", lambda s: s.activation(out=zbf[:, 0:ntok], in_=zbf[:, 0:ntok],
                                               func=AF.Silu),
              reads=[zbf], writes=[zbf])
        em.op("vector", lambda v, k=k: v.tensor_mul(ygb[k][:], ybf[:, 0:ntok], zbf[:, 0:ntok]),
              reads=[ybf, zbf], writes=[ygb[k]])
    xb = [em.sb(f"xb{j}", (128, D), F32) for j in range(2)]
    xnb = [em.sb(f"xnb{j}", (128, D), F32) for j in range(2)]
    yb = [em.sb(f"yb{j}", (128, D), F32) for j in range(2)]
    psO = em.ps("psO", (128, 1024), F32)
    for i in range(nt):
        xbb, xnn, ybb = xb[i % 2], xnb[i % 2], yb[i % 2]
        em.dma(xbb, xbb[:], None, x[i * 128:(i + 1) * 128, :])
        em.dma(xnn, xnn[:], None, xn1_i[i * 128:(i + 1) * 128, :])
        for dh in range(2):
            for k in range(16):
                em.op("tensor", lambda t, i=i, dh=dh, k=k: t.matmul(
                    psO[:, dh * 512:(dh + 1) * 512], ygb[k][:, i * 128:(i + 1) * 128],
                    wout_bf[:, k * D + dh * 512:k * D + (dh + 1) * 512],
                    start=(k == 0), stop=(k == 15)),
                    reads=[ygb[k], wout_bf], writes=[psO])
        em.op("vector", lambda v, xnn=xnn: v.tensor_add(xnn[:], xnn[:], psO[:]),
              reads=[xnn, psO], writes=[xnn])
        em.op("vector", lambda v, ybb=ybb, xbb=xbb, xnn=xnn: v.tensor_add(ybb[:], xnn[:], xbb[:]),
              reads=[xnn, xbb], writes=[ybb])
        em.dma(y_o, y_o[i * 128:(i + 1) * 128, :], ybb, ybb[:])
    em.pop()


def build_m3(ntok=TOK):
    em = Em("m3")
    em.cconst(0.0)
    t = {
        "x": em.din("x", (ntok, D), F32),
        "xn1": em.din("xn1", (ntok, D), F32),
        "ym": em.din("ym", (DI, ntok), F32),
        "xc": em.din("xc", (DI, ntok), F32),
        "zm": em.din("zm", (DI, ntok), F32),
        "dvec": em.din("dvec", (128, 16), F32),
        "w_out": em.din("w_out", (DI, D), F32),
        "y": em.dout("y", (ntok, D), F32),
    }
    emit_m3(em, t, ntok)
    em.finish()
    return em


_EMS = {}


def _get(name, builder):
    if name not in _EMS:
        _EMS[name] = builder()
    return _EMS[name]


def _run8(em, maps):
    if em.const_vals:
        ca = em.consts_input()
        for m in maps:
            m["consts_arr"] = ca
    res = run_bass_kernel_spmd(em.nc, maps, core_ids=list(range(NCORES)))
    return res.results


def _peer_call(x8, W_up, W_down, W_q, keys, de, ue):
    em = _get("peer", lambda: build_peer(ntok=TOK))
    base = peer_host_inputs(np.zeros((1,)), W_up, W_down, W_q, keys, de, ue)
    del base["x"]
    maps = [dict(base, x=np.ascontiguousarray(x8[c], dtype=np.float32)) for c in range(NCORES)]
    res = _run8(em, maps)
    return np.stack([r["y"] for r in res])


def _qffn_call(x8, w_int8, ws, bias):
    em = _get("qffn", lambda: build_qffn(ntok=TOK))
    base = qffn_host_inputs(np.zeros((1,)), w_int8, ws, bias)
    del base["x"]
    maps = [dict(base, x=np.ascontiguousarray(x8[c], dtype=np.float32)) for c in range(NCORES)]
    res = _run8(em, maps)
    return np.stack([r["y"] for r in res])


def _mamba_call(x8, ln_g, ln_b, W_in, conv_w, conv_b, W_x, W_dt, b_dt, A_log, Dv, W_out):
    def cvt16(v):
        return np.ascontiguousarray(np.asarray(v, np.float32).reshape(16, 128).T)
    em1 = _get("m1", lambda: build_m1(ntok=TOK))
    maps = []
    for c in range(NCORES):
        if c % 2 == 0:
            halo = np.zeros((3, D), np.float32)
            hm = np.zeros((3, 1), np.float32)
        else:
            halo = np.ascontiguousarray(x8[c - 1][TOK - 3:TOK], dtype=np.float32)
            hm = np.ones((3, 1), np.float32)
        maps.append({
            "x": np.ascontiguousarray(x8[c], dtype=np.float32), "xhalo": halo, "hmask": hm,
            "ln_g": np.asarray(ln_g, np.float32).reshape(1, D),
            "ln_b": np.asarray(ln_b, np.float32).reshape(1, D),
            "w_in": np.ascontiguousarray(W_in, dtype=np.float32),
            "conv_w": np.ascontiguousarray(conv_w, dtype=np.float32),
            "conv_b": cvt16(conv_b),
            "w_x": np.ascontiguousarray(W_x, dtype=np.float32),
            "w_dt": np.ascontiguousarray(W_dt, dtype=np.float32),
            "b_dt": cvt16(b_dt),
            "ident": np.eye(128, dtype=bfd),
        })
    r1 = _run8(em1, maps)
    em2 = _get("m2", lambda: build_m2(nch=1024, slen=2 * TOK))
    alog = np.ascontiguousarray(A_log, dtype=np.float32)
    maps2 = []
    for c in range(NCORES):
        b, half = c // 2, c % 2
        xc_full = np.concatenate([r1[2 * b]["xc"], r1[2 * b + 1]["xc"]], axis=1)
        dt_full = np.concatenate([r1[2 * b]["dt"], r1[2 * b + 1]["dt"]], axis=1)
        bc_full = np.concatenate([r1[2 * b]["bc"], r1[2 * b + 1]["bc"]], axis=1)
        maps2.append({
            "xc": np.ascontiguousarray(xc_full[half * 1024:(half + 1) * 1024]),
            "dt": np.ascontiguousarray(dt_full[half * 1024:(half + 1) * 1024]),
            "bm": np.ascontiguousarray(bc_full[0:16]),
            "cm": np.ascontiguousarray(bc_full[16:32]),
            "alog": np.ascontiguousarray(alog[half * 1024:(half + 1) * 1024]),
        })
    r2 = _run8(em2, maps2)
    em3 = _get("m3", lambda: build_m3(ntok=TOK))
    maps3 = []
    for c in range(NCORES):
        b, half = c // 2, c % 2
        y_full = np.concatenate([r2[2 * b]["y"], r2[2 * b + 1]["y"]], axis=0)
        maps3.append({
            "x": np.ascontiguousarray(x8[c], dtype=np.float32),
            "xn1": r1[c]["xn1"],
            "ym": np.ascontiguousarray(y_full[:, half * TOK:(half + 1) * TOK]),
            "xc": r1[c]["xc"], "zm": r1[c]["z"],
            "dvec": cvt16(Dv),
            "w_out": np.ascontiguousarray(W_out, dtype=np.float32),
        })
    r3 = _run8(em3, maps3)
    return np.stack([r["y"] for r in r3])


def _attn_call(x8, Wq, Wk, Wv, Wpost):
    em1 = _get("attn1", lambda: build_attn1(ntok=TOK))
    base = attn_host_inputs1(np.zeros((1,)), Wq, Wk, Wv)
    del base["x"]
    maps = [dict(base, x=np.ascontiguousarray(x8[c], dtype=np.float32)) for c in range(NCORES)]
    r1 = _run8(em1, maps)
    em2 = _get("attn2", lambda: build_attn2(ntok=TOK, sk=2 * TOK))
    maps2 = []
    wpost_c = np.ascontiguousarray(Wpost, dtype=np.float32)
    for c in range(NCORES):
        b = c // 2
        k_full = np.concatenate([r1[2 * b]["k"], r1[2 * b + 1]["k"]], axis=1)
        v_full = np.concatenate([r1[2 * b]["v"], r1[2 * b + 1]["v"]], axis=0)
        maps2.append({
            "x": np.ascontiguousarray(x8[c], dtype=np.float32),
            "xn": r1[c]["xn"], "q": r1[c]["q"],
            "k": np.ascontiguousarray(k_full), "v": np.ascontiguousarray(v_full),
            "wpost": wpost_c, "ident": np.eye(128, dtype=bfd),
        })
    r2 = _run8(em2, maps2)
    return np.stack([r["y"] for r in r2])


def build_fused():
    em = Em("fused", rd=False)
    em.cconst(0.0)
    ntok, sk = TOK, 2 * TOK
    p = em.preg

    x0 = em.din("x0", (ntok, D), F32)
    ident = em.din("ident", (128, 128), BF16)
    y_out = em.dout("yout", (ntok, D), F32)
    id_t = em.sb("id_t", (128, 128), BF16)
    em.dma(id_t, id_t[:], None, ident[:])

    t_peer = peer_dins(em, ntok)
    t_peer["id_t"] = id_t
    t_qffn = {
        "w_int8T": em.din("w_int8T", (D, D), I32),
        "wscale_col": em.din("wscale_col", (128, 1), F32),
        "bias_row": em.din("bias_row", (1, D), F32),
        "ident": ident, "id_t": id_t,
    }
    t_m1 = {
        "hmask": em.din("hmask", (3, 1), F32),
        "ln_g": em.din("ln_g", (1, D), F32),
        "ln_b": em.din("ln_b", (1, D), F32),
        "w_in": em.din("w_in", (D, 2 * DI), F32),
        "conv_w": em.din("conv_w", (DI, DCONV), F32),
        "conv_b": em.din("conv_b", (128, 16), F32),
        "w_x": em.din("w_x", (DI, 96), F32),
        "w_dt": em.din("w_dt", (DTRANK, DI), F32),
        "b_dt": em.din("b_dt", (128, 16), F32),
        "id_t": id_t,
    }
    alog = em.din("alog", (1024, DSTATE), F32)
    bsel_d = em.din("bsel", (48, DSTATE * 128), BF16)
    t_attn = {
        "wq": em.din("wq", (D, D), F32),
        "wk": em.din("wk", (D, 256), F32),
        "wv": em.din("wv", (D, 256), F32),
        "wpost": em.din("wpost", (D, D), F32),
    }
    dvec = em.din("dvec", (128, 16), F32)
    w_out_m = em.din("w_out", (DI, D), F32)

    x1 = em.dtmp("x1", (ntok, D), F32)
    x2 = em.dtmp("x2", (ntok, D), F32)
    x3 = em.dtmp("x3", (ntok, D), F32)
    x4 = em.dtmp("x4", (ntok, D), F32)
    x5 = em.dtmp("x5", (ntok, D), F32)
    xn1_loc = em.dtmp("xn1_loc", (ntok, D), F32)
    z_loc = em.dtmp("z_loc", (DI, ntok), F32)
    xn_a_loc = em.dtmp("xn_a_loc", (ntok, D), F32)
    q_loc = em.dtmp("q_loc", (D, ntok), BF16)
    bscr = em.dtmp("bscr2", (1, 16), F32)
    rd_dram = em.dtmp("rd_dram2", (AH, ntok), F32)

    halo_sh = em.dshared("halo_sh", (2, 3, D), F32)
    xc_sh = em.dshared("xc_sh", (DI, sk), F32)
    dt_sh = em.dshared("dt_sh", (DI, sk), F32)
    bc_sh = em.dshared("bc_sh", (32, sk), F32)
    y_sh = em.dshared("y_sh", (DI, sk), F32)
    k_sh = em.dshared("k_sh", (256, sk), BF16)
    v_sh = em.dshared("v_sh", (sk, 256), BF16)

    mir = {}
    plan = []
    for nm, b, rows, cols in [
            ("deT", t_peer["deT"], D, NEXP),
            ("wupT", t_peer["wupT"], D, IFF), ("wdnT", t_peer["wdnT"], IFF, D),
            ("ue", t_peer["ue"], NEXP, D)]:
        mir[nm] = em.dtmp(nm + "_bfm", (rows, cols), BF16)
        plan.append((b, mir[nm], rows, cols))
    t_peer.update(deT=mir["deT"], wupT=mir["wupT"],
                  wdnT=mir["wdnT"], ue=mir["ue"])

    em.push()
    emit_peer(em, dict(t_peer, x=x0, y=x1), ntok,
              hook=lambda: emit_precast(em, plan))
    em.pop()
    em.fence()
    em.push()
    hb = em.sb("halo_b", (3, D), F32)
    em.dma(hb, hb[:], x1, x1[ntok - 3:ntok, :])
    em.dma(halo_sh, halo_sh[bass.ts(p, 1), :, :].squeeze(0), hb, hb[:])
    em.pop()
    em.barrier(protects=[halo_sh])

    em.push()
    emit_m1(em, dict(t_m1, x=x1,
                     xhalo=BufView(halo_sh, halo_sh.t[0]),
                     xn1=xn1_loc,
                     xc=BufView(xc_sh, xc_sh.t[:, bass.ts(p, ntok)]),
                     z=z_loc,
                     dt=BufView(dt_sh, dt_sh.t[:, bass.ts(p, ntok)]),
                     bc=BufView(bc_sh, bc_sh.t[:, bass.ts(p, ntok)])), ntok)
    em.pop()
    em.fence()
    em.barrier(protects=[xc_sh, dt_sh, bc_sh])
    em.push()
    emit_m2(em, dict(xc=BufView(xc_sh, xc_sh.t[bass.ts(p, 1024), :]),
                     dt=BufView(dt_sh, dt_sh.t[bass.ts(p, 1024), :]),
                     bm=BufView(bc_sh, bc_sh.t[0:16, :]),
                     cm=BufView(bc_sh, bc_sh.t[16:32, :]),
                     alog=alog, bsel=bsel_d,
                     ym=BufView(y_sh, y_sh.t[bass.ts(p, 1024), :])), 1024, sk)
    em.pop()
    em.fence()
    em.barrier(protects=[y_sh])
    em.push()
    emit_m3(em, dict(x=x1, xn1=xn1_loc,
                     ym=BufView(y_sh, y_sh.t[:, bass.ts(p, ntok)]),
                     xc=BufView(xc_sh, xc_sh.t[:, bass.ts(p, ntok)]),
                     zm=z_loc, dvec=dvec, w_out=w_out_m, y=x2), ntok)
    em.pop()
    em.fence()

    em.push()
    emit_peer(em, dict(t_peer, x=x2, y=x3), ntok)
    em.pop()
    em.fence()

    em.push()
    emit_qffn(em, dict(t_qffn, x=x3, y=x4), ntok)
    em.pop()
    em.fence()

    em.push()
    emit_attn1(em, dict(t_attn, id_t=id_t, x=x4, xn=xn_a_loc, q=q_loc,
                        k=BufView(k_sh, k_sh.t[:, bass.ts(p, ntok)]),
                        v=BufView(v_sh, v_sh.t[bass.ts(p, ntok), :])), ntok)
    em.pop()
    em.fence()
    em.barrier(protects=[k_sh, v_sh])
    em.push()
    emit_attn2(em, dict(t_attn, id_t=id_t, x=x4, xn=xn_a_loc, q=q_loc,
                        k=k_sh, v=v_sh, y=x5, bscr=bscr, rd_dram=rd_dram), ntok, sk)
    em.pop()
    em.fence()

    em.push()
    emit_qffn(em, dict(t_qffn, x=x5, y=y_out), ntok)
    em.pop()
    em.finish()
    return em


def kernel(L0_attn_x, L0_attn_ln_g, L0_attn_ln_b, L0_attn_W_in, L0_attn_conv_w,
           L0_attn_conv_b, L0_attn_W_x, L0_attn_W_dt, L0_attn_b_dt, L0_attn_A_log,
           L0_attn_D, L0_attn_W_out, L0_ffn_W_up, L0_ffn_W_down, L0_ffn_W_q,
           L0_ffn_keys, L0_ffn_down_embed, L0_ffn_up_embed, L1_attn_Wq, L1_attn_Wk,
           L1_attn_Wv, L1_attn_Wpost, L1_ffn_w_int8, L1_ffn_weight_scale, L1_ffn_bias):
    def cvt16(v):
        return np.ascontiguousarray(np.asarray(v, np.float32).reshape(16, 128).T)

    xf = np.asarray(L0_attn_x, dtype=np.float32).reshape(NCORES * TOK, D)
    em = _get("fused", build_fused)
    base = {
        "ident": np.eye(128, dtype=bfd),
        "wqT": np.ascontiguousarray(np.asarray(L0_ffn_W_q, np.float32).T),
        "keysP": np.ascontiguousarray(np.asarray(L0_ffn_keys, np.float32).transpose(2, 0, 3, 1)),
        "deT": np.ascontiguousarray(np.asarray(L0_ffn_down_embed, np.float32).T),
        "ue": np.ascontiguousarray(np.asarray(L0_ffn_up_embed, np.float32)),
        "wupT": np.ascontiguousarray(np.asarray(L0_ffn_W_up, np.float32).T),
        "wdnT": np.ascontiguousarray(np.asarray(L0_ffn_W_down, np.float32).T),
        "w_int8T": np.ascontiguousarray(np.asarray(L1_ffn_w_int8, np.int32).T),
        "wscale_col": np.ascontiguousarray(np.broadcast_to(
            np.asarray(L1_ffn_weight_scale, np.float32).reshape(1, 1), (128, 1))),
        "bias_row": np.ascontiguousarray(np.asarray(L1_ffn_bias, np.float32).reshape(1, D)),
        "ln_g": np.asarray(L0_attn_ln_g, np.float32).reshape(1, D),
        "ln_b": np.asarray(L0_attn_ln_b, np.float32).reshape(1, D),
        "w_in": np.ascontiguousarray(np.asarray(L0_attn_W_in, np.float32)),
        "conv_w": np.ascontiguousarray(np.asarray(L0_attn_conv_w, np.float32)),
        "conv_b": cvt16(L0_attn_conv_b),
        "w_x": np.ascontiguousarray(np.asarray(L0_attn_W_x, np.float32)),
        "w_dt": np.ascontiguousarray(np.asarray(L0_attn_W_dt, np.float32)),
        "b_dt": cvt16(L0_attn_b_dt),
        "wq": np.ascontiguousarray(np.asarray(L1_attn_Wq, np.float32)),
        "wk": np.ascontiguousarray(np.asarray(L1_attn_Wk, np.float32)),
        "wv": np.ascontiguousarray(np.asarray(L1_attn_Wv, np.float32)),
        "wpost": np.ascontiguousarray(np.asarray(L1_attn_Wpost, np.float32)),
        "dvec": cvt16(L0_attn_D),
        "w_out": np.ascontiguousarray(np.asarray(L0_attn_W_out, np.float32)),
        "bsel": mk_bsel(),
        "consts_arr": em.consts_input(),
    }
    alog_f = np.ascontiguousarray(np.asarray(L0_attn_A_log, np.float32))
    nonce = int(np.random.randint(1, 2 ** 30))
    barseq = (nonce + np.arange(16, dtype=np.int64)).astype(np.int32).reshape(1, 16)
    maps = []
    for c in range(NCORES):
        par = c % 2
        m = dict(base)
        m["x0"] = np.ascontiguousarray(xf[c * TOK:(c + 1) * TOK])
        m["par"] = np.array([[par, 1 - par]], np.int32)
        m["barseq"] = barseq
        m["hmask"] = (np.zeros((3, 1), np.float32) if par == 0
                      else np.ones((3, 1), np.float32))
        m["alog"] = np.ascontiguousarray(alog_f[par * 1024:(par + 1) * 1024])
        maps.append(m)
    res = run_bass_kernel_spmd(em.nc, maps, core_ids=list(range(NCORES)))
    out = np.stack([r["yout"] for r in res.results])
    return np.ascontiguousarray(out.reshape(4, 4096, D))


# version 19
# speedup vs baseline: 1.0397x; 1.0397x over previous
import contextlib
import numpy as np
import ml_dtypes
import concourse.bass as bass
import concourse.bacc as bacc
import concourse.mybir as mybir
from concourse.bass_utils import run_bass_kernel_spmd

F32 = mybir.dt.float32
BF16 = mybir.dt.bfloat16
I32 = mybir.dt.int32
U32 = mybir.dt.uint32
AF = mybir.ActivationFunctionType
ALU = mybir.AluOpType
AX = mybir.AxisListType

NCORES = 8
TOK = 2048
D = 1024
DI = 2048
DSTATE = 16
DTRANK = 64
DCONV = 4
IFF = 4096
NEXP = 4096
NKEY = 64
HEADS_P = 4
AH = 16
AG = 4
AE = 64
LN_EPS = 1e-5
RMS_EPS = 1e-6
ACT_EPS = 1e-5
BIG = 1e30

bfd = ml_dtypes.bfloat16

_ENGINES = ("sync", "vector", "scalar", "tensor", "gpsimd")


class Buf:
    __slots__ = ("t", "wtok", "rtoks", "name")

    def __init__(self, t, name):
        self.t = t
        self.name = name
        self.wtok = None
        self.rtoks = []

    def __getitem__(self, key):
        return self.t[key]

    @property
    def ap(self):
        return self.t[:] if hasattr(self.t, "__enter__") or not hasattr(self.t, "ap") else self.t


class BufView:
    def __init__(self, parent, ap):
        self.parent = parent
        self.t = ap
        self.name = parent.name + "_view"

    def __getitem__(self, key):
        return self.t[key]

    @property
    def wtok(self):
        return self.parent.wtok

    @wtok.setter
    def wtok(self, v):
        self.parent.wtok = v

    @property
    def rtoks(self):
        return self.parent.rtoks

    @rtoks.setter
    def rtoks(self, v):
        self.parent.rtoks = v


class Em:

    def __init__(self, name, rd=True):
        self.nc = bacc.Bacc("TRN2", target_bir_lowering=False, debug=False,
                            detect_race_conditions=rd)
        self.ctx = contextlib.ExitStack()
        self.name = name
        self.out_names = []
        self.sems = {}
        self.cnt = {}
        for e in _ENGINES:
            self.sems[e] = self.ctx.enter_context(self.nc.semaphore("sem_" + e))
            self.cnt[e] = 0
        self.ndma = 20
        for i in range(self.ndma):
            self.sems[f"dma{i}"] = self.ctx.enter_context(self.nc.semaphore(f"sem_dma{i}"))
            self.cnt[f"dma{i}"] = 0
        self._dma_rr = 0
        self.ngdma = 8
        for i in range(self.ngdma):
            self.sems[f"gdma{i}"] = self.ctx.enter_context(self.nc.semaphore(f"sem_gdma{i}"))
            self.cnt[f"gdma{i}"] = 0
        self._gdma_rr = 0
        self._waited = {e: {} for e in _ENGINES}
        self._cbuf = None
        self.const_vals = []
        self._scopes = []
        cd = self.din("consts_arr", (1, 32), F32)
        self._cbuf_pending = cd
        self._bar = None
        self.nbar = 0

    def din(self, name, shape, dtype):
        return Buf(self.nc.dram_tensor(name, list(shape), dtype, kind="ExternalInput").ap(), name)

    def dout(self, name, shape, dtype):
        self.out_names.append(name)
        return Buf(self.nc.dram_tensor(name, list(shape), dtype, kind="ExternalOutput").ap(), name)

    def dtmp(self, name, shape, dtype):
        return Buf(self.nc.dram_tensor(name, list(shape), dtype, kind="Internal").ap(), name)

    def sb(self, name, shape, dtype):
        st = self._scopes[-1] if self._scopes else self.ctx
        self._uctr = getattr(self, "_uctr", 0) + 1
        name = f"{name}_u{self._uctr}"
        return Buf(st.enter_context(self.nc.sbuf_tensor(name, list(shape), dtype)), name)

    def ps(self, name, shape, dtype=F32):
        st = self._scopes[-1] if self._scopes else self.ctx
        self._uctr = getattr(self, "_uctr", 0) + 1
        name = f"{name}_u{self._uctr}"
        return Buf(st.enter_context(self.nc.psum_tensor(name, list(shape), dtype)), name)

    def push(self):
        self._scopes.append(contextlib.ExitStack())

    def pop(self):
        self.fence()
        self._scopes.pop().close()

    def _wait(self, engine, tok):
        if tok is None:
            return
        sem, n = tok
        key = id(sem)
        w = self._waited[engine]
        if w.get(key, -1) >= n:
            return
        getattr(self.nc, engine).wait_ge(sem, n)
        w[key] = n

    SKIP_SAME_ENGINE = ("tensor",)

    def _skip(self, engine, tok):
        if tok[0] is self.sems.get(engine):
            return engine in self.SKIP_SAME_ENGINE
        return False

    def _pre(self, engine, reads, writes, war=True):
        for b in reads:
            if b is not None and b.wtok is not None:
                tok = b.wtok
                if self._skip(engine, tok):
                    continue
                self._wait(engine, tok)
        for b in writes:
            if b is None:
                continue
            if b.wtok is not None:
                tok = b.wtok
                if not self._skip(engine, b.wtok):
                    self._wait(engine, b.wtok)
            if war:
                for tok in b.rtoks:
                    if self._skip(engine, tok):
                        continue
                    self._wait(engine, tok)

    def _post(self, engine, inst, reads, writes, sem_name=None, inc=1):
        sem_name = sem_name or engine
        sem = self.sems[sem_name]
        inst.then_inc(sem, inc)
        self.cnt[sem_name] += inc
        tok = (sem, self.cnt[sem_name])
        for b in writes:
            if b is not None:
                b.wtok = tok
                b.rtoks = []
        for b in reads:
            if b is not None:
                b.rtoks.append(tok)
        return tok

    def cconst(self, val, rows=128):
        val = float(val)
        if self._cbuf is None:
            assert not self._scopes, "first cconst must happen in base scope; pre-warm with em.cconst(0.0)"
            self._cbuf = self.sb("consts_buf", (128, 32), F32)
            self.dma(self._cbuf, self._cbuf[:], None, self._cbuf_pending[:].to_broadcast([128, 32]))
        if val not in self.const_vals:
            assert len(self.const_vals) < 32
            self.const_vals.append(val)
        i = self.const_vals.index(val)
        return self._cbuf[0:rows, i:i + 1]

    def consts_input(self):
        a = np.zeros((1, 32), np.float32)
        for i, v in enumerate(self.const_vals):
            a[0, i] = v
        return a

    def op(self, engine, fn, reads=(), writes=()):
        reads = list(reads)
        if self._cbuf is not None:
            reads.append(self._cbuf)
        self._pre(engine, reads, writes)
        inst = fn(getattr(self.nc, engine))
        return self._post(engine, inst, reads, writes)

    def dma(self, dst, dst_ap, src, src_ap, engine="sync"):
        reads = [src] if src is not None else []
        writes = [dst] if dst is not None else []
        self._pre(engine, reads, writes)
        if engine == "gpsimd":
            sname = f"gdma{self._gdma_rr}"
            self._gdma_rr = (self._gdma_rr + 1) % self.ngdma
        else:
            sname = f"dma{self._dma_rr}"
            self._dma_rr = (self._dma_rr + 1) % self.ndma
        if self.cnt[sname] > 0:
            self._wait(engine, (self.sems[sname], self.cnt[sname]))
        inst = getattr(self.nc, engine).dma_start(out=dst_ap, in_=src_ap)
        return self._post(engine, inst, reads, writes, sem_name=sname, inc=16)

    def dshared(self, name, shape, dtype):
        return Buf(self.nc.dram_tensor(name, list(shape), dtype, kind="Internal",
                                       addr_space="Shared").ap(), name)

    def _bar_init(self):
        if self._bar is not None:
            return self._bar
        par = self.din("par", (1, 2), I32)
        bseq = self.din("barseq", (1, 16), I32)
        pb = self.sb("bar_pb", (1, 2), I32)
        bs = self.sb("bar_bs", (1, 16), I32)
        fl = self.sb("bar_fl", (1, 1), I32)
        flags = self.dshared("bar_flags", (2, 16), I32)
        self.dma(pb, pb[:], None, par[:])
        self.dma(bs, bs[:], None, bseq[:])
        self.op("vector", lambda v: v.memset(fl[:], 0), reads=[], writes=[fl])
        self._wait("sync", pb.wtok)
        self._wait("sync", bs.wtok)
        self._wait("sync", fl.wtok)
        p = self.nc.sync.value_load(pb[0:1, 0:1], min_val=0, max_val=1)
        q = self.nc.sync.value_load(pb[0:1, 1:2], min_val=0, max_val=1)
        spin_sem = self.ctx.enter_context(self.nc.semaphore("sem_spin"))
        spin_tgt = self.nc.sync.to_reg(0)
        self._bar = dict(pb=pb, bs=bs, fl=fl, flags=flags, p=p, q=q,
                         spin_sem=spin_sem, spin_tgt=spin_tgt)
        return self._bar

    @property
    def preg(self):
        return self._bar_init()["p"]

    def barrier(self, protects=()):
        b = self._bar_init()
        k = self.nbar
        self.nbar += 1
        assert k < 16
        nc = self.nc
        for buf in protects:
            if buf.wtok is not None:
                self._wait("sync", buf.wtok)
        inst = nc.sync.dma_start(out=b["flags"][bass.ts(b["p"], 1), k:k + 1]
                                 .squeeze(0).unsqueeze(0), in_=b["bs"][0:1, k:k + 1])
        sname = f"dma{self._dma_rr}"
        self._dma_rr = (self._dma_rr + 1) % self.ndma
        if self.cnt[sname] > 0:
            self._wait("sync", (self.sems[sname], self.cnt[sname]))
        inst.then_inc(self.sems[sname], 16)
        self.cnt[sname] += 16
        self._wait("sync", (self.sems[sname], self.cnt[sname]))
        tgt = nc.sync.value_load(b["bs"][0:1, k:k + 1])

        def cond():
            v = nc.sync.value_load(b["fl"][0:1, 0:1])
            return v != tgt
        with nc.sync.While(cond):
            nc.sync.dma_start(out=b["fl"][:], in_=b["flags"][bass.ts(b["q"], 1), k:k + 1]
                              .squeeze(0).unsqueeze(0)).then_inc(b["spin_sem"], 16)
            nc.sync.reg_add(b["spin_tgt"], b["spin_tgt"], 16)
            nc.sync.wait_ge(b["spin_sem"], b["spin_tgt"])
        tok_inst = nc.sync.nop()
        tok = self._post("sync", tok_inst, [], list(protects))
        return tok

    def fence(self, engines=_ENGINES):
        for e in engines:
            for s_name, sem in self.sems.items():
                if self.cnt[s_name] > 0:
                    self._wait(e, (sem, self.cnt[s_name]))

    def finish(self, out_dma_engine="sync"):
        self.fence(engines=(out_dma_engine,))
        self.nc.compile()

    def close(self):
        self.ctx.close()




def run_em(em, in_map, core_ids=None, n_cores=1, trace=False):
    core_ids = core_ids if core_ids is not None else list(range(n_cores))
    maps = in_map if isinstance(in_map, list) else [in_map] * len(core_ids)
    res = run_bass_kernel_spmd(em.nc, maps, core_ids=core_ids, trace=trace)
    return res



def emit_ln(em, x_t, xn_t, scr, stats, width=D, eps=LN_EPS, x_slice=None, xn_slice=None):
    xap = x_t[:] if x_slice is None else x_slice
    oap = xn_t[:] if xn_slice is None else xn_slice
    s_sum = stats[:, 0:1]
    s_sq = stats[:, 1:2]
    s_mu = stats[:, 2:3]
    s_var = stats[:, 3:4]
    s_rstd = stats[:, 4:5]
    s_b = stats[:, 5:6]
    em.op("vector", lambda v: v.reduce_sum(out=s_sum, in_=xap, axis=AX.X),
          reads=[x_t], writes=[stats])
    em.op("scalar", lambda s: s.activation(out=scr[:, 0:width], in_=xap, func=AF.Square,
                                           accum_out=s_sq),
          reads=[x_t], writes=[scr, stats])
    em.op("vector", lambda v: v.tensor_scalar_mul(s_mu, s_sum, 1.0 / width),
          reads=[stats], writes=[stats])
    em.op("vector", lambda v: v.tensor_mul(s_var, s_mu, s_mu), reads=[stats], writes=[stats])
    em.op("vector", lambda v: v.scalar_tensor_tensor(out=s_var, in0=s_sq, scalar=1.0 / width,
                                                     in1=s_var, op0=ALU.mult, op1=ALU.subtract),
          reads=[stats], writes=[stats])
    parts = stats.t.shape[0] if hasattr(stats.t, 'shape') else 128
    epsc = em.cconst(eps, rows=parts)
    em.op("scalar", lambda s: s.activation(out=s_rstd, in_=s_var, func=AF.Ln, bias=epsc),
          reads=[stats], writes=[stats])
    em.op("scalar", lambda s: s.activation(out=s_rstd, in_=s_rstd, func=AF.Exp, scale=-0.5),
          reads=[stats], writes=[stats])
    em.op("vector", lambda v: v.scalar_tensor_tensor(out=s_b, in0=s_mu, scalar=-1.0,
                                                     in1=s_rstd, op0=ALU.mult, op1=ALU.mult),
          reads=[stats], writes=[stats])
    em.op("scalar", lambda s: s.activation(out=oap, in_=xap, func=AF.Identity,
                                           scale=s_rstd, bias=s_b),
          reads=[x_t, stats], writes=[xn_t])


def emit_transpose128(em, src_t, src_ap, ps_t, ps_ap, dst_t, dst_ap, ident_t):
    em.op("tensor", lambda t: t.transpose(ps_ap, src_ap, ident_t[:]),
          reads=[src_t, ident_t], writes=[ps_t])
    em.op("vector", lambda v: v.tensor_copy(out=dst_ap, in_=ps_ap),
          reads=[ps_t], writes=[dst_t])


def _weng(b):
    return "sync" if b.t.dtype == BF16 else "gpsimd"


def emit_precast(em, plan):
    em.push()
    CW = 4096
    nbuf = 4
    pin = [em.sb(f"pci{j}", (128, CW), F32) for j in range(nbuf)]
    pot = [em.sb(f"pco{j}", (128, CW), BF16) for j in range(nbuf)]
    k = 0
    for srcb, dstb, rows, cols in plan:
        for r in range(0, rows, 128):
            for c0 in range(0, cols, CW):
                w = min(CW, cols - c0)
                ib, ob = pin[k % nbuf], pot[k % nbuf]
                em.dma(ib, ib[:, 0:w], None, srcb[r:r + 128, c0:c0 + w])
                if k % 2 == 0:
                    em.op("vector", lambda v, ib=ib, ob=ob, w=w: v.tensor_copy(
                        out=ob[:, 0:w], in_=ib[:, 0:w]), reads=[ib], writes=[ob])
                else:
                    em.op("scalar", lambda s, ib=ib, ob=ob, w=w: s.activation(
                        out=ob[:, 0:w], in_=ib[:, 0:w], func=AF.Identity),
                        reads=[ib], writes=[ob])
                em.dma(dstb, dstb[r:r + 128, c0:c0 + w], ob, ob[:, 0:w])
                k += 1
    em.pop()


def emit_qffn(em, t, ntok=TOK):
    nt = ntok // 128
    x = t["x"]; wT = t["w_int8T"]; wsc = t["wscale_col"]; bias = t["bias_row"]
    ident = t["ident"]; y = t["y"]
    em.push()
    wt_i32 = em.sb("wt_i32", (128, 8 * D), I32)
    wt_bf = em.sb("wt_bf", (128, 8 * D), BF16)
    bias_bc = em.sb("bias_bc", (128, D), F32)
    wsc_t = em.sb("wsc_t", (128, 1), F32)
    id_t = em.sb("id_t", (128, 128), BF16)

    NS = 3
    xt = [em.sb(f"xt{i}", (128, D), F32) for i in range(NS)]
    scr_a = [em.sb(f"scr{j}", (128, D), F32) for j in range(NS)]
    xn_a = [em.sb(f"xn{j}", (128, D), F32) for j in range(NS)]
    qbf_a = [em.sb(f"qbf{j}", (128, D), BF16) for j in range(NS)]
    qi_a = [em.sb(f"qi{j}", (128, D), I32) for j in range(NS)]
    stats_a = [em.sb(f"stats{j}", (128, 8), F32) for j in range(NS)]
    st2_a = [em.sb(f"st2{j}", (128, 8), F32) for j in range(NS)]
    qfm_a = [em.sb(f"qfm{j}", (128, D), BF16) for j in range(2)]
    ot_a = [em.sb(f"ot{j}", (128, D), F32) for j in range(2)]
    yt = [em.sb(f"yt{i}", (128, D), F32) for i in range(2)]

    psM_a = [em.ps(f"psM{j}", (128, 1024), F32) for j in range(2)]
    psT_a = [em.ps(f"psT{j}", (128, 512), BF16) for j in range(2)]

    em.dma(id_t, id_t[:], None, ident[:])
    em.dma(wsc_t, wsc_t[:], None, wsc[:])
    em.dma(bias_bc, bias_bc[:], None, bias[:].to_broadcast([128, D]))
    for c in range(8):
        em.dma(wt_i32, wt_i32[:, c * D:(c + 1) * D], None, wT[c * 128:(c + 1) * 128, :])
    em.op("vector", lambda v: v.tensor_copy(out=wt_bf[:], in_=wt_i32[:]),
          reads=[wt_i32], writes=[wt_bf])

    def stage0(i):
        xb, scr, xn, qbf, qi = xt[i % NS], scr_a[i % NS], xn_a[i % NS], qbf_a[i % NS], qi_a[i % NS]
        stats, st2 = stats_a[i % NS], st2_a[i % NS]
        em.dma(xb, xb[:], None, x[i * 128:(i + 1) * 128, :])
        emit_ln(em, xb, xn, scr, stats)
        s_ms = st2[:, 0:1]; s_r2 = st2[:, 1:2]; s_am = st2[:, 2:3]
        s_scale = st2[:, 3:4]; s_comb = st2[:, 4:5]; s_fact = st2[:, 5:6]
        em.op("scalar", lambda s: s.activation(out=scr[:], in_=xn[:], func=AF.Square,
                                               accum_out=s_ms), reads=[xn], writes=[scr, st2])
        em.op("scalar", lambda s: s.activation(out=s_r2, in_=s_ms, func=AF.Ln,
                                               scale=1.0 / D, bias=em.cconst(RMS_EPS)),
              reads=[st2], writes=[st2])
        em.op("scalar", lambda s: s.activation(out=s_r2, in_=s_r2, func=AF.Exp, scale=-0.5),
              reads=[st2], writes=[st2])
        em.op("vector", lambda v: v.reduce_max(out=s_am, in_=xn[:], axis=AX.X,
                                               apply_absolute_value=True),
              reads=[xn], writes=[st2])
        em.op("vector", lambda v: v.tensor_mul(s_am, s_am, s_r2), reads=[st2], writes=[st2])
        em.op("vector", lambda v: v.tensor_scalar_max(s_am, s_am, float(ACT_EPS)),
              reads=[st2], writes=[st2])
        em.op("vector", lambda v: v.reciprocal(out=s_scale, in_=s_am), reads=[st2], writes=[st2])
        em.op("vector", lambda v: v.scalar_tensor_tensor(out=s_comb, in0=s_scale, scalar=127.0,
                                                         in1=s_r2, op0=ALU.mult, op1=ALU.mult),
              reads=[st2], writes=[st2])
        em.op("vector", lambda v: v.scalar_tensor_tensor(out=s_fact, in0=s_am,
                                                         scalar=1.0 / 127.0, in1=wsc_t[:],
                                                         op0=ALU.mult, op1=ALU.mult),
              reads=[st2, wsc_t], writes=[st2])
        em.op("scalar", lambda s: s.activation(out=scr[:], in_=xn[:], func=AF.Identity,
                                               scale=s_comb), reads=[xn, st2], writes=[scr])
        em.op("vector", lambda v: v.tensor_scalar(out=scr[:], in0=scr[:], scalar1=127.0,
                                                  scalar2=-128.0, op0=ALU.min, op1=ALU.max),
              reads=[scr], writes=[scr])
        em.op("vector", lambda v: v.tensor_copy(out=qi[:], in_=scr[:]), reads=[scr], writes=[qi])
        em.op("vector", lambda v: v.tensor_copy(out=xn[:], in_=qi[:]), reads=[qi], writes=[xn])
        em.op("vector", lambda v: v.tensor_sub(scr[:], scr[:], xn[:]), reads=[scr, xn], writes=[scr])
        em.op("vector", lambda v: v.tensor_scalar(out=qbf[:], in0=scr[:], scalar1=0.5,
                                                  scalar2=None, op0=ALU.is_gt),
              reads=[scr], writes=[qbf])
        em.op("vector", lambda v: v.tensor_add(xn[:], xn[:], qbf[:]), reads=[xn, qbf], writes=[xn])
        em.op("vector", lambda v: v.tensor_scalar(out=qbf[:], in0=scr[:], scalar1=-0.5,
                                                  scalar2=None, op0=ALU.is_lt),
              reads=[scr], writes=[qbf])
        em.op("vector", lambda v: v.tensor_sub(xn[:], xn[:], qbf[:]), reads=[xn, qbf], writes=[xn])
        em.op("vector", lambda v: v.tensor_copy(out=qbf[:], in_=xn[:]), reads=[xn], writes=[qbf])

    def stage1(i):
        qbf, qfm, psT = qbf_a[i % NS], qfm_a[i % 2], psT_a[i % 2]
        for c in range(8):
            emit_transpose128(em, qbf, qbf[:, c * 128:(c + 1) * 128], psT, psT[:, 0:128],
                              qfm, qfm[:, c * 128:(c + 1) * 128], id_t)

    def stage2(i):
        qfm, ot, yb, xb = qfm_a[i % 2], ot_a[i % 2], yt[i % 2], xt[i % NS]
        st2, psM = st2_a[i % NS], psM_a[i % 2]
        s_fact = st2[:, 5:6]
        for h in range(2):
            for c in range(8):
                em.op("tensor",
                      lambda t, h=h, c=c: t.matmul(
                          psM[:, h * 512:(h + 1) * 512],
                          qfm[:, c * 128:(c + 1) * 128],
                          wt_bf[:, c * D + h * 512: c * D + (h + 1) * 512],
                          start=(c == 0), stop=(c == 7)),
                      reads=[qfm, wt_bf], writes=[psM])
        em.op("scalar", lambda s: s.activation(out=ot[:], in_=psM[:], func=AF.Identity,
                                               scale=s_fact), reads=[psM, st2], writes=[ot])
        em.op("vector", lambda v: v.tensor_add(ot[:], ot[:], bias_bc[:]),
              reads=[ot, bias_bc], writes=[ot])
        em.op("vector", lambda v: v.scalar_tensor_tensor(out=yb[:], in0=ot[:], scalar=0.5,
                                                         in1=xb[:], op0=ALU.mult, op1=ALU.add),
              reads=[ot, xb], writes=[yb])
        em.dma(y, y[i * 128:(i + 1) * 128, :], yb, yb[:])

    for k in range(nt + 2):
        if k < nt:
            stage0(k)
        if 1 <= k <= nt:
            stage1(k - 1)
        if k >= 2:
            stage2(k - 2)
    em.pop()


def build_qffn(ntok=TOK):
    em = Em("qffn")
    em.cconst(0.0)
    t = {
        "x": em.din("x", (ntok, D), F32),
        "w_int8T": em.din("w_int8T", (D, D), I32),
        "wscale_col": em.din("wscale_col", (128, 1), F32),
        "bias_row": em.din("bias_row", (1, D), F32),
        "ident": em.din("ident", (128, 128), BF16),
        "y": em.dout("y", (ntok, D), F32),
    }
    emit_qffn(em, t, ntok)
    em.finish()
    return em


def qffn_host_inputs(x_shard, w_int8, weight_scale, bias, em=None):
    return {
        "x": np.ascontiguousarray(x_shard, dtype=np.float32),
        "w_int8T": np.ascontiguousarray(w_int8.T).astype(np.int32, copy=False),
        "wscale_col": np.ascontiguousarray(np.broadcast_to(np.asarray(weight_scale, np.float32).reshape(1, 1), (128, 1))),
        "bias_row": np.ascontiguousarray(bias, dtype=np.float32).reshape(1, D),
        "ident": np.eye(128, dtype=bfd),
    }




def emit_peer(em, t, ntok=TOK, dbg=False, hook=None):
    nt = ntok // 128
    x = t["x"]; wqT = t["wqT"]; keysP = t["keysP"]; deT = t["deT"]; ue = t["ue"]
    wupT = t["wupT"]; wdnT = t["wdnT"]; y = t["y"]
    wt_dram = t["wt_dram"]; u_dram = t["u_dram"]; id_t = t["id_t"]
    if dbg:
        dbg_xn = em.dout("dbg_xn", (128, D), BF16)
        dbg_sim = em.dout("dbg_sim", (128, 512), F32)
        dbg_sxm = em.dout("dbg_sxm", (128, 256), BF16)
        dbg_sym = em.dout("dbg_sym", (128, 256), BF16)
        dbg_small = em.dout("dbg_small", (128, 12), F32)
        dbg_proj = em.dout("dbg_proj", (128, NEXP), BF16)
        dbg_w = em.dout("dbg_w", (128, NEXP), BF16)

    xt = [em.sb(f"xt{i}", (128, D), F32) for i in range(2)]
    scr = em.sb("scr", (128, D), F32)
    stats = em.sb("stats", (128, 8), F32)
    em.push()
    xn_fm = [em.sb(f"xnfm{c}", (128, ntok), BF16) for c in range(8)]
    xn_tm = em.sb("xn_tm", (128, D), BF16)
    psT = em.ps("psT", (128, 512), BF16)
    for i in range(nt):
        xb = xt[i % 2]
        em.dma(xb, xb[:], None, x[i * 128:(i + 1) * 128, :])
        emit_ln(em, xb, xn_tm, scr, stats)
        if dbg and i == 0:
            em.dma(dbg_xn, dbg_xn[:], xn_tm, xn_tm[:])
        for c in range(8):
            emit_transpose128(em, xn_tm, xn_tm[:, c * 128:(c + 1) * 128],
                              psT, psT[:, 0:128],
                              xn_fm[c], xn_fm[c][:, i * 128:(i + 1) * 128], id_t)

    sxm_all = em.sb("sxm_all", (128, nt * 256), BF16)
    sym_all = em.sb("sym_all", (128, nt * 256), BF16)
    thr_all = em.sb("thr_all", (128, nt * 4), F32)
    ngm_all = em.sb("ngm_all", (128, nt * 4), F32)
    rcp_all = em.sb("rcp_all", (128, nt * 4), F32)

    em.push()
    wq_bf = em.sb("wq_bf", (128, 8 * 512), BF16)
    for c in range(8):
        em.dma(wq_bf, wq_bf[:, c * 512:(c + 1) * 512], None,
               wqT[c * 128:(c + 1) * 128, :], engine=_weng(wqT))
    keys_bf = em.sb("keys_bf", (128, 8 * 64), BF16)
    for p in range(2):
        for h in range(HEADS_P):
            em.dma(keys_bf, keys_bf[0:64, (p * 4 + h) * 64:(p * 4 + h + 1) * 64], None,
                   keysP[p, h], engine="gpsimd")
            em.dma(keys_bf, keys_bf[64:128, (p * 4 + h) * 64:(p * 4 + h + 1) * 64], None,
                   keysP[p, h], engine="gpsimd")
    q_fm = [em.sb(f"qfm{c}", (128, ntok), BF16) for c in range(4)]
    psQ = em.ps("psQ", (128, 512), F32)
    tqs = min(512, ntok)
    for qc in range(4):
        for th in range(ntok // tqs):
            for c in range(8):
                em.op("tensor", lambda t, qc=qc, th=th, c=c: t.matmul(
                    psQ[:, 0:tqs], wq_bf[:, c * 512 + qc * 128: c * 512 + (qc + 1) * 128],
                    xn_fm[c][:, th * tqs:(th + 1) * tqs],
                    start=(c == 0), stop=(c == 7)),
                    reads=[wq_bf, xn_fm[c]], writes=[psQ])
            em.op("scalar", lambda s, qc=qc, th=th: s.activation(
                out=q_fm[qc][:, th * tqs:(th + 1) * tqs], in_=psQ[:, 0:tqs], func=AF.Identity),
                reads=[psQ], writes=[q_fm[qc]])

    simb = em.sb("simb", (128, 512), F32)
    mrb = em.sb("mrb", (128, 512), F32)
    eqb = em.sb("eqb", (128, 512), F32)
    d1b = em.sb("d1b", (128, 512), F32)
    v8x = em.sb("v8x", (128, 32), BF16)
    v8y = em.sb("v8y", (128, 32), BF16)
    v8f = em.sb("v8f", (128, 8), F32)
    p64 = em.sb("p64", (128, 256), BF16)
    v8p = em.sb("v8p", (128, 32), BF16)
    pexp = em.sb("pexp", (128, 32), F32)
    psS = em.ps("psS", (128, 64), F32)
    for i in range(nt):
        for p in range(2):
            for h in range(HEADS_P):
                qc = p * 2 + h // 2
                base = (h % 2) * 64
                em.op("tensor", lambda t, p=p, h=h, qc=qc, base=base, i=i: t.matmul(
                    psS[:], q_fm[qc][base:base + 64, i * 128:(i + 1) * 128],
                    keys_bf[base:base + 64, (p * 4 + h) * 64:(p * 4 + h + 1) * 64],
                    start=True, stop=True),
                    reads=[q_fm[qc], keys_bf], writes=[psS])
                em.op("scalar", lambda s, p=p, h=h: s.activation(
                    out=simb[:, (p * 4 + h) * 64:(p * 4 + h + 1) * 64], in_=psS[:],
                    func=AF.Identity), reads=[psS], writes=[simb])
        for p in range(2):
            for h in range(HEADS_P):
                seg = simb[:, (p * 4 + h) * 64:(p * 4 + h + 1) * 64]
                mseg = mrb[:, (p * 4 + h) * 64:(p * 4 + h + 1) * 64]
                em.op("vector", lambda v, seg=seg: v.max(out=v8f[:], in_=seg),
                      reads=[simb], writes=[v8f])
                em.op("vector", lambda v, seg=seg, mseg=mseg: v.match_replace(
                    out=mseg, in_to_replace=v8f[:], in_values=seg, imm_value=-BIG),
                      reads=[simb, v8f], writes=[mrb])
        em.op("vector", lambda v: v.tensor_tensor(out=eqb[:], in0=simb[:], in1=mrb[:],
                                                  op=ALU.is_equal),
              reads=[simb, mrb], writes=[eqb])
        em.op("vector", lambda v: v.tensor_scalar_add(d1b[:], simb[:], BIG),
              reads=[simb], writes=[d1b])
        em.op("vector", lambda v: v.tensor_mul(d1b[:], d1b[:], eqb[:]),
              reads=[d1b, eqb], writes=[d1b])
        em.op("vector", lambda v, i=i: v.tensor_sub(sxm_all[:, i * 256:(i + 1) * 256],
                                                    simb[:, 0:256], d1b[:, 0:256]),
              reads=[simb, d1b], writes=[sxm_all])
        em.op("vector", lambda v, i=i: v.tensor_sub(sym_all[:, i * 256:(i + 1) * 256],
                                                    simb[:, 256:512], d1b[:, 256:512]),
              reads=[simb, d1b], writes=[sym_all])
        for h in range(HEADS_P):
            em.op("vector", lambda v, i=i, h=h: v.max(
                out=v8x[:, h * 8:(h + 1) * 8],
                in_=sxm_all[:, i * 256 + h * 64: i * 256 + (h + 1) * 64]),
                reads=[sxm_all], writes=[v8x])
            em.op("vector", lambda v, i=i, h=h: v.max(
                out=v8y[:, h * 8:(h + 1) * 8],
                in_=sym_all[:, i * 256 + h * 64: i * 256 + (h + 1) * 64]),
                reads=[sym_all], writes=[v8y])
        em.op("vector", lambda v: v.tensor_tensor(
            out=p64[:].rearrange("p (h i j) -> p h i j", h=4, i=8),
            in0=v8x[:].rearrange("p (h i) -> p h i", h=4).unsqueeze(3).to_broadcast([128, 4, 8, 8]),
            in1=v8y[:].rearrange("p (h j) -> p h j", h=4).unsqueeze(2).to_broadcast([128, 4, 8, 8]),
            op=ALU.add), reads=[v8x, v8y], writes=[p64])
        for h in range(HEADS_P):
            em.op("vector", lambda v, h=h: v.max(out=v8p[:, h * 8:(h + 1) * 8],
                                                 in_=p64[:, h * 64:(h + 1) * 64]),
                  reads=[p64], writes=[v8p])
        for h in range(HEADS_P):
            em.op("vector", lambda v, i=i, h=h: v.tensor_copy(
                out=thr_all[:, i * 4 + h: i * 4 + h + 1], in_=v8p[:, h * 8 + 7: h * 8 + 8]),
                reads=[v8p], writes=[thr_all])
            em.op("vector", lambda v, i=i, h=h: v.tensor_scalar_mul(
                ngm_all[:, i * 4 + h: i * 4 + h + 1], v8p[:, h * 8: h * 8 + 1], -1.0),
                reads=[v8p], writes=[ngm_all])
            em.op("scalar", lambda s, i=i, h=h: s.activation(
                out=pexp[:, h * 8:(h + 1) * 8], in_=v8p[:, h * 8:(h + 1) * 8], func=AF.Exp,
                bias=ngm_all[:, i * 4 + h: i * 4 + h + 1]),
                reads=[v8p, ngm_all], writes=[pexp])
        em.op("vector", lambda v, i=i: v.reduce_sum(
            out=rcp_all[:, i * 4:(i + 1) * 4],
            in_=pexp[:].rearrange("p (h k) -> p h k", h=4), axis=AX.X),
            reads=[pexp], writes=[rcp_all])
        em.op("vector", lambda v, i=i: v.reciprocal(
            out=rcp_all[:, i * 4:(i + 1) * 4], in_=rcp_all[:, i * 4:(i + 1) * 4]),
            reads=[rcp_all], writes=[rcp_all])
        if dbg and i == 0:
            em.dma(dbg_sim, dbg_sim[:], simb, simb[:])
            em.dma(dbg_sxm, dbg_sxm[:], sxm_all, sxm_all[:, 0:256])
            em.dma(dbg_sym, dbg_sym[:], sym_all, sym_all[:, 0:256])
            em.dma(dbg_small, dbg_small[:, 0:4], thr_all, thr_all[:, 0:4])
            em.dma(dbg_small, dbg_small[:, 4:8], ngm_all, ngm_all[:, 0:4])
            em.dma(dbg_small, dbg_small[:, 8:12], rcp_all, rcp_all[:, 0:4])

    em.pop()
    if hook is not None:
        hook()

    em.push()
    deT_bf = [em.sb(f"deT{c}", (128, NEXP), BF16) for c in range(8)]
    for c in range(8):
        for half in range(2):
            em.dma(deT_bf[c], deT_bf[c][:, half * 2048:(half + 1) * 2048], None,
                   deT[c * 128:(c + 1) * 128, half * 2048:(half + 1) * 2048], engine=_weng(deT))
    proj_1 = em.sb("proj_b", (128, NEXP), BF16)
    proj_b = [proj_1, proj_1]
    zb = [em.sb(f"zb{j}", (128, NEXP), BF16) for j in range(2)]
    eb = [em.sb(f"eb{j}", (128, NEXP), BF16) for j in range(2)]
    mb_ = [em.sb(f"mb{j}", (128, NEXP), BF16) for j in range(2)]
    wacc = [em.sb(f"wacc{j}", (128, NEXP), BF16) for j in range(2)]
    wts_1 = em.sb("wts", (128, 32 * 128), BF16)
    wts = [wts_1, wts_1]
    psP = em.ps("psP", (128, 512), F32)
    psP2 = em.ps("psP2", (128, 512), F32)
    psW = em.ps("psW", (128, 512), BF16)
    psW2 = em.ps("psW2", (128, 512), BF16)
    for i in range(nt):
        ti = i % 2
        prj, wa, wt_ = proj_b[ti], wacc[ti], wts[ti]
        for ec in range(8):
            pp = psP if ec % 2 == 0 else psP2
            for c in range(8):
                em.op("tensor", lambda t, i=i, ec=ec, c=c, pp=pp: t.matmul(
                    pp[:], xn_fm[c][:, i * 128:(i + 1) * 128],
                    deT_bf[c][:, ec * 512:(ec + 1) * 512],
                    start=(c == 0), stop=(c == 7)),
                    reads=[xn_fm[c], deT_bf[c]], writes=[pp])
            em.op("scalar", lambda s, ec=ec, pp=pp, prj=prj: s.activation(
                out=prj[:, ec * 512:(ec + 1) * 512], in_=pp[:], func=AF.Identity),
                reads=[pp], writes=[prj])
        for hh in range(2):
            for j2 in range(2):
                h = hh * 2 + j2
                em.op("vector", lambda v, i=i, h=h, zz=zb[j2]: v.tensor_tensor(
                    out=zz[:].rearrange("p (i j) -> p i j", i=64),
                    in0=sxm_all[:, i * 256 + h * 64:i * 256 + (h + 1) * 64]
                        .unsqueeze(2).to_broadcast([128, 64, 64]),
                    in1=sym_all[:, i * 256 + h * 64:i * 256 + (h + 1) * 64]
                        .unsqueeze(1).to_broadcast([128, 64, 64]),
                    op=ALU.add), reads=[sxm_all, sym_all], writes=[zb[j2]])
            for j2 in range(2):
                h = hh * 2 + j2
                em.op("scalar", lambda s, i=i, h=h, zz=zb[j2], ee=eb[j2]: s.activation(
                    out=ee[:], in_=zz[:], func=AF.Exp,
                    bias=ngm_all[:, i * 4 + h:i * 4 + h + 1]),
                    reads=[zb[j2], ngm_all], writes=[eb[j2]])
            for j2 in range(2):
                h = hh * 2 + j2
                em.op("vector", lambda v, i=i, h=h, zz=zb[j2], ee=eb[j2], mm=mb_[j2]:
                      v.scalar_tensor_tensor(
                          out=mm[:], in0=zz[:], scalar=thr_all[:, i * 4 + h:i * 4 + h + 1],
                          in1=ee[:], op0=ALU.is_ge, op1=ALU.mult),
                      reads=[zb[j2], thr_all, eb[j2]], writes=[mb_[j2]])
                em.op("vector", lambda v, mm=mb_[j2], prj=prj: v.tensor_mul(mm[:], mm[:], prj[:]),
                      reads=[mb_[j2], prj], writes=[mb_[j2]])
            for j2 in range(2):
                h = hh * 2 + j2
                dst = wa if h == 0 else eb[j2]
                em.op("scalar", lambda s, i=i, h=h, mm=mb_[j2], ss=dst: s.activation(
                    out=ss[:], in_=mm[:], func=AF.Silu,
                    scale=rcp_all[:, i * 4 + h:i * 4 + h + 1]),
                    reads=[mb_[j2], rcp_all], writes=[dst])
            for j2 in range(2):
                h = hh * 2 + j2
                if h > 0:
                    em.op("vector", lambda v, ss=eb[j2], wa=wa: v.tensor_add(wa[:], wa[:], ss[:]),
                          reads=[wa, eb[j2]], writes=[wa])
        if dbg and i == 0:
            em.dma(dbg_proj, dbg_proj[:], prj, prj[:])
            em.dma(dbg_w, dbg_w[:], wa, wa[:])
        pw = psW if ti == 0 else psW2
        for c in range(32):
            emit_transpose128(em, wa, wa[:, c * 128:(c + 1) * 128], pw, pw[:, 0:128],
                              wt_, wt_[:, c * 128:(c + 1) * 128], id_t)
        em.dma(wt_dram, wt_dram[i].rearrange("c p t -> p c t"), wt_,
               wt_[:].rearrange("p (c t) -> p c t", c=32), engine="sync")

    em.pop()

    em.push()
    wup_bf = [em.sb(f"wup{c}", (128, IFF), BF16) for c in range(8)]
    for c in range(8):
        for half in range(2):
            em.dma(wup_bf[c], wup_bf[c][:, half * 2048:(half + 1) * 2048], None,
                   wupT[c * 128:(c + 1) * 128, half * 2048:(half + 1) * 2048], engine=_weng(wupT))
    ub = [em.sb(f"ub{j}", (128, ntok), BF16) for j in range(2)]
    psU = em.ps("psU", (128, 512), F32)
    psU2 = em.ps("psU2", (128, 512), F32)
    tqs2 = min(512, ntok)
    for ic in range(32):
        u_t = ub[ic % 2]
        for tq in range(ntok // tqs2):
            pp = psU if tq % 2 == 0 else psU2
            for c in range(8):
                em.op("tensor", lambda t, ic=ic, tq=tq, c=c, pp=pp: t.matmul(
                    pp[:, 0:tqs2], wup_bf[c][:, ic * 128:(ic + 1) * 128],
                    xn_fm[c][:, tq * tqs2:(tq + 1) * tqs2],
                    start=(c == 0), stop=(c == 7)),
                    reads=[wup_bf[c], xn_fm[c]], writes=[pp])
            em.op("scalar", lambda s, tq=tq, pp=pp, u_t=u_t: s.activation(
                out=u_t[:, tq * tqs2:(tq + 1) * tqs2], in_=pp[:, 0:tqs2], func=AF.Silu),
                reads=[pp], writes=[u_t])
        em.dma(u_dram, u_dram[ic], u_t, u_t[:], engine="sync")

    em.pop()
    em.pop()

    em.push()
    wdn_bf = [em.sb(f"wdn{c}", (128, 32 * 128), BF16) for c in range(8)]
    for ic in range(32):
        em.dma(wdn_bf[ic // 4], wdn_bf[ic // 4][:, (ic % 4) * 1024:(ic % 4 + 1) * 1024],
               None, wdnT[ic * 128:(ic + 1) * 128, :], engine=_weng(wdnT))
    ue_bf = [em.sb(f"ue{c}", (128, 32 * 128), BF16) for c in range(8)]
    for ec in range(32):
        em.dma(ue_bf[ec // 4], ue_bf[ec // 4][:, (ec % 4) * 1024:(ec % 4 + 1) * 1024],
               None, ue[ec * 128:(ec + 1) * 128, :], engine=_weng(ue))
    ustrip = [em.sb(f"ustrip{j}", (128, 4096), BF16) for j in range(2)]
    wtstrip = [em.sb(f"wtstrip{j}", (128, 4096), BF16) for j in range(2)]
    yb = [em.sb(f"yb{j}", (128, D), F32) for j in range(2)]
    psO = em.ps("psO", (128, 1024), F32)
    for i in range(nt):
        us = ustrip[i % 2]
        ws_ = wtstrip[i % 2]
        xb = xt[i % 2]
        ybuf = yb[i % 2]
        em.dma(us, us[:].rearrange("p (c t) -> p c t", c=32), u_dram,
               u_dram[:, :, i * 128:(i + 1) * 128].rearrange("c p t -> p c t"), engine="sync")
        em.dma(ws_, ws_[:].rearrange("p (c t) -> p c t", c=32), wt_dram,
               wt_dram[i].rearrange("c p t -> p c t"), engine="sync")
        em.dma(xb, xb[:], None, x[i * 128:(i + 1) * 128, :])
        for dh in range(2):
            for ic in range(32):
                em.op("tensor", lambda t, i=i, dh=dh, ic=ic, us=us: t.matmul(
                    psO[:, dh * 512:(dh + 1) * 512], us[:, ic * 128:(ic + 1) * 128],
                    wdn_bf[ic // 4][:, (ic % 4) * 1024 + dh * 512:(ic % 4) * 1024 + (dh + 1) * 512],
                    start=(ic == 0), stop=False),
                    reads=[us, wdn_bf[ic // 4]], writes=[psO])
            for ec in range(32):
                em.op("tensor", lambda t, i=i, dh=dh, ec=ec, ws_=ws_: t.matmul(
                    psO[:, dh * 512:(dh + 1) * 512], ws_[:, ec * 128:(ec + 1) * 128],
                    ue_bf[ec // 4][:, (ec % 4) * 1024 + dh * 512:(ec % 4) * 1024 + (dh + 1) * 512],
                    start=False, stop=(ec == 31)),
                    reads=[ws_, ue_bf[ec // 4]], writes=[psO])
        em.op("vector", lambda v, ybuf=ybuf, xb=xb: v.scalar_tensor_tensor(
            out=ybuf[:], in0=psO[:], scalar=0.5, in1=xb[:], op0=ALU.mult, op1=ALU.add),
            reads=[psO, xb], writes=[ybuf])
        em.dma(y, y[i * 128:(i + 1) * 128, :], ybuf, ybuf[:])

    em.pop()


def peer_dins(em, ntok=TOK):
    t = {
        "wqT": em.din("wqT", (D, 512), F32),
        "keysP": em.din("keysP", (2, HEADS_P, 64, 64), F32),
        "deT": em.din("deT", (D, NEXP), F32),
        "ue": em.din("ue", (NEXP, D), F32),
        "wupT": em.din("wupT", (D, IFF), F32),
        "wdnT": em.din("wdnT", (IFF, D), F32),
        "wt_dram": em.dtmp("wt_dram", (ntok // 128, 32, 128, 128), BF16),
        "u_dram": em.dtmp("u_dram", (32, 128, ntok), BF16),
    }
    return t


def build_peer(ntok=TOK, dbg=False):
    em = Em("peer")
    em.cconst(0.0)
    t = peer_dins(em, ntok)
    t["x"] = em.din("x", (ntok, D), F32)
    t["ident"] = em.din("ident", (128, 128), BF16)
    t["y"] = em.dout("y", (ntok, D), F32)
    id_t = em.sb("id_t", (128, 128), BF16)
    em.dma(id_t, id_t[:], None, t["ident"][:])
    t["id_t"] = id_t
    emit_peer(em, t, ntok, dbg=dbg)
    em.finish()
    return em


def peer_host_inputs(x_shard, W_up, W_down, W_q, keys, down_embed, up_embed):
    return {
        "x": np.ascontiguousarray(x_shard, dtype=np.float32),
        "wqT": np.ascontiguousarray(W_q.T, dtype=np.float32),
        "keysP": np.ascontiguousarray(keys.transpose(2, 0, 3, 1), dtype=np.float32),
        "deT": np.ascontiguousarray(down_embed.T, dtype=np.float32),
        "ue": np.ascontiguousarray(up_embed, dtype=np.float32),
        "wupT": np.ascontiguousarray(W_up.T, dtype=np.float32),
        "wdnT": np.ascontiguousarray(W_down.T, dtype=np.float32),
        "ident": np.eye(128, dtype=bfd),
    }


def emit_attn1(em, t, ntok=TOK):
    nt = ntok // 128
    x = t["x"]; wq = t["wq"]; wk = t["wk"]; wv = t["wv"]; id_t = t["id_t"]
    xn_o = t["xn"]; q_o = t["q"]; k_o = t["k"]; v_o = t["v"]
    em.push()
    xt = [em.sb(f"xt{i}", (128, D), F32) for i in range(2)]
    scr = em.sb("scr", (128, D), F32)
    stats = em.sb("stats", (128, 8), F32)
    xn_f32 = em.sb("xn_f32", (128, D), F32)
    xn_tm = em.sb("xn_tm", (128, D), BF16)
    xn_fm = [em.sb(f"xnfm{c}", (128, ntok), BF16) for c in range(8)]
    psT = em.ps("psT", (128, 512), BF16)
    for i in range(nt):
        xb = xt[i % 2]
        em.dma(xb, xb[:], None, x[i * 128:(i + 1) * 128, :])
        emit_ln(em, xb, xn_f32, scr, stats)
        em.dma(xn_o, xn_o[i * 128:(i + 1) * 128, :], xn_f32, xn_f32[:])
        em.op("vector", lambda v: v.tensor_copy(out=xn_tm[:], in_=xn_f32[:]),
              reads=[xn_f32], writes=[xn_tm])
        for c in range(8):
            emit_transpose128(em, xn_tm, xn_tm[:, c * 128:(c + 1) * 128],
                              psT, psT[:, 0:128],
                              xn_fm[c], xn_fm[c][:, i * 128:(i + 1) * 128], id_t)

    wq_bf = [em.sb(f"wqb{c}", (128, D), BF16) for c in range(8)]
    wk_bf = [em.sb(f"wkb{c}", (128, 256), BF16) for c in range(8)]
    wv_bf = [em.sb(f"wvb{c}", (128, 256), BF16) for c in range(8)]
    for c in range(8):
        em.dma(wq_bf[c], wq_bf[c][:], None, wq[c * 128:(c + 1) * 128, :], engine=_weng(wq))
        em.dma(wk_bf[c], wk_bf[c][:], None, wk[c * 128:(c + 1) * 128, :], engine=_weng(wk))
        em.dma(wv_bf[c], wv_bf[c][:], None, wv[c * 128:(c + 1) * 128, :], engine=_weng(wv))
    ob_a = [em.sb(f"ob{j}", (128, 512), BF16) for j in range(2)]
    psM = em.ps("psM", (128, 512), F32)
    psM2 = em.ps("psM2", (128, 512), F32)
    tqs = min(512, ntok)
    for qc in range(8):
        for th in range(ntok // tqs):
            pp = psM if th % 2 == 0 else psM2
            ob = ob_a[th % 2]
            for c in range(8):
                em.op("tensor", lambda t, qc=qc, th=th, c=c, pp=pp: t.matmul(
                    pp[:, 0:tqs], wq_bf[c][:, qc * 128:(qc + 1) * 128],
                    xn_fm[c][:, th * tqs:(th + 1) * tqs],
                    start=(c == 0), stop=(c == 7)),
                    reads=[wq_bf[c], xn_fm[c]], writes=[pp])
            em.op("scalar", lambda s, pp=pp, ob=ob: s.activation(out=ob[:, 0:tqs], in_=pp[:, 0:tqs],
                                                                 func=AF.Identity),
                  reads=[pp], writes=[ob])
            em.dma(q_o, q_o[qc * 128:(qc + 1) * 128, th * tqs:(th + 1) * tqs], ob, ob[:, 0:tqs])
    for kc in range(2):
        for th in range(ntok // tqs):
            pp = psM if th % 2 == 0 else psM2
            ob = ob_a[th % 2]
            for c in range(8):
                em.op("tensor", lambda t, kc=kc, th=th, c=c, pp=pp: t.matmul(
                    pp[:, 0:tqs], wk_bf[c][:, kc * 128:(kc + 1) * 128],
                    xn_fm[c][:, th * tqs:(th + 1) * tqs],
                    start=(c == 0), stop=(c == 7)),
                    reads=[wk_bf[c], xn_fm[c]], writes=[pp])
            em.op("scalar", lambda s, pp=pp, ob=ob: s.activation(out=ob[:, 0:tqs], in_=pp[:, 0:tqs],
                                                                 func=AF.Identity, scale=0.125),
                  reads=[pp], writes=[ob])
            em.dma(k_o, k_o[kc * 128:(kc + 1) * 128, th * tqs:(th + 1) * tqs], ob, ob[:, 0:tqs])
    for i in range(nt):
        pp = psM if i % 2 == 0 else psM2
        ob = ob_a[i % 2]
        for c in range(8):
            em.op("tensor", lambda t, i=i, c=c, pp=pp: t.matmul(
                pp[:, 0:256], xn_fm[c][:, i * 128:(i + 1) * 128], wv_bf[c][:],
                start=(c == 0), stop=(c == 7)),
                reads=[xn_fm[c], wv_bf[c]], writes=[pp])
        em.op("scalar", lambda s, pp=pp, ob=ob: s.activation(out=ob[:, 0:256], in_=pp[:, 0:256],
                                                             func=AF.Identity),
              reads=[pp], writes=[ob])
        em.dma(v_o, v_o[i * 128:(i + 1) * 128, :], ob, ob[:, 0:256])
    em.pop()


def build_attn1(ntok=TOK):
    em = Em("attn1")
    em.cconst(0.0)
    t = {
        "x": em.din("x", (ntok, D), F32),
        "wq": em.din("wq", (D, D), F32),
        "wk": em.din("wk", (D, 256), F32),
        "wv": em.din("wv", (D, 256), F32),
        "ident": em.din("ident", (128, 128), BF16),
        "xn": em.dout("xn", (ntok, D), F32),
        "q": em.dout("q", (D, ntok), BF16),
        "k": em.dout("k", (256, ntok), BF16),
        "v": em.dout("v", (ntok, 256), BF16),
    }
    id_t = em.sb("id_t", (128, 128), BF16)
    em.dma(id_t, id_t[:], None, t["ident"][:])
    t["id_t"] = id_t
    emit_attn1(em, t, ntok)
    em.finish()
    return em


def emit_attn2(em, t, ntok=TOK, sk=2 * TOK):
    nt = ntok // 128
    ntk = sk // 128
    nsc = max(1, ntok // 512)
    scs = min(512, ntok)
    x = t["x"]; xn_i = t["xn"]; q_i = t["q"]; k_i = t["k"]; v_i = t["v"]
    wpost = t["wpost"]; id_t = t["id_t"]; y = t["y"]
    bscr = t["bscr"]; rd_dram = t["rd_dram"]
    em.push()
    qf = [em.sb(f"qf{c}", (128, ntok), BF16) for c in range(8)]
    for c in range(8):
        em.dma(qf[c], qf[c][:], None, q_i[c * 128:(c + 1) * 128, :])
    kdup = [em.sb(f"kdup{g}", (128, sk), BF16) for g in range(AG)]
    for g in range(AG):
        em.dma(kdup[g], kdup[g][0:64, :], None, k_i[g * 64:(g + 1) * 64, :])
        em.dma(kdup[g], kdup[g][64:128, :], None, k_i[g * 64:(g + 1) * 64, :])
    vaug = em.sb("vaug", (128, ntk * AG * 65), BF16)
    for g in range(AG):
        em.dma(vaug, vaug[:].rearrange("p (tkg e) -> p tkg e", e=65)[:, g::AG, 0:64],
               None, v_i[:].rearrange("(tk p) e -> p tk e", p=128)[:, :, g * 64:(g + 1) * 64])
    em.op("vector", lambda v: v.memset(
        vaug[:].rearrange("p (tk g e) -> p tk g e", tk=ntk, g=AG)[:, :, :, 64:65], 1.0),
        reads=[], writes=[vaug])
    af = [em.sb(f"af{c}", (128, ntok), BF16) for c in range(8)]
    negB = em.sb("negB", (128, 1), F32)

    em.push()
    sq = em.sb("sq", (128, 512), F32)
    ones2 = em.sb("ones2", (128, 2), F32)
    em.op("vector", lambda v: v.memset(ones2[0:64, :], 0.0), reads=[], writes=[ones2])
    em.op("vector", lambda v: v.memset(ones2[0:64, 0:1], 1.0), reads=[], writes=[ones2])
    em.op("vector", lambda v: v.memset(ones2[64:128, 0:1], 0.0), reads=[], writes=[ones2])
    em.op("vector", lambda v: v.memset(ones2[64:128, 1:2], 1.0), reads=[], writes=[ones2])
    nmx = em.sb("nmx", (2, 64), F32)
    psN = em.ps("psN", (128, 512), F32)
    col = 0
    col_q = 0
    for (src_, nchunks) in ((qf, ntok // 512 if ntok >= 512 else 1), (kdup, sk // 512 if sk >= 512 else 1)):
        nbuf = len(src_) if src_ is qf else AG
        for c in range(nbuf):
            for th in range(nchunks):
                w_ = min(512, (ntok if src_ is qf else sk))
                em.op("scalar", lambda s, c=c, th=th, src=src_, w_=w_: s.activation(
                    out=sq[:, 0:w_], in_=src[c][:, th * w_:(th + 1) * w_], func=AF.Square),
                    reads=[src_[c]], writes=[sq])
                em.op("tensor", lambda t, w_=w_: t.matmul(
                    psN[:2, 0:w_], ones2[:], sq[:, 0:w_], start=True, stop=True),
                    reads=[ones2, sq], writes=[psN])
                em.op("vector", lambda v, col=col, w_=w_: v.reduce_max(
                    out=nmx[:, col:col + 1], in_=psN[:2, 0:w_], axis=AX.X),
                    reads=[psN], writes=[nmx])
                col += 1
        if src_ is qf:
            col_q = col
    ncq = col_q
    nred = em.sb("nred", (2, 2), F32)
    em.op("vector", lambda v: v.reduce_max(out=nred[:, 0:1], in_=nmx[:, 0:ncq], axis=AX.X),
          reads=[nmx], writes=[nred])
    em.op("vector", lambda v: v.reduce_max(out=nred[:, 1:2], in_=nmx[:, ncq:col], axis=AX.X),
          reads=[nmx], writes=[nred])
    em.dma(bscr, bscr[0:1, 0:4], nred, nred[:, 0:2])
    brow = em.sb("brow", (1, 8), F32)
    em.dma(brow, brow[0:1, 0:4], bscr, bscr[0:1, 0:4])
    em.op("vector", lambda v: v.reduce_max(out=brow[:, 4:5],
                                           in_=brow[:, 0:4].rearrange("p (a b) -> p b a", b=2)[:, 0:1, :],
                                           axis=AX.X),
          reads=[brow], writes=[brow])
    em.op("vector", lambda v: v.reduce_max(out=brow[:, 5:6],
                                           in_=brow[:, 0:4].rearrange("p (a b) -> p b a", b=2)[:, 1:2, :],
                                           axis=AX.X),
          reads=[brow], writes=[brow])
    em.op("vector", lambda v: v.tensor_mul(brow[:, 6:7], brow[:, 4:5], brow[:, 5:6]),
          reads=[brow], writes=[brow])
    em.op("scalar", lambda s: s.activation(out=brow[:, 6:7], in_=brow[:, 6:7], func=AF.Sqrt),
          reads=[brow], writes=[brow])
    em.op("vector", lambda v: v.tensor_scalar(out=brow[:, 7:8], in0=brow[:, 6:7],
                                              scalar1=-1.05, scalar2=-0.05,
                                              op0=ALU.mult, op1=ALU.add),
          reads=[brow], writes=[brow])
    em.dma(bscr, bscr[0:1, 4:5], brow, brow[:, 7:8])
    em.dma(negB, negB[:], bscr, bscr[0:1, 4:5].to_broadcast([128, 1]))
    em.pop()

    em.push()
    scs2 = min(1024, ntok)
    nsc2 = ntok // scs2
    hw = scs2 // 2
    den = em.sb("den", (1, 1024), F32)
    eb = [em.sb(f"ebb{j}", (128, scs2), BF16) for j in range(2)]
    psE = [em.ps(f"psE{j}", (128, scs2), F32) for j in range(2)]
    psA = [[em.ps(f"psA{j}h{hf}", (65, hw), F32) for hf in range(2)] for j in range(2)]
    for h in range(AH):
        g = h // 4
        base = (h % 2) * 64
        qb = qf[h // 2]
        for sc in range(nsc2):
            pa = psA[sc % 2]
            q0 = sc * scs2

            def qk(tk, half, pe, h=h, g=g, base=base, qb=qb, q0=q0):
                em.op("tensor", lambda t, tk=tk, half=half, pe=pe: t.matmul(
                    pe[:, half * hw:(half + 1) * hw],
                    kdup[g][base:base + 64, tk * 128:(tk + 1) * 128],
                    qb[base:base + 64, q0 + half * hw:q0 + (half + 1) * hw],
                    start=True, stop=True),
                    reads=[kdup[g], qb], writes=[pe])

            qk(0, 0, psE[0]); qk(0, 1, psE[0])
            for tk in range(ntk):
                pe = psE[tk % 2]
                ebb = eb[tk % 2]
                em.op("scalar", lambda s, pe=pe, ebb=ebb: s.activation(
                    out=ebb[:], in_=pe[:], func=AF.Exp, bias=negB[:]),
                    reads=[pe, negB], writes=[ebb])
                if tk + 1 < ntk:
                    qk(tk + 1, 0, psE[(tk + 1) % 2]); qk(tk + 1, 1, psE[(tk + 1) % 2])
                for half in range(2):
                    em.op("tensor", lambda t, g=g, tk=tk, half=half, ebb=ebb, pa=pa: t.matmul(
                        pa[half][:],
                        vaug[:, (tk * AG + g) * 65:(tk * AG + g + 1) * 65],
                        ebb[:, half * hw:(half + 1) * hw],
                        start=(tk == 0), stop=(tk == ntk - 1)),
                        reads=[vaug, ebb], writes=[pa[half]])
            for half in range(2):
                em.op("vector", lambda v, h=h, base=base, q0=q0, half=half, pa=pa: v.tensor_copy(
                    out=af[h // 2][base:base + 64, q0 + half * hw:q0 + (half + 1) * hw],
                    in_=pa[half][0:64, :]),
                    reads=[pa[half]], writes=[af[h // 2]])
                em.op("vector", lambda v, half=half, pa=pa: v.tensor_copy(
                    out=den[0:1, half * hw:(half + 1) * hw], in_=pa[half][64:65, :]),
                    reads=[pa[half]], writes=[den])
            em.dma(rd_dram, rd_dram[h, q0:q0 + scs2].unsqueeze(0), den, den[0:1, 0:scs2])

    em.pop()

    em.push()
    rb = em.sb("rb", (128, ntok), F32)
    for c in range(8):
        em.dma(rb, rb[0:64, :], rd_dram, rd_dram[2 * c].unsqueeze(0).to_broadcast([64, ntok]))
        em.dma(rb, rb[64:128, :], rd_dram, rd_dram[2 * c + 1].unsqueeze(0).to_broadcast([64, ntok]))
        em.op("vector", lambda v: v.reciprocal(out=rb[:], in_=rb[:]), reads=[rb], writes=[rb])
        em.op("vector", lambda v, c=c: v.tensor_mul(af[c][:], af[c][:], rb[:]),
              reads=[af[c], rb], writes=[af[c]])
    wp_bf = [em.sb(f"wpb{c}", (128, D), BF16) for c in range(8)]
    for c in range(8):
        em.dma(wp_bf[c], wp_bf[c][:], None, wpost[c * 128:(c + 1) * 128, :], engine=_weng(wpost))
    xb = [em.sb(f"xb{j}", (128, D), F32) for j in range(2)]
    xnb = [em.sb(f"xnb{j}", (128, D), F32) for j in range(2)]
    yb = [em.sb(f"yb{j}", (128, D), F32) for j in range(2)]
    psO = em.ps("psO", (128, 1024), F32)
    for i in range(nt):
        xbb = xb[i % 2]
        xnn = xnb[i % 2]
        ybb = yb[i % 2]
        em.dma(xbb, xbb[:], None, x[i * 128:(i + 1) * 128, :])
        em.dma(xnn, xnn[:], None, xn_i[i * 128:(i + 1) * 128, :])
        for dh in range(2):
            for c in range(8):
                em.op("tensor", lambda t, i=i, dh=dh, c=c: t.matmul(
                    psO[:, dh * 512:(dh + 1) * 512], af[c][:, i * 128:(i + 1) * 128],
                    wp_bf[c][:, dh * 512:(dh + 1) * 512],
                    start=(c == 0), stop=(c == 7)),
                    reads=[af[c], wp_bf[c]], writes=[psO])
        em.op("vector", lambda v, xnn=xnn: v.tensor_add(xnn[:], xnn[:], psO[:]),
              reads=[xnn, psO], writes=[xnn])
        em.op("vector", lambda v, ybb=ybb, xbb=xbb, xnn=xnn: v.tensor_add(ybb[:], xnn[:], xbb[:]),
              reads=[xnn, xbb], writes=[ybb])
        em.dma(y, y[i * 128:(i + 1) * 128, :], ybb, ybb[:])
    em.pop()
    em.pop()


def build_attn2(ntok=TOK, sk=2 * TOK):
    em = Em("attn2")
    em.cconst(0.0)
    t = {
        "x": em.din("x", (ntok, D), F32),
        "xn": em.din("xn", (ntok, D), F32),
        "q": em.din("q", (D, ntok), BF16),
        "k": em.din("k", (256, sk), BF16),
        "v": em.din("v", (sk, 256), BF16),
        "wpost": em.din("wpost", (D, D), F32),
        "ident": em.din("ident", (128, 128), BF16),
        "y": em.dout("y", (ntok, D), F32),
        "bscr": em.dtmp("bscr", (1, 16), F32),
        "rd_dram": em.dtmp("rd_dram", (AH, ntok), F32),
    }
    id_t = em.sb("id_t", (128, 128), BF16)
    em.dma(id_t, id_t[:], None, t["ident"][:])
    t["id_t"] = id_t
    emit_attn2(em, t, ntok, sk)
    em.finish()
    return em


def attn_host_inputs1(x_shard, Wq, Wk, Wv):
    return {
        "x": np.ascontiguousarray(x_shard, dtype=np.float32),
        "wq": np.ascontiguousarray(Wq, dtype=np.float32),
        "wk": np.ascontiguousarray(Wk, dtype=np.float32),
        "wv": np.ascontiguousarray(Wv, dtype=np.float32),
        "ident": np.eye(128, dtype=bfd),
    }


def emit_m1(em, t, ntok=TOK):
    nt = ntok // 128
    ntc = max(1, ntok // 512)
    tqs = min(512, ntok)
    x = t["x"]; xhalo = t["xhalo"]; hmask = t["hmask"]; ln_g = t["ln_g"]; ln_b = t["ln_b"]
    w_in = t["w_in"]; conv_w = t["conv_w"]; conv_b = t["conv_b"]; w_x = t["w_x"]
    w_dt = t["w_dt"]; b_dt = t["b_dt"]; id_t = t["id_t"]
    xn1_o = t["xn1"]; xc_o = t["xc"]; z_o = t["z"]; dt_o = t["dt"]; bc_o = t["bc"]
    em.push()
    g_bc = em.sb("g_bc", (128, D), F32)
    b_bc = em.sb("b_bc", (128, D), F32)
    em.dma(g_bc, g_bc[:], None, ln_g[:].to_broadcast([128, D]))
    em.dma(b_bc, b_bc[:], None, ln_b[:].to_broadcast([128, D]))

    xh_fm = [em.sb(f"xhfm{k}", (128, 3 + ntok), BF16) for k in range(16)]
    em.push()
    xn2_fm = [em.sb(f"xn2fm{c}", (128, ntok), BF16) for c in range(8)]
    xn2h_fm = em.sb("xn2h_fm", (128, 24), BF16)
    em.push()
    xt = [em.sb(f"xt{i}", (128, D), F32) for i in range(2)]
    scr = em.sb("scr", (128, D), F32)
    stats = em.sb("stats", (128, 8), F32)
    xnt = em.sb("xnt", (128, D), F32)
    xn2t = em.sb("xn2t", (128, D), F32)
    xn2b = em.sb("xn2b", (128, D), BF16)
    psT = em.ps("psT", (128, 512), BF16)
    for i in range(nt):
        xb = xt[i % 2]
        em.dma(xb, xb[:], None, x[i * 128:(i + 1) * 128, :])
        emit_ln(em, xb, xnt, scr, stats)
        em.op("vector", lambda v: v.tensor_mul(xn2t[:], xnt[:], g_bc[:]),
              reads=[xnt, g_bc], writes=[xn2t])
        em.op("vector", lambda v: v.tensor_add(xn2t[:], xn2t[:], b_bc[:]),
              reads=[xn2t, b_bc], writes=[xn2t])
        em.dma(xn1_o, xn1_o[i * 128:(i + 1) * 128, :], xnt, xnt[:])
        em.op("vector", lambda v: v.tensor_copy(out=xn2b[:], in_=xn2t[:]),
              reads=[xn2t], writes=[xn2b])
        for c in range(8):
            emit_transpose128(em, xn2b, xn2b[:, c * 128:(c + 1) * 128],
                              psT, psT[:, 0:128],
                              xn2_fm[c], xn2_fm[c][:, i * 128:(i + 1) * 128], id_t)
    hx = em.sb("hx", (3, D), F32)
    hm = em.sb("hm", (3, 1), F32)
    hstats = em.sb("hstats", (3, 8), F32)
    hscr = em.sb("hscr", (3, D), F32)
    hxn = em.sb("hxn", (3, D), F32)
    hxb = em.sb("hxb", (3, D), BF16)
    em.dma(hx, hx[:], None, xhalo[:])
    em.dma(hm, hm[:], None, hmask[:])
    emit_ln(em, hx, hxn, hscr, hstats, width=D)
    em.op("vector", lambda v: v.tensor_mul(hxn[:], hxn[:], g_bc[0:3, :]),
          reads=[hxn, g_bc], writes=[hxn])
    em.op("vector", lambda v: v.tensor_add(hxn[:], hxn[:], b_bc[0:3, :]),
          reads=[hxn, b_bc], writes=[hxn])
    em.op("vector", lambda v: v.tensor_scalar(out=hxb[:], in0=hxn[:], scalar1=hm[:],
                                              scalar2=None, op0=ALU.mult),
          reads=[hxn, hm], writes=[hxb])
    psH = em.ps("psH", (128, 128), BF16)
    for c in range(8):
        em.op("tensor", lambda t, c=c: t.transpose(psH[:, 0:3], hxb[:, c * 128:(c + 1) * 128],
                                                   id_t[0:3, 0:3]),
              reads=[hxb, id_t], writes=[psH])
        em.op("vector", lambda v, c=c: v.tensor_copy(out=xn2h_fm[:, c * 3:(c + 1) * 3],
                                                     in_=psH[:, 0:3]),
              reads=[psH], writes=[xn2h_fm])
    em.pop()

    em.push()
    w_in_bf = [em.sb(f"winb{c}", (128, 2 * DI), BF16) for c in range(8)]
    for c in range(8):
        for hf in range(2):
            em.dma(w_in_bf[c], w_in_bf[c][:, hf * DI:(hf + 1) * DI], None,
                   w_in[c * 128:(c + 1) * 128, hf * DI:(hf + 1) * DI], engine=_weng(w_in))
    zb = em.sb("zbm", (128, 512), F32)
    psZ = em.ps("psZ", (128, 512), F32)
    psZ2 = em.ps("psZ2", (128, 512), F32)
    for oc in range(32):
        is_xh = oc < 16
        for th in range(ntc):
            pp = psZ if th % 2 == 0 else psZ2
            for c in range(8):
                em.op("tensor", lambda t, oc=oc, th=th, c=c, pp=pp: t.matmul(
                    pp[:, 0:tqs], w_in_bf[c][:, oc * 128:(oc + 1) * 128],
                    xn2_fm[c][:, th * tqs:(th + 1) * tqs],
                    start=(c == 0), stop=(c == 7)),
                    reads=[w_in_bf[c], xn2_fm[c]], writes=[pp])
            if is_xh:
                em.op("vector", lambda v, oc=oc, th=th, pp=pp: v.tensor_copy(
                    out=xh_fm[oc][:, 3 + th * tqs: 3 + (th + 1) * tqs], in_=pp[:, 0:tqs]),
                    reads=[pp], writes=[xh_fm[oc]])
            else:
                em.op("vector", lambda v, pp=pp: v.tensor_copy(out=zb[:, 0:tqs], in_=pp[:, 0:tqs]),
                      reads=[pp], writes=[zb])
                em.dma(z_o, z_o[(oc - 16) * 128:(oc - 15) * 128, th * tqs:(th + 1) * tqs],
                       zb, zb[:, 0:tqs])
        if is_xh:
            pp = psZ
            for c in range(8):
                em.op("tensor", lambda t, oc=oc, c=c, pp=pp: t.matmul(
                    pp[:, 0:3], w_in_bf[c][:, oc * 128:(oc + 1) * 128],
                    xn2h_fm[:, c * 3:(c + 1) * 3],
                    start=(c == 0), stop=(c == 7)),
                    reads=[w_in_bf[c], xn2h_fm], writes=[pp])
            em.op("vector", lambda v, oc=oc, pp=pp: v.tensor_copy(
                out=xh_fm[oc][:, 0:3], in_=pp[:, 0:3]), reads=[pp], writes=[xh_fm[oc]])
    em.pop()
    em.pop()

    xc_bf = [em.sb(f"xcbf{k}", (128, ntok), BF16) for k in range(16)]
    em.push()
    cw = em.sb("cw", (128, 16 * DCONV), F32)
    cb = em.sb("cb", (128, 16), F32)
    for k in range(16):
        em.dma(cw, cw[:, k * DCONV:(k + 1) * DCONV], None, conv_w[k * 128:(k + 1) * 128, :])
    em.dma(cb, cb[:], None, conv_b[:])
    acc = em.sb("acc", (128, TOK), F32)
    xcf = em.sb("xcf", (128, TOK), F32)
    for k in range(16):
        em.op("vector", lambda v, k=k: v.tensor_scalar(
            out=acc[:, 0:ntok], in0=xh_fm[k][:, 0:ntok], scalar1=cw[:, k * DCONV:k * DCONV + 1],
            scalar2=None, op0=ALU.mult),
            reads=[xh_fm[k], cw], writes=[acc])
        for tap in range(1, DCONV):
            em.op("vector", lambda v, k=k, tap=tap: v.scalar_tensor_tensor(
                out=acc[:, 0:ntok], in0=xh_fm[k][:, tap:tap + ntok],
                scalar=cw[:, k * DCONV + tap:k * DCONV + tap + 1],
                in1=acc[:, 0:ntok], op0=ALU.mult, op1=ALU.add),
                reads=[xh_fm[k], cw, acc], writes=[acc])
        em.op("scalar", lambda s, k=k: s.activation(out=xcf[:, 0:ntok], in_=acc[:, 0:ntok],
                                                    func=AF.Silu, bias=cb[:, k:k + 1]),
              reads=[acc, cb], writes=[xcf])
        em.dma(xc_o, xc_o[k * 128:(k + 1) * 128, :], xcf, xcf[:, 0:ntok])
        em.op("vector", lambda v, k=k: v.tensor_copy(out=xc_bf[k][:], in_=xcf[:, 0:ntok]),
              reads=[xcf], writes=[xc_bf[k]])
    em.pop()

    em.push()
    wx_bf = em.sb("wx_bf", (128, 16 * 96), BF16)
    for k in range(16):
        em.dma(wx_bf, wx_bf[:, k * 96:(k + 1) * 96], None,
               w_x[k * 128:(k + 1) * 128, :], engine="gpsimd")
    dbc = em.sb("dbc", (96, ntok), F32)
    dbc_bf = em.sb("dbc_bf", (96, ntok), BF16)
    psD = em.ps("psD", (128, 512), F32)
    for th in range(ntc):
        for k in range(16):
            em.op("tensor", lambda t, th=th, k=k: t.matmul(
                psD[0:96, 0:tqs], wx_bf[:, k * 96:(k + 1) * 96],
                xc_bf[k][:, th * tqs:(th + 1) * tqs], start=(k == 0), stop=(k == 15)),
                reads=[wx_bf, xc_bf[k]], writes=[psD])
        em.op("vector", lambda v, th=th: v.tensor_copy(out=dbc[:, th * tqs:(th + 1) * tqs],
                                                       in_=psD[0:96, 0:tqs]),
              reads=[psD], writes=[dbc])
    em.dma(bc_o, bc_o[:], dbc, dbc[64:96, :])
    em.op("vector", lambda v: v.tensor_copy(out=dbc_bf[:], in_=dbc[:]),
          reads=[dbc], writes=[dbc_bf])
    wdt_bf = em.sb("wdt_bf", (64, DI), BF16)
    em.dma(wdt_bf, wdt_bf[:], None, w_dt[:], engine="gpsimd")
    bdt_c = em.sb("bdt_c", (128, 16), F32)
    em.dma(bdt_c, bdt_c[:], None, b_dt[:])
    ut = em.sb("ut", (128, 512), F32)
    t1 = em.sb("t1m", (128, 512), F32)
    t2 = em.sb("t2m", (128, 512), F32)
    dtt = em.sb("dtt", (128, 512), F32)
    one_c = em.cconst(1.0)
    for dc in range(16):
        for th in range(ntc):
            em.op("tensor", lambda t, dc=dc, th=th: t.matmul(
                psD[:, 0:tqs], wdt_bf[:, dc * 128:(dc + 1) * 128],
                dbc_bf[0:64, th * tqs:(th + 1) * tqs], start=True, stop=True),
                reads=[wdt_bf, dbc_bf], writes=[psD])
            em.op("scalar", lambda s, dc=dc: s.activation(out=ut[:, 0:tqs], in_=psD[:, 0:tqs],
                                                          func=AF.Identity,
                                                          bias=bdt_c[:, dc:dc + 1]),
                  reads=[psD, bdt_c], writes=[ut])
            em.op("scalar", lambda s: s.activation(out=t1[:, 0:tqs], in_=ut[:, 0:tqs], func=AF.Abs),
                  reads=[ut], writes=[t1])
            em.op("scalar", lambda s: s.activation(out=t1[:, 0:tqs], in_=t1[:, 0:tqs], func=AF.Exp,
                                                   scale=-1.0),
                  reads=[t1], writes=[t1])
            em.op("scalar", lambda s: s.activation(out=t1[:, 0:tqs], in_=t1[:, 0:tqs], func=AF.Ln,
                                                   bias=one_c),
                  reads=[t1], writes=[t1])
            em.op("scalar", lambda s: s.activation(out=t2[:, 0:tqs], in_=ut[:, 0:tqs], func=AF.Relu),
                  reads=[ut], writes=[t2])
            em.op("vector", lambda v: v.tensor_add(dtt[:, 0:tqs], t1[:, 0:tqs], t2[:, 0:tqs]),
                  reads=[t1, t2], writes=[dtt])
            em.dma(dt_o, dt_o[dc * 128:(dc + 1) * 128, th * tqs:(th + 1) * tqs], dtt, dtt[:, 0:tqs])
    em.pop()
    em.pop()


def build_m1(ntok=TOK):
    em = Em("m1")
    em.cconst(0.0)
    t = {
        "x": em.din("x", (ntok, D), F32),
        "xhalo": em.din("xhalo", (3, D), F32),
        "hmask": em.din("hmask", (3, 1), F32),
        "ln_g": em.din("ln_g", (1, D), F32),
        "ln_b": em.din("ln_b", (1, D), F32),
        "w_in": em.din("w_in", (D, 2 * DI), F32),
        "conv_w": em.din("conv_w", (DI, DCONV), F32),
        "conv_b": em.din("conv_b", (128, 16), F32),
        "w_x": em.din("w_x", (DI, 96), F32),
        "w_dt": em.din("w_dt", (DTRANK, DI), F32),
        "b_dt": em.din("b_dt", (128, 16), F32),
        "ident": em.din("ident", (128, 128), BF16),
        "xn1": em.dout("xn1", (ntok, D), F32),
        "xc": em.dout("xc", (DI, ntok), F32),
        "z": em.dout("z", (DI, ntok), F32),
        "dt": em.dout("dt", (DI, ntok), F32),
        "bc": em.dout("bc", (32, ntok), F32),
    }
    id_t = em.sb("id_t", (128, 128), BF16)
    em.dma(id_t, id_t[:], None, t["ident"][:])
    t["id_t"] = id_t
    emit_m1(em, t, ntok)
    em.finish()
    return em


def emit_m2(em, t, nch=1024, slen=2 * TOK):
    ndb = nch // 128
    CB = min(512, slen)
    CB2 = min(1024, slen)
    ncb2 = slen // CB2
    xc_i = t["xc"]; dt_i = t["dt"]; b_i = t["bm"]; c_i = t["cm"]; alog = t["alog"]
    y_o = t["ym"]; bsel = t["bsel"]
    em.push()
    ab = em.sb("ab", (128, ndb * DSTATE), F32)
    for db in range(ndb):
        em.dma(ab, ab[:, db * DSTATE:(db + 1) * DSTATE], None,
               alog[db * 128:(db + 1) * 128, :])
    em.op("scalar", lambda s: s.activation(out=ab[:], in_=ab[:], func=AF.Exp, scale=1.0),
          reads=[ab], writes=[ab])
    em.op("vector", lambda v: v.tensor_scalar_mul(ab[:], ab[:], -1.0), reads=[ab], writes=[ab])
    zcol = em.sb("zcol", (128, 1), F32)
    em.op("vector", lambda v: v.memset(zcol[:], 0.0), reads=[], writes=[zcol])

    bselb = em.sb("bselb", (48, DSTATE * 128), BF16)
    em.dma(bselb, bselb[0:16, :], None, bsel[0:16, :])
    em.dma(bselb, bselb[32:48, :], None, bsel[32:48, :])
    brow = em.sb("brow", (48, slen), BF16)
    em.dma(brow, brow[0:16, :], None, b_i[:], engine="gpsimd")
    em.dma(brow, brow[32:48, :], None, c_i[:], engine="gpsimd")

    dtb = [em.sb(f"dtb{j}", (128, slen), F32) for j in range(2)]
    wxc = em.sb("wxc", (128, slen), F32)
    wb = [em.sb(f"wb{j}", (128, slen), BF16) for j in range(2)]
    bbc = [em.sb(f"bbc{j}", (128, slen), BF16) for j in range(2)]
    cbc = [em.sb(f"cbc{j}", (128, slen), BF16) for j in range(2)]
    da = [em.sb(f"da{j}", (128, slen), BF16) for j in range(2)]
    dbu = em.sb("dbu", (128, slen), BF16)
    hbb = em.sb("hb", (128, slen), BF16)
    ynb = em.sb("yn", (128, slen), BF16)
    yacc = [em.sb(f"yacc{j}", (128, slen), F32) for j in range(2)]
    psB = [em.ps(f"psB{j}", (128, CB2), F32) for j in range(2)]
    psC = [em.ps(f"psC{j}", (128, CB2), F32) for j in range(2)]
    for db in range(ndb):
        jd = db % 2
        em.dma(dtb[jd], dtb[jd][:], None, dt_i[db * 128:(db + 1) * 128, :])
        em.dma(wxc, wxc[:], None, xc_i[db * 128:(db + 1) * 128, :])
        em.op("vector", lambda v, jd=jd: v.tensor_mul(wb[jd][:], wxc[:], dtb[jd][:]),
              reads=[wxc, dtb[jd]], writes=[wb[jd]])
        for n in range(DSTATE):
            j = n % 2
            for cc in range(ncb2):
                pb, pc = psB[cc % 2], psC[cc % 2]
                for h in range(CB2 // CB):
                    em.op("tensor", lambda t_, n=n, cc=cc, h=h, pb=pb: t_.matmul(
                        pb[:, h * CB:(h + 1) * CB], bselb[0:16, n * 128:(n + 1) * 128],
                        brow[0:16, cc * CB2 + h * CB: cc * CB2 + (h + 1) * CB],
                        start=True, stop=True),
                        reads=[bselb, brow], writes=[pb])
                    em.op("tensor", lambda t_, n=n, cc=cc, h=h, pc=pc: t_.matmul(
                        pc[:, h * CB:(h + 1) * CB], bselb[32:48, n * 128:(n + 1) * 128],
                        brow[32:48, cc * CB2 + h * CB: cc * CB2 + (h + 1) * CB],
                        start=True, stop=True),
                        reads=[bselb, brow], writes=[pc])
                em.op("scalar", lambda s, cc=cc, j=j, pb=pb: s.activation(
                    out=bbc[j][:, cc * CB2:(cc + 1) * CB2], in_=pb[:], func=AF.Identity),
                    reads=[pb], writes=[bbc[j]])
                em.op("scalar", lambda s, cc=cc, j=j, pc=pc: s.activation(
                    out=cbc[j][:, cc * CB2:(cc + 1) * CB2], in_=pc[:], func=AF.Identity),
                    reads=[pc], writes=[cbc[j]])
            em.op("scalar", lambda s, db=db, n=n, j=j, jd=jd: s.activation(
                out=da[j][:], in_=dtb[jd][:], func=AF.Exp,
                scale=ab[:, db * DSTATE + n:db * DSTATE + n + 1]),
                reads=[dtb[jd], ab], writes=[da[j]])
            em.op("vector", lambda v, j=j, jd=jd: v.tensor_mul(
                dbu[:], wb[jd][:], bbc[j][:]), reads=[wb[jd], bbc[j]], writes=[dbu])
            em.op("vector", lambda v, j=j: v.tensor_tensor_scan(
                out=hbb[:], data0=da[j][:], data1=dbu[:], initial=zcol[:],
                op0=ALU.mult, op1=ALU.add),
                reads=[da[j], dbu, zcol], writes=[hbb])
            if n == 0:
                em.op("vector", lambda v, j=j, jd=jd: v.tensor_mul(
                    yacc[jd][:], hbb[:], cbc[j][:]), reads=[hbb, cbc[j]], writes=[yacc[jd]])
            else:
                em.op("vector", lambda v, j=j: v.tensor_mul(
                    ynb[:], hbb[:], cbc[j][:]), reads=[hbb, cbc[j]], writes=[ynb])
                em.op("vector", lambda v, jd=jd: v.tensor_add(
                    yacc[jd][:], yacc[jd][:], ynb[:]), reads=[yacc[jd], ynb], writes=[yacc[jd]])
        em.dma(y_o, y_o[db * 128:(db + 1) * 128, :], yacc[jd], yacc[jd][:])
    em.pop()


def mk_bsel():
    b = np.zeros((48, DSTATE * 128), dtype=bfd)
    for n in range(DSTATE):
        b[n, n * 128:(n + 1) * 128] = 1.0
        b[32 + n, n * 128:(n + 1) * 128] = 1.0
    return b


def build_m2(nch=1024, slen=2 * TOK):
    em = Em("m2")
    em.cconst(0.0)
    t = {
        "xc": em.din("xc", (nch, slen), F32),
        "dt": em.din("dt", (nch, slen), F32),
        "bm": em.din("bm", (DSTATE, slen), F32),
        "cm": em.din("cm", (DSTATE, slen), F32),
        "alog": em.din("alog", (nch, DSTATE), F32),
        "bsel": em.din("bsel", (48, DSTATE * 128), BF16),
        "ym": em.dout("y", (nch, slen), F32),
    }
    emit_m2(em, t, nch, slen)
    em.finish()
    return em


def emit_m3(em, t, ntok=TOK):
    nt = ntok // 128
    x = t["x"]; xn1_i = t["xn1"]; y_i = t["ym"]; xc_i = t["xc"]; z_i = t["zm"]
    dvec = t["dvec"]; w_out = t["w_out"]; y_o = t["y"]
    em.push()
    dcol = em.sb("dcol", (128, 16), F32)
    em.dma(dcol, dcol[:], None, dvec[:])
    wout_bf = em.sb("wout_bf", (128, 16 * D), BF16)
    for k in range(16):
        em.dma(wout_bf, wout_bf[:, k * D:(k + 1) * D], None,
               w_out[k * 128:(k + 1) * 128, :], engine=_weng(w_out))
    ygb = [em.sb(f"ygb{k}", (128, ntok), BF16) for k in range(16)]
    ybf = em.sb("ybf", (128, TOK), F32)
    xcb = em.sb("xcb", (128, TOK), F32)
    zbf = em.sb("zbf", (128, TOK), F32)
    for k in range(16):
        em.dma(ybf, ybf[:, 0:ntok], None, y_i[k * 128:(k + 1) * 128, :])
        em.dma(xcb, xcb[:, 0:ntok], None, xc_i[k * 128:(k + 1) * 128, :])
        em.dma(zbf, zbf[:, 0:ntok], None, z_i[k * 128:(k + 1) * 128, :])
        em.op("vector", lambda v, k=k: v.scalar_tensor_tensor(
            out=ybf[:, 0:ntok], in0=xcb[:, 0:ntok], scalar=dcol[:, k:k + 1],
            in1=ybf[:, 0:ntok], op0=ALU.mult, op1=ALU.add),
            reads=[xcb, dcol, ybf], writes=[ybf])
        em.op("scalar", lambda s: s.activation(out=zbf[:, 0:ntok], in_=zbf[:, 0:ntok],
                                               func=AF.Silu),
              reads=[zbf], writes=[zbf])
        em.op("vector", lambda v, k=k: v.tensor_mul(ygb[k][:], ybf[:, 0:ntok], zbf[:, 0:ntok]),
              reads=[ybf, zbf], writes=[ygb[k]])
    xb = [em.sb(f"xb{j}", (128, D), F32) for j in range(2)]
    xnb = [em.sb(f"xnb{j}", (128, D), F32) for j in range(2)]
    yb = [em.sb(f"yb{j}", (128, D), F32) for j in range(2)]
    psO = em.ps("psO", (128, 1024), F32)
    for i in range(nt):
        xbb, xnn, ybb = xb[i % 2], xnb[i % 2], yb[i % 2]
        em.dma(xbb, xbb[:], None, x[i * 128:(i + 1) * 128, :])
        em.dma(xnn, xnn[:], None, xn1_i[i * 128:(i + 1) * 128, :])
        for dh in range(2):
            for k in range(16):
                em.op("tensor", lambda t, i=i, dh=dh, k=k: t.matmul(
                    psO[:, dh * 512:(dh + 1) * 512], ygb[k][:, i * 128:(i + 1) * 128],
                    wout_bf[:, k * D + dh * 512:k * D + (dh + 1) * 512],
                    start=(k == 0), stop=(k == 15)),
                    reads=[ygb[k], wout_bf], writes=[psO])
        em.op("vector", lambda v, xnn=xnn: v.tensor_add(xnn[:], xnn[:], psO[:]),
              reads=[xnn, psO], writes=[xnn])
        em.op("vector", lambda v, ybb=ybb, xbb=xbb, xnn=xnn: v.tensor_add(ybb[:], xnn[:], xbb[:]),
              reads=[xnn, xbb], writes=[ybb])
        em.dma(y_o, y_o[i * 128:(i + 1) * 128, :], ybb, ybb[:])
    em.pop()


def build_m3(ntok=TOK):
    em = Em("m3")
    em.cconst(0.0)
    t = {
        "x": em.din("x", (ntok, D), F32),
        "xn1": em.din("xn1", (ntok, D), F32),
        "ym": em.din("ym", (DI, ntok), F32),
        "xc": em.din("xc", (DI, ntok), F32),
        "zm": em.din("zm", (DI, ntok), F32),
        "dvec": em.din("dvec", (128, 16), F32),
        "w_out": em.din("w_out", (DI, D), F32),
        "y": em.dout("y", (ntok, D), F32),
    }
    emit_m3(em, t, ntok)
    em.finish()
    return em


_EMS = {}


def _get(name, builder):
    if name not in _EMS:
        _EMS[name] = builder()
    return _EMS[name]


def _run8(em, maps):
    if em.const_vals:
        ca = em.consts_input()
        for m in maps:
            m["consts_arr"] = ca
    res = run_bass_kernel_spmd(em.nc, maps, core_ids=list(range(NCORES)))
    return res.results


def _peer_call(x8, W_up, W_down, W_q, keys, de, ue):
    em = _get("peer", lambda: build_peer(ntok=TOK))
    base = peer_host_inputs(np.zeros((1,)), W_up, W_down, W_q, keys, de, ue)
    del base["x"]
    maps = [dict(base, x=np.ascontiguousarray(x8[c], dtype=np.float32)) for c in range(NCORES)]
    res = _run8(em, maps)
    return np.stack([r["y"] for r in res])


def _qffn_call(x8, w_int8, ws, bias):
    em = _get("qffn", lambda: build_qffn(ntok=TOK))
    base = qffn_host_inputs(np.zeros((1,)), w_int8, ws, bias)
    del base["x"]
    maps = [dict(base, x=np.ascontiguousarray(x8[c], dtype=np.float32)) for c in range(NCORES)]
    res = _run8(em, maps)
    return np.stack([r["y"] for r in res])


def _mamba_call(x8, ln_g, ln_b, W_in, conv_w, conv_b, W_x, W_dt, b_dt, A_log, Dv, W_out):
    def cvt16(v):
        return np.ascontiguousarray(np.asarray(v, np.float32).reshape(16, 128).T)
    em1 = _get("m1", lambda: build_m1(ntok=TOK))
    maps = []
    for c in range(NCORES):
        if c % 2 == 0:
            halo = np.zeros((3, D), np.float32)
            hm = np.zeros((3, 1), np.float32)
        else:
            halo = np.ascontiguousarray(x8[c - 1][TOK - 3:TOK], dtype=np.float32)
            hm = np.ones((3, 1), np.float32)
        maps.append({
            "x": np.ascontiguousarray(x8[c], dtype=np.float32), "xhalo": halo, "hmask": hm,
            "ln_g": np.asarray(ln_g, np.float32).reshape(1, D),
            "ln_b": np.asarray(ln_b, np.float32).reshape(1, D),
            "w_in": np.ascontiguousarray(W_in, dtype=np.float32),
            "conv_w": np.ascontiguousarray(conv_w, dtype=np.float32),
            "conv_b": cvt16(conv_b),
            "w_x": np.ascontiguousarray(W_x, dtype=np.float32),
            "w_dt": np.ascontiguousarray(W_dt, dtype=np.float32),
            "b_dt": cvt16(b_dt),
            "ident": np.eye(128, dtype=bfd),
        })
    r1 = _run8(em1, maps)
    em2 = _get("m2", lambda: build_m2(nch=1024, slen=2 * TOK))
    alog = np.ascontiguousarray(A_log, dtype=np.float32)
    maps2 = []
    for c in range(NCORES):
        b, half = c // 2, c % 2
        xc_full = np.concatenate([r1[2 * b]["xc"], r1[2 * b + 1]["xc"]], axis=1)
        dt_full = np.concatenate([r1[2 * b]["dt"], r1[2 * b + 1]["dt"]], axis=1)
        bc_full = np.concatenate([r1[2 * b]["bc"], r1[2 * b + 1]["bc"]], axis=1)
        maps2.append({
            "xc": np.ascontiguousarray(xc_full[half * 1024:(half + 1) * 1024]),
            "dt": np.ascontiguousarray(dt_full[half * 1024:(half + 1) * 1024]),
            "bm": np.ascontiguousarray(bc_full[0:16]),
            "cm": np.ascontiguousarray(bc_full[16:32]),
            "alog": np.ascontiguousarray(alog[half * 1024:(half + 1) * 1024]),
        })
    r2 = _run8(em2, maps2)
    em3 = _get("m3", lambda: build_m3(ntok=TOK))
    maps3 = []
    for c in range(NCORES):
        b, half = c // 2, c % 2
        y_full = np.concatenate([r2[2 * b]["y"], r2[2 * b + 1]["y"]], axis=0)
        maps3.append({
            "x": np.ascontiguousarray(x8[c], dtype=np.float32),
            "xn1": r1[c]["xn1"],
            "ym": np.ascontiguousarray(y_full[:, half * TOK:(half + 1) * TOK]),
            "xc": r1[c]["xc"], "zm": r1[c]["z"],
            "dvec": cvt16(Dv),
            "w_out": np.ascontiguousarray(W_out, dtype=np.float32),
        })
    r3 = _run8(em3, maps3)
    return np.stack([r["y"] for r in r3])


def _attn_call(x8, Wq, Wk, Wv, Wpost):
    em1 = _get("attn1", lambda: build_attn1(ntok=TOK))
    base = attn_host_inputs1(np.zeros((1,)), Wq, Wk, Wv)
    del base["x"]
    maps = [dict(base, x=np.ascontiguousarray(x8[c], dtype=np.float32)) for c in range(NCORES)]
    r1 = _run8(em1, maps)
    em2 = _get("attn2", lambda: build_attn2(ntok=TOK, sk=2 * TOK))
    maps2 = []
    wpost_c = np.ascontiguousarray(Wpost, dtype=np.float32)
    for c in range(NCORES):
        b = c // 2
        k_full = np.concatenate([r1[2 * b]["k"], r1[2 * b + 1]["k"]], axis=1)
        v_full = np.concatenate([r1[2 * b]["v"], r1[2 * b + 1]["v"]], axis=0)
        maps2.append({
            "x": np.ascontiguousarray(x8[c], dtype=np.float32),
            "xn": r1[c]["xn"], "q": r1[c]["q"],
            "k": np.ascontiguousarray(k_full), "v": np.ascontiguousarray(v_full),
            "wpost": wpost_c, "ident": np.eye(128, dtype=bfd),
        })
    r2 = _run8(em2, maps2)
    return np.stack([r["y"] for r in r2])


def build_fused():
    em = Em("fused", rd=False)
    em.cconst(0.0)
    ntok, sk = TOK, 2 * TOK
    p = em.preg

    x0 = em.din("x0", (ntok, D), F32)
    ident = em.din("ident", (128, 128), BF16)
    y_out = em.dout("yout", (ntok, D), F32)
    id_t = em.sb("id_t", (128, 128), BF16)
    em.dma(id_t, id_t[:], None, ident[:])

    t_peer = peer_dins(em, ntok)
    t_peer["id_t"] = id_t
    t_qffn = {
        "w_int8T": em.din("w_int8T", (D, D), I32),
        "wscale_col": em.din("wscale_col", (128, 1), F32),
        "bias_row": em.din("bias_row", (1, D), F32),
        "ident": ident, "id_t": id_t,
    }
    t_m1 = {
        "hmask": em.din("hmask", (3, 1), F32),
        "ln_g": em.din("ln_g", (1, D), F32),
        "ln_b": em.din("ln_b", (1, D), F32),
        "w_in": em.din("w_in", (D, 2 * DI), F32),
        "conv_w": em.din("conv_w", (DI, DCONV), F32),
        "conv_b": em.din("conv_b", (128, 16), F32),
        "w_x": em.din("w_x", (DI, 96), F32),
        "w_dt": em.din("w_dt", (DTRANK, DI), F32),
        "b_dt": em.din("b_dt", (128, 16), F32),
        "id_t": id_t,
    }
    alog = em.din("alog", (1024, DSTATE), F32)
    bsel_d = em.din("bsel", (48, DSTATE * 128), BF16)
    t_attn = {
        "wq": em.din("wq", (D, D), F32),
        "wk": em.din("wk", (D, 256), F32),
        "wv": em.din("wv", (D, 256), F32),
        "wpost": em.din("wpost", (D, D), F32),
    }
    dvec = em.din("dvec", (128, 16), F32)
    w_out_m = em.din("w_out", (DI, D), F32)

    x1 = em.dtmp("x1", (ntok, D), F32)
    x2 = em.dtmp("x2", (ntok, D), F32)
    x3 = em.dtmp("x3", (ntok, D), F32)
    x4 = em.dtmp("x4", (ntok, D), F32)
    x5 = em.dtmp("x5", (ntok, D), F32)
    xn1_loc = em.dtmp("xn1_loc", (ntok, D), F32)
    z_loc = em.dtmp("z_loc", (DI, ntok), F32)
    xn_a_loc = em.dtmp("xn_a_loc", (ntok, D), F32)
    q_loc = em.dtmp("q_loc", (D, ntok), BF16)
    bscr = em.dtmp("bscr2", (1, 16), F32)
    rd_dram = em.dtmp("rd_dram2", (AH, ntok), F32)

    halo_sh = em.dshared("halo_sh", (2, 3, D), F32)
    xc_sh = em.dshared("xc_sh", (DI, sk), F32)
    dt_sh = em.dshared("dt_sh", (DI, sk), F32)
    bc_sh = em.dshared("bc_sh", (32, sk), F32)
    y_sh = em.dshared("y_sh", (DI, sk), F32)
    k_sh = em.dshared("k_sh", (256, sk), BF16)
    v_sh = em.dshared("v_sh", (sk, 256), BF16)

    mir = {}
    plan = []
    for nm, b, rows, cols in [
            ("deT", t_peer["deT"], D, NEXP),
            ("wupT", t_peer["wupT"], D, IFF), ("wdnT", t_peer["wdnT"], IFF, D),
            ("ue", t_peer["ue"], NEXP, D)]:
        mir[nm] = em.dtmp(nm + "_bfm", (rows, cols), BF16)
        plan.append((b, mir[nm], rows, cols))
    t_peer.update(deT=mir["deT"], wupT=mir["wupT"],
                  wdnT=mir["wdnT"], ue=mir["ue"])

    em.push()
    emit_peer(em, dict(t_peer, x=x0, y=x1), ntok,
              hook=lambda: emit_precast(em, plan))
    em.pop()
    em.fence()
    em.push()
    hb = em.sb("halo_b", (3, D), F32)
    em.dma(hb, hb[:], x1, x1[ntok - 3:ntok, :])
    em.dma(halo_sh, halo_sh[bass.ts(p, 1), :, :].squeeze(0), hb, hb[:])
    em.pop()
    em.barrier(protects=[halo_sh])

    em.push()
    emit_m1(em, dict(t_m1, x=x1,
                     xhalo=BufView(halo_sh, halo_sh.t[0]),
                     xn1=xn1_loc,
                     xc=BufView(xc_sh, xc_sh.t[:, bass.ts(p, ntok)]),
                     z=z_loc,
                     dt=BufView(dt_sh, dt_sh.t[:, bass.ts(p, ntok)]),
                     bc=BufView(bc_sh, bc_sh.t[:, bass.ts(p, ntok)])), ntok)
    em.pop()
    em.fence()
    em.barrier(protects=[xc_sh, dt_sh, bc_sh])
    em.push()
    emit_m2(em, dict(xc=BufView(xc_sh, xc_sh.t[bass.ts(p, 1024), :]),
                     dt=BufView(dt_sh, dt_sh.t[bass.ts(p, 1024), :]),
                     bm=BufView(bc_sh, bc_sh.t[0:16, :]),
                     cm=BufView(bc_sh, bc_sh.t[16:32, :]),
                     alog=alog, bsel=bsel_d,
                     ym=BufView(y_sh, y_sh.t[bass.ts(p, 1024), :])), 1024, sk)
    em.pop()
    em.fence()
    em.barrier(protects=[y_sh])
    em.push()
    emit_m3(em, dict(x=x1, xn1=xn1_loc,
                     ym=BufView(y_sh, y_sh.t[:, bass.ts(p, ntok)]),
                     xc=BufView(xc_sh, xc_sh.t[:, bass.ts(p, ntok)]),
                     zm=z_loc, dvec=dvec, w_out=w_out_m, y=x2), ntok)
    em.pop()
    em.fence()

    em.push()
    emit_peer(em, dict(t_peer, x=x2, y=x3), ntok)
    em.pop()
    em.fence()

    em.push()
    emit_qffn(em, dict(t_qffn, x=x3, y=x4), ntok)
    em.pop()
    em.fence()

    em.push()
    emit_attn1(em, dict(t_attn, id_t=id_t, x=x4, xn=xn_a_loc, q=q_loc,
                        k=BufView(k_sh, k_sh.t[:, bass.ts(p, ntok)]),
                        v=BufView(v_sh, v_sh.t[bass.ts(p, ntok), :])), ntok)
    em.pop()
    em.fence()
    em.barrier(protects=[k_sh, v_sh])
    em.push()
    emit_attn2(em, dict(t_attn, id_t=id_t, x=x4, xn=xn_a_loc, q=q_loc,
                        k=k_sh, v=v_sh, y=x5, bscr=bscr, rd_dram=rd_dram), ntok, sk)
    em.pop()
    em.fence()

    em.push()
    emit_qffn(em, dict(t_qffn, x=x5, y=y_out), ntok)
    em.pop()
    em.finish()
    return em


def kernel(L0_attn_x, L0_attn_ln_g, L0_attn_ln_b, L0_attn_W_in, L0_attn_conv_w,
           L0_attn_conv_b, L0_attn_W_x, L0_attn_W_dt, L0_attn_b_dt, L0_attn_A_log,
           L0_attn_D, L0_attn_W_out, L0_ffn_W_up, L0_ffn_W_down, L0_ffn_W_q,
           L0_ffn_keys, L0_ffn_down_embed, L0_ffn_up_embed, L1_attn_Wq, L1_attn_Wk,
           L1_attn_Wv, L1_attn_Wpost, L1_ffn_w_int8, L1_ffn_weight_scale, L1_ffn_bias):
    def cvt16(v):
        return np.ascontiguousarray(np.asarray(v, np.float32).reshape(16, 128).T)

    xf = np.asarray(L0_attn_x, dtype=np.float32).reshape(NCORES * TOK, D)
    em = _get("fused", build_fused)
    base = {
        "ident": np.eye(128, dtype=bfd),
        "wqT": np.ascontiguousarray(np.asarray(L0_ffn_W_q, np.float32).T),
        "keysP": np.ascontiguousarray(np.asarray(L0_ffn_keys, np.float32).transpose(2, 0, 3, 1)),
        "deT": np.ascontiguousarray(np.asarray(L0_ffn_down_embed, np.float32).T),
        "ue": np.ascontiguousarray(np.asarray(L0_ffn_up_embed, np.float32)),
        "wupT": np.ascontiguousarray(np.asarray(L0_ffn_W_up, np.float32).T),
        "wdnT": np.ascontiguousarray(np.asarray(L0_ffn_W_down, np.float32).T),
        "w_int8T": np.ascontiguousarray(np.asarray(L1_ffn_w_int8, np.int32).T),
        "wscale_col": np.ascontiguousarray(np.broadcast_to(
            np.asarray(L1_ffn_weight_scale, np.float32).reshape(1, 1), (128, 1))),
        "bias_row": np.ascontiguousarray(np.asarray(L1_ffn_bias, np.float32).reshape(1, D)),
        "ln_g": np.asarray(L0_attn_ln_g, np.float32).reshape(1, D),
        "ln_b": np.asarray(L0_attn_ln_b, np.float32).reshape(1, D),
        "w_in": np.ascontiguousarray(np.asarray(L0_attn_W_in, np.float32)),
        "conv_w": np.ascontiguousarray(np.asarray(L0_attn_conv_w, np.float32)),
        "conv_b": cvt16(L0_attn_conv_b),
        "w_x": np.ascontiguousarray(np.asarray(L0_attn_W_x, np.float32)),
        "w_dt": np.ascontiguousarray(np.asarray(L0_attn_W_dt, np.float32)),
        "b_dt": cvt16(L0_attn_b_dt),
        "wq": np.ascontiguousarray(np.asarray(L1_attn_Wq, np.float32)),
        "wk": np.ascontiguousarray(np.asarray(L1_attn_Wk, np.float32)),
        "wv": np.ascontiguousarray(np.asarray(L1_attn_Wv, np.float32)),
        "wpost": np.ascontiguousarray(np.asarray(L1_attn_Wpost, np.float32)),
        "dvec": cvt16(L0_attn_D),
        "w_out": np.ascontiguousarray(np.asarray(L0_attn_W_out, np.float32)),
        "bsel": mk_bsel(),
        "consts_arr": em.consts_input(),
    }
    alog_f = np.ascontiguousarray(np.asarray(L0_attn_A_log, np.float32))
    nonce = int(np.random.randint(1, 2 ** 30))
    barseq = (nonce + np.arange(16, dtype=np.int64)).astype(np.int32).reshape(1, 16)
    maps = []
    for c in range(NCORES):
        par = c % 2
        m = dict(base)
        m["x0"] = np.ascontiguousarray(xf[c * TOK:(c + 1) * TOK])
        m["par"] = np.array([[par, 1 - par]], np.int32)
        m["barseq"] = barseq
        m["hmask"] = (np.zeros((3, 1), np.float32) if par == 0
                      else np.ones((3, 1), np.float32))
        m["alog"] = np.ascontiguousarray(alog_f[par * 1024:(par + 1) * 1024])
        maps.append(m)
    res = run_bass_kernel_spmd(em.nc, maps, core_ids=list(range(NCORES)))
    out = np.stack([r["yout"] for r in res.results])
    return np.ascontiguousarray(out.reshape(4, 4096, D))


# version 20
# speedup vs baseline: 1.2504x; 1.2027x over previous
import contextlib
import numpy as np
import ml_dtypes
import concourse.bass as bass
import concourse.bacc as bacc
import concourse.mybir as mybir
from concourse.bass_utils import run_bass_kernel_spmd

F32 = mybir.dt.float32
BF16 = mybir.dt.bfloat16
I32 = mybir.dt.int32
U32 = mybir.dt.uint32
AF = mybir.ActivationFunctionType
ALU = mybir.AluOpType
AX = mybir.AxisListType

NCORES = 8
TOK = 2048
D = 1024
DI = 2048
DSTATE = 16
DTRANK = 64
DCONV = 4
IFF = 4096
NEXP = 4096
NKEY = 64
HEADS_P = 4
AH = 16
AG = 4
AE = 64
LN_EPS = 1e-5
RMS_EPS = 1e-6
ACT_EPS = 1e-5
BIG = 1e30

bfd = ml_dtypes.bfloat16

_ENGINES = ("sync", "vector", "scalar", "tensor", "gpsimd")


class Buf:
    __slots__ = ("t", "wtok", "rtoks", "name")

    def __init__(self, t, name):
        self.t = t
        self.name = name
        self.wtok = None
        self.rtoks = []

    def __getitem__(self, key):
        return self.t[key]

    @property
    def ap(self):
        return self.t[:] if hasattr(self.t, "__enter__") or not hasattr(self.t, "ap") else self.t


class BufView:
    def __init__(self, parent, ap):
        self.parent = parent
        self.t = ap
        self.name = parent.name + "_view"

    def __getitem__(self, key):
        return self.t[key]

    @property
    def wtok(self):
        return self.parent.wtok

    @wtok.setter
    def wtok(self, v):
        self.parent.wtok = v

    @property
    def rtoks(self):
        return self.parent.rtoks

    @rtoks.setter
    def rtoks(self, v):
        self.parent.rtoks = v


class Em:

    def __init__(self, name, rd=True):
        self.nc = bacc.Bacc("TRN2", target_bir_lowering=False, debug=False,
                            detect_race_conditions=rd)
        self.ctx = contextlib.ExitStack()
        self.name = name
        self.out_names = []
        self.sems = {}
        self.cnt = {}
        for e in _ENGINES:
            self.sems[e] = self.ctx.enter_context(self.nc.semaphore("sem_" + e))
            self.cnt[e] = 0
        self.ndma = 20
        for i in range(self.ndma):
            self.sems[f"dma{i}"] = self.ctx.enter_context(self.nc.semaphore(f"sem_dma{i}"))
            self.cnt[f"dma{i}"] = 0
        self._dma_rr = 0
        self.ngdma = 8
        for i in range(self.ngdma):
            self.sems[f"gdma{i}"] = self.ctx.enter_context(self.nc.semaphore(f"sem_gdma{i}"))
            self.cnt[f"gdma{i}"] = 0
        self._gdma_rr = 0
        self._waited = {e: {} for e in _ENGINES}
        self._cbuf = None
        self.const_vals = []
        self._scopes = []
        cd = self.din("consts_arr", (1, 32), F32)
        self._cbuf_pending = cd
        self._bar = None
        self.nbar = 0

    def din(self, name, shape, dtype):
        return Buf(self.nc.dram_tensor(name, list(shape), dtype, kind="ExternalInput").ap(), name)

    def dout(self, name, shape, dtype):
        self.out_names.append(name)
        return Buf(self.nc.dram_tensor(name, list(shape), dtype, kind="ExternalOutput").ap(), name)

    def dtmp(self, name, shape, dtype):
        return Buf(self.nc.dram_tensor(name, list(shape), dtype, kind="Internal").ap(), name)

    def sb(self, name, shape, dtype):
        st = self._scopes[-1] if self._scopes else self.ctx
        self._uctr = getattr(self, "_uctr", 0) + 1
        name = f"{name}_u{self._uctr}"
        return Buf(st.enter_context(self.nc.sbuf_tensor(name, list(shape), dtype)), name)

    def ps(self, name, shape, dtype=F32):
        st = self._scopes[-1] if self._scopes else self.ctx
        self._uctr = getattr(self, "_uctr", 0) + 1
        name = f"{name}_u{self._uctr}"
        return Buf(st.enter_context(self.nc.psum_tensor(name, list(shape), dtype)), name)

    def push(self):
        self._scopes.append(contextlib.ExitStack())

    def pop(self):
        self.fence()
        self._scopes.pop().close()

    def _wait(self, engine, tok):
        if tok is None:
            return
        sem, n = tok
        key = id(sem)
        w = self._waited[engine]
        if w.get(key, -1) >= n:
            return
        getattr(self.nc, engine).wait_ge(sem, n)
        w[key] = n

    SKIP_SAME_ENGINE = ("tensor",)

    def _skip(self, engine, tok):
        if tok[0] is self.sems.get(engine):
            return engine in self.SKIP_SAME_ENGINE
        return False

    def _pre(self, engine, reads, writes, war=True):
        for b in reads:
            if b is not None and b.wtok is not None:
                tok = b.wtok
                if self._skip(engine, tok):
                    continue
                self._wait(engine, tok)
        for b in writes:
            if b is None:
                continue
            if b.wtok is not None:
                tok = b.wtok
                if not self._skip(engine, b.wtok):
                    self._wait(engine, b.wtok)
            if war:
                for tok in b.rtoks:
                    if self._skip(engine, tok):
                        continue
                    self._wait(engine, tok)

    def _post(self, engine, inst, reads, writes, sem_name=None, inc=1):
        sem_name = sem_name or engine
        sem = self.sems[sem_name]
        inst.then_inc(sem, inc)
        self.cnt[sem_name] += inc
        tok = (sem, self.cnt[sem_name])
        for b in writes:
            if b is not None:
                b.wtok = tok
                b.rtoks = []
        for b in reads:
            if b is not None:
                b.rtoks.append(tok)
        return tok

    def cconst(self, val, rows=128):
        val = float(val)
        if self._cbuf is None:
            assert not self._scopes, "first cconst must happen in base scope; pre-warm with em.cconst(0.0)"
            self._cbuf = self.sb("consts_buf", (128, 32), F32)
            self.dma(self._cbuf, self._cbuf[:], None, self._cbuf_pending[:].to_broadcast([128, 32]))
        if val not in self.const_vals:
            assert len(self.const_vals) < 32
            self.const_vals.append(val)
        i = self.const_vals.index(val)
        return self._cbuf[0:rows, i:i + 1]

    def consts_input(self):
        a = np.zeros((1, 32), np.float32)
        for i, v in enumerate(self.const_vals):
            a[0, i] = v
        return a

    def op(self, engine, fn, reads=(), writes=()):
        reads = list(reads)
        if self._cbuf is not None:
            reads.append(self._cbuf)
        self._pre(engine, reads, writes)
        inst = fn(getattr(self.nc, engine))
        return self._post(engine, inst, reads, writes)

    def dma(self, dst, dst_ap, src, src_ap, engine="sync"):
        reads = [src] if src is not None else []
        writes = [dst] if dst is not None else []
        self._pre(engine, reads, writes)
        if engine == "gpsimd":
            sname = f"gdma{self._gdma_rr}"
            self._gdma_rr = (self._gdma_rr + 1) % self.ngdma
        else:
            sname = f"dma{self._dma_rr}"
            self._dma_rr = (self._dma_rr + 1) % self.ndma
        if self.cnt[sname] > 0:
            self._wait(engine, (self.sems[sname], self.cnt[sname]))
        inst = getattr(self.nc, engine).dma_start(out=dst_ap, in_=src_ap)
        return self._post(engine, inst, reads, writes, sem_name=sname, inc=16)

    def dshared(self, name, shape, dtype):
        return Buf(self.nc.dram_tensor(name, list(shape), dtype, kind="Internal",
                                       addr_space="Shared").ap(), name)

    def _bar_init(self):
        if self._bar is not None:
            return self._bar
        par = self.din("par", (1, 2), I32)
        bseq = self.din("barseq", (1, 16), I32)
        pb = self.sb("bar_pb", (1, 2), I32)
        bs = self.sb("bar_bs", (1, 16), I32)
        fl = self.sb("bar_fl", (1, 1), I32)
        flags = self.dshared("bar_flags", (2, 16), I32)
        self.dma(pb, pb[:], None, par[:])
        self.dma(bs, bs[:], None, bseq[:])
        self.op("vector", lambda v: v.memset(fl[:], 0), reads=[], writes=[fl])
        self._wait("sync", pb.wtok)
        self._wait("sync", bs.wtok)
        self._wait("sync", fl.wtok)
        p = self.nc.sync.value_load(pb[0:1, 0:1], min_val=0, max_val=1)
        q = self.nc.sync.value_load(pb[0:1, 1:2], min_val=0, max_val=1)
        spin_sem = self.ctx.enter_context(self.nc.semaphore("sem_spin"))
        spin_tgt = self.nc.sync.to_reg(0)
        self._bar = dict(pb=pb, bs=bs, fl=fl, flags=flags, p=p, q=q,
                         spin_sem=spin_sem, spin_tgt=spin_tgt)
        return self._bar

    @property
    def preg(self):
        return self._bar_init()["p"]

    def barrier(self, protects=()):
        b = self._bar_init()
        k = self.nbar
        self.nbar += 1
        assert k < 16
        nc = self.nc
        for buf in protects:
            if buf.wtok is not None:
                self._wait("sync", buf.wtok)
        inst = nc.sync.dma_start(out=b["flags"][bass.ts(b["p"], 1), k:k + 1]
                                 .squeeze(0).unsqueeze(0), in_=b["bs"][0:1, k:k + 1])
        sname = f"dma{self._dma_rr}"
        self._dma_rr = (self._dma_rr + 1) % self.ndma
        if self.cnt[sname] > 0:
            self._wait("sync", (self.sems[sname], self.cnt[sname]))
        inst.then_inc(self.sems[sname], 16)
        self.cnt[sname] += 16
        self._wait("sync", (self.sems[sname], self.cnt[sname]))
        tgt = nc.sync.value_load(b["bs"][0:1, k:k + 1])

        def cond():
            v = nc.sync.value_load(b["fl"][0:1, 0:1])
            return v != tgt
        with nc.sync.While(cond):
            nc.sync.dma_start(out=b["fl"][:], in_=b["flags"][bass.ts(b["q"], 1), k:k + 1]
                              .squeeze(0).unsqueeze(0)).then_inc(b["spin_sem"], 16)
            nc.sync.reg_add(b["spin_tgt"], b["spin_tgt"], 16)
            nc.sync.wait_ge(b["spin_sem"], b["spin_tgt"])
        tok_inst = nc.sync.nop()
        tok = self._post("sync", tok_inst, [], list(protects))
        return tok

    def fence(self, engines=_ENGINES):
        for e in engines:
            for s_name, sem in self.sems.items():
                if self.cnt[s_name] > 0:
                    self._wait(e, (sem, self.cnt[s_name]))

    def finish(self, out_dma_engine="sync"):
        self.fence(engines=(out_dma_engine,))
        self.nc.compile()

    def close(self):
        self.ctx.close()




def run_em(em, in_map, core_ids=None, n_cores=1, trace=False):
    core_ids = core_ids if core_ids is not None else list(range(n_cores))
    maps = in_map if isinstance(in_map, list) else [in_map] * len(core_ids)
    res = run_bass_kernel_spmd(em.nc, maps, core_ids=core_ids, trace=trace)
    return res



def emit_ln(em, x_t, xn_t, scr, stats, width=D, eps=LN_EPS, x_slice=None, xn_slice=None):
    xap = x_t[:] if x_slice is None else x_slice
    oap = xn_t[:] if xn_slice is None else xn_slice
    s_sum = stats[:, 0:1]
    s_sq = stats[:, 1:2]
    s_mu = stats[:, 2:3]
    s_var = stats[:, 3:4]
    s_rstd = stats[:, 4:5]
    s_b = stats[:, 5:6]
    em.op("vector", lambda v: v.reduce_sum(out=s_sum, in_=xap, axis=AX.X),
          reads=[x_t], writes=[stats])
    em.op("scalar", lambda s: s.activation(out=scr[:, 0:width], in_=xap, func=AF.Square,
                                           accum_out=s_sq),
          reads=[x_t], writes=[scr, stats])
    em.op("vector", lambda v: v.tensor_scalar_mul(s_mu, s_sum, 1.0 / width),
          reads=[stats], writes=[stats])
    em.op("vector", lambda v: v.tensor_mul(s_var, s_mu, s_mu), reads=[stats], writes=[stats])
    em.op("vector", lambda v: v.scalar_tensor_tensor(out=s_var, in0=s_sq, scalar=1.0 / width,
                                                     in1=s_var, op0=ALU.mult, op1=ALU.subtract),
          reads=[stats], writes=[stats])
    parts = stats.t.shape[0] if hasattr(stats.t, 'shape') else 128
    epsc = em.cconst(eps, rows=parts)
    em.op("scalar", lambda s: s.activation(out=s_rstd, in_=s_var, func=AF.Ln, bias=epsc),
          reads=[stats], writes=[stats])
    em.op("scalar", lambda s: s.activation(out=s_rstd, in_=s_rstd, func=AF.Exp, scale=-0.5),
          reads=[stats], writes=[stats])
    em.op("vector", lambda v: v.scalar_tensor_tensor(out=s_b, in0=s_mu, scalar=-1.0,
                                                     in1=s_rstd, op0=ALU.mult, op1=ALU.mult),
          reads=[stats], writes=[stats])
    em.op("scalar", lambda s: s.activation(out=oap, in_=xap, func=AF.Identity,
                                           scale=s_rstd, bias=s_b),
          reads=[x_t, stats], writes=[xn_t])


def emit_transpose128(em, src_t, src_ap, ps_t, ps_ap, dst_t, dst_ap, ident_t):
    em.op("tensor", lambda t: t.transpose(ps_ap, src_ap, ident_t[:]),
          reads=[src_t, ident_t], writes=[ps_t])
    em.op("vector", lambda v: v.tensor_copy(out=dst_ap, in_=ps_ap),
          reads=[ps_t], writes=[dst_t])


def _weng(b):
    return "sync" if b.t.dtype == BF16 else "gpsimd"


def emit_precast(em, plan):
    em.push()
    CW = 4096
    nbuf = 4
    pin = [em.sb(f"pci{j}", (128, CW), F32) for j in range(nbuf)]
    pot = [em.sb(f"pco{j}", (128, CW), BF16) for j in range(nbuf)]
    k = 0
    for srcb, dstb, rows, cols in plan:
        for r in range(0, rows, 128):
            for c0 in range(0, cols, CW):
                w = min(CW, cols - c0)
                ib, ob = pin[k % nbuf], pot[k % nbuf]
                em.dma(ib, ib[:, 0:w], None, srcb[r:r + 128, c0:c0 + w])
                if k % 2 == 0:
                    em.op("vector", lambda v, ib=ib, ob=ob, w=w: v.tensor_copy(
                        out=ob[:, 0:w], in_=ib[:, 0:w]), reads=[ib], writes=[ob])
                else:
                    em.op("scalar", lambda s, ib=ib, ob=ob, w=w: s.activation(
                        out=ob[:, 0:w], in_=ib[:, 0:w], func=AF.Identity),
                        reads=[ib], writes=[ob])
                em.dma(dstb, dstb[r:r + 128, c0:c0 + w], ob, ob[:, 0:w])
                k += 1
    em.pop()


def emit_qffn(em, t, ntok=TOK):
    nt = ntok // 128
    x = t["x"]; wT = t["w_int8T"]; wsc = t["wscale_col"]; bias = t["bias_row"]
    ident = t["ident"]; y = t["y"]
    em.push()
    wt_i32 = em.sb("wt_i32", (128, 8 * D), I32)
    wt_bf = em.sb("wt_bf", (128, 8 * D), BF16)
    bias_bc = em.sb("bias_bc", (128, D), F32)
    wsc_t = em.sb("wsc_t", (128, 1), F32)
    id_t = em.sb("id_t", (128, 128), BF16)

    NS = 3
    xt = [em.sb(f"xt{i}", (128, D), F32) for i in range(NS)]
    scr_a = [em.sb(f"scr{j}", (128, D), F32) for j in range(NS)]
    xn_a = [em.sb(f"xn{j}", (128, D), F32) for j in range(NS)]
    qbf_a = [em.sb(f"qbf{j}", (128, D), BF16) for j in range(NS)]
    qi_a = [em.sb(f"qi{j}", (128, D), I32) for j in range(NS)]
    stats_a = [em.sb(f"stats{j}", (128, 8), F32) for j in range(NS)]
    st2_a = [em.sb(f"st2{j}", (128, 8), F32) for j in range(NS)]
    qfm_a = [em.sb(f"qfm{j}", (128, D), BF16) for j in range(2)]
    ot_a = [em.sb(f"ot{j}", (128, D), F32) for j in range(2)]
    yt = [em.sb(f"yt{i}", (128, D), F32) for i in range(2)]

    psM_a = [em.ps(f"psM{j}", (128, 1024), F32) for j in range(2)]
    psT_a = [em.ps(f"psT{j}", (128, 512), BF16) for j in range(2)]

    em.dma(id_t, id_t[:], None, ident[:])
    em.dma(wsc_t, wsc_t[:], None, wsc[:])
    em.dma(bias_bc, bias_bc[:], None, bias[:].to_broadcast([128, D]))
    for c in range(8):
        em.dma(wt_i32, wt_i32[:, c * D:(c + 1) * D], None, wT[c * 128:(c + 1) * 128, :])
    em.op("vector", lambda v: v.tensor_copy(out=wt_bf[:], in_=wt_i32[:]),
          reads=[wt_i32], writes=[wt_bf])

    def stage0(i):
        xb, scr, xn, qbf, qi = xt[i % NS], scr_a[i % NS], xn_a[i % NS], qbf_a[i % NS], qi_a[i % NS]
        stats, st2 = stats_a[i % NS], st2_a[i % NS]
        em.dma(xb, xb[:], None, x[i * 128:(i + 1) * 128, :])
        emit_ln(em, xb, xn, scr, stats)
        s_ms = st2[:, 0:1]; s_r2 = st2[:, 1:2]; s_am = st2[:, 2:3]
        s_scale = st2[:, 3:4]; s_comb = st2[:, 4:5]; s_fact = st2[:, 5:6]
        em.op("scalar", lambda s: s.activation(out=scr[:], in_=xn[:], func=AF.Square,
                                               accum_out=s_ms), reads=[xn], writes=[scr, st2])
        em.op("scalar", lambda s: s.activation(out=s_r2, in_=s_ms, func=AF.Ln,
                                               scale=1.0 / D, bias=em.cconst(RMS_EPS)),
              reads=[st2], writes=[st2])
        em.op("scalar", lambda s: s.activation(out=s_r2, in_=s_r2, func=AF.Exp, scale=-0.5),
              reads=[st2], writes=[st2])
        em.op("vector", lambda v: v.reduce_max(out=s_am, in_=xn[:], axis=AX.X,
                                               apply_absolute_value=True),
              reads=[xn], writes=[st2])
        em.op("vector", lambda v: v.tensor_mul(s_am, s_am, s_r2), reads=[st2], writes=[st2])
        em.op("vector", lambda v: v.tensor_scalar_max(s_am, s_am, float(ACT_EPS)),
              reads=[st2], writes=[st2])
        em.op("vector", lambda v: v.reciprocal(out=s_scale, in_=s_am), reads=[st2], writes=[st2])
        em.op("vector", lambda v: v.scalar_tensor_tensor(out=s_comb, in0=s_scale, scalar=127.0,
                                                         in1=s_r2, op0=ALU.mult, op1=ALU.mult),
              reads=[st2], writes=[st2])
        em.op("vector", lambda v: v.scalar_tensor_tensor(out=s_fact, in0=s_am,
                                                         scalar=1.0 / 127.0, in1=wsc_t[:],
                                                         op0=ALU.mult, op1=ALU.mult),
              reads=[st2, wsc_t], writes=[st2])
        em.op("scalar", lambda s: s.activation(out=scr[:], in_=xn[:], func=AF.Identity,
                                               scale=s_comb), reads=[xn, st2], writes=[scr])
        em.op("vector", lambda v: v.tensor_scalar(out=scr[:], in0=scr[:], scalar1=127.0,
                                                  scalar2=-128.0, op0=ALU.min, op1=ALU.max),
              reads=[scr], writes=[scr])
        em.op("vector", lambda v: v.tensor_copy(out=qi[:], in_=scr[:]), reads=[scr], writes=[qi])
        em.op("vector", lambda v: v.tensor_copy(out=xn[:], in_=qi[:]), reads=[qi], writes=[xn])
        em.op("vector", lambda v: v.tensor_sub(scr[:], scr[:], xn[:]), reads=[scr, xn], writes=[scr])
        em.op("vector", lambda v: v.tensor_scalar(out=qbf[:], in0=scr[:], scalar1=0.5,
                                                  scalar2=None, op0=ALU.is_gt),
              reads=[scr], writes=[qbf])
        em.op("vector", lambda v: v.tensor_add(xn[:], xn[:], qbf[:]), reads=[xn, qbf], writes=[xn])
        em.op("vector", lambda v: v.tensor_scalar(out=qbf[:], in0=scr[:], scalar1=-0.5,
                                                  scalar2=None, op0=ALU.is_lt),
              reads=[scr], writes=[qbf])
        em.op("vector", lambda v: v.tensor_sub(xn[:], xn[:], qbf[:]), reads=[xn, qbf], writes=[xn])
        em.op("vector", lambda v: v.tensor_copy(out=qbf[:], in_=xn[:]), reads=[xn], writes=[qbf])

    def stage1(i):
        qbf, qfm, psT = qbf_a[i % NS], qfm_a[i % 2], psT_a[i % 2]
        for c in range(8):
            emit_transpose128(em, qbf, qbf[:, c * 128:(c + 1) * 128], psT, psT[:, 0:128],
                              qfm, qfm[:, c * 128:(c + 1) * 128], id_t)

    def stage2(i):
        qfm, ot, yb, xb = qfm_a[i % 2], ot_a[i % 2], yt[i % 2], xt[i % NS]
        st2, psM = st2_a[i % NS], psM_a[i % 2]
        s_fact = st2[:, 5:6]
        for h in range(2):
            for c in range(8):
                em.op("tensor",
                      lambda t, h=h, c=c: t.matmul(
                          psM[:, h * 512:(h + 1) * 512],
                          qfm[:, c * 128:(c + 1) * 128],
                          wt_bf[:, c * D + h * 512: c * D + (h + 1) * 512],
                          start=(c == 0), stop=(c == 7)),
                      reads=[qfm, wt_bf], writes=[psM])
        em.op("scalar", lambda s: s.activation(out=ot[:], in_=psM[:], func=AF.Identity,
                                               scale=s_fact), reads=[psM, st2], writes=[ot])
        em.op("vector", lambda v: v.tensor_add(ot[:], ot[:], bias_bc[:]),
              reads=[ot, bias_bc], writes=[ot])
        em.op("vector", lambda v: v.scalar_tensor_tensor(out=yb[:], in0=ot[:], scalar=0.5,
                                                         in1=xb[:], op0=ALU.mult, op1=ALU.add),
              reads=[ot, xb], writes=[yb])
        em.dma(y, y[i * 128:(i + 1) * 128, :], yb, yb[:])

    for k in range(nt + 2):
        if k < nt:
            stage0(k)
        if 1 <= k <= nt:
            stage1(k - 1)
        if k >= 2:
            stage2(k - 2)
    em.pop()


def build_qffn(ntok=TOK):
    em = Em("qffn")
    em.cconst(0.0)
    t = {
        "x": em.din("x", (ntok, D), F32),
        "w_int8T": em.din("w_int8T", (D, D), I32),
        "wscale_col": em.din("wscale_col", (128, 1), F32),
        "bias_row": em.din("bias_row", (1, D), F32),
        "ident": em.din("ident", (128, 128), BF16),
        "y": em.dout("y", (ntok, D), F32),
    }
    emit_qffn(em, t, ntok)
    em.finish()
    return em


def qffn_host_inputs(x_shard, w_int8, weight_scale, bias, em=None):
    return {
        "x": np.ascontiguousarray(x_shard, dtype=np.float32),
        "w_int8T": np.ascontiguousarray(w_int8.T).astype(np.int32, copy=False),
        "wscale_col": np.ascontiguousarray(np.broadcast_to(np.asarray(weight_scale, np.float32).reshape(1, 1), (128, 1))),
        "bias_row": np.ascontiguousarray(bias, dtype=np.float32).reshape(1, D),
        "ident": np.eye(128, dtype=bfd),
    }




def emit_peer(em, t, ntok=TOK, dbg=False, hook=None):
    nt = ntok // 128
    x = t["x"]; wqT = t["wqT"]; keysP = t["keysP"]; deT = t["deT"]; ue = t["ue"]
    wupT = t["wupT"]; wdnT = t["wdnT"]; y = t["y"]
    wt_dram = t["wt_dram"]; u_dram = t["u_dram"]; id_t = t["id_t"]
    if dbg:
        dbg_xn = em.dout("dbg_xn", (128, D), BF16)
        dbg_sim = em.dout("dbg_sim", (128, 512), F32)
        dbg_sxm = em.dout("dbg_sxm", (128, 256), BF16)
        dbg_sym = em.dout("dbg_sym", (128, 256), BF16)
        dbg_small = em.dout("dbg_small", (128, 12), F32)
        dbg_proj = em.dout("dbg_proj", (128, NEXP), BF16)
        dbg_w = em.dout("dbg_w", (128, NEXP), BF16)

    xt = [em.sb(f"xt{i}", (128, D), F32) for i in range(2)]
    scr = em.sb("scr", (128, D), F32)
    stats = em.sb("stats", (128, 8), F32)
    em.push()
    xn_fm = [em.sb(f"xnfm{c}", (128, ntok), BF16) for c in range(8)]
    xn_tm = em.sb("xn_tm", (128, D), BF16)
    psT = em.ps("psT", (128, 512), BF16)
    for i in range(nt):
        xb = xt[i % 2]
        em.dma(xb, xb[:], None, x[i * 128:(i + 1) * 128, :])
        emit_ln(em, xb, xn_tm, scr, stats)
        if dbg and i == 0:
            em.dma(dbg_xn, dbg_xn[:], xn_tm, xn_tm[:])
        for c in range(8):
            emit_transpose128(em, xn_tm, xn_tm[:, c * 128:(c + 1) * 128],
                              psT, psT[:, 0:128],
                              xn_fm[c], xn_fm[c][:, i * 128:(i + 1) * 128], id_t)

    sxm_all = em.sb("sxm_all", (128, nt * 256), BF16)
    sym_all = em.sb("sym_all", (128, nt * 256), BF16)
    thr_all = em.sb("thr_all", (128, nt * 4), F32)
    ngm_all = em.sb("ngm_all", (128, nt * 4), F32)
    rcp_all = em.sb("rcp_all", (128, nt * 4), F32)

    em.push()
    wq_bf = em.sb("wq_bf", (128, 8 * 512), BF16)
    for c in range(8):
        em.dma(wq_bf, wq_bf[:, c * 512:(c + 1) * 512], None,
               wqT[c * 128:(c + 1) * 128, :], engine=_weng(wqT))
    keys_bf = em.sb("keys_bf", (128, 8 * 64), BF16)
    for p in range(2):
        for h in range(HEADS_P):
            em.dma(keys_bf, keys_bf[0:64, (p * 4 + h) * 64:(p * 4 + h + 1) * 64], None,
                   keysP[p, h], engine="gpsimd")
            em.dma(keys_bf, keys_bf[64:128, (p * 4 + h) * 64:(p * 4 + h + 1) * 64], None,
                   keysP[p, h], engine="gpsimd")
    q_fm = [em.sb(f"qfm{c}", (128, ntok), BF16) for c in range(4)]
    psQ = em.ps("psQ", (128, 512), F32)
    tqs = min(512, ntok)
    for qc in range(4):
        for th in range(ntok // tqs):
            for c in range(8):
                em.op("tensor", lambda t, qc=qc, th=th, c=c: t.matmul(
                    psQ[:, 0:tqs], wq_bf[:, c * 512 + qc * 128: c * 512 + (qc + 1) * 128],
                    xn_fm[c][:, th * tqs:(th + 1) * tqs],
                    start=(c == 0), stop=(c == 7)),
                    reads=[wq_bf, xn_fm[c]], writes=[psQ])
            em.op("scalar", lambda s, qc=qc, th=th: s.activation(
                out=q_fm[qc][:, th * tqs:(th + 1) * tqs], in_=psQ[:, 0:tqs], func=AF.Identity),
                reads=[psQ], writes=[q_fm[qc]])

    simb = em.sb("simb", (128, 512), F32)
    mrb = em.sb("mrb", (128, 512), F32)
    eqb = em.sb("eqb", (128, 512), F32)
    d1b = em.sb("d1b", (128, 512), F32)
    v8x = em.sb("v8x", (128, 32), BF16)
    v8y = em.sb("v8y", (128, 32), BF16)
    v8f = em.sb("v8f", (128, 8), F32)
    p64 = em.sb("p64", (128, 256), BF16)
    v8p = em.sb("v8p", (128, 32), BF16)
    pexp = em.sb("pexp", (128, 32), F32)
    psS = em.ps("psS", (128, 64), F32)
    for i in range(nt):
        for p in range(2):
            for h in range(HEADS_P):
                qc = p * 2 + h // 2
                base = (h % 2) * 64
                em.op("tensor", lambda t, p=p, h=h, qc=qc, base=base, i=i: t.matmul(
                    psS[:], q_fm[qc][base:base + 64, i * 128:(i + 1) * 128],
                    keys_bf[base:base + 64, (p * 4 + h) * 64:(p * 4 + h + 1) * 64],
                    start=True, stop=True),
                    reads=[q_fm[qc], keys_bf], writes=[psS])
                em.op("scalar", lambda s, p=p, h=h: s.activation(
                    out=simb[:, (p * 4 + h) * 64:(p * 4 + h + 1) * 64], in_=psS[:],
                    func=AF.Identity), reads=[psS], writes=[simb])
        for p in range(2):
            for h in range(HEADS_P):
                seg = simb[:, (p * 4 + h) * 64:(p * 4 + h + 1) * 64]
                mseg = mrb[:, (p * 4 + h) * 64:(p * 4 + h + 1) * 64]
                em.op("vector", lambda v, seg=seg: v.max(out=v8f[:], in_=seg),
                      reads=[simb], writes=[v8f])
                em.op("vector", lambda v, seg=seg, mseg=mseg: v.match_replace(
                    out=mseg, in_to_replace=v8f[:], in_values=seg, imm_value=-BIG),
                      reads=[simb, v8f], writes=[mrb])
        em.op("vector", lambda v: v.tensor_tensor(out=eqb[:], in0=simb[:], in1=mrb[:],
                                                  op=ALU.is_equal),
              reads=[simb, mrb], writes=[eqb])
        em.op("vector", lambda v: v.tensor_scalar_add(d1b[:], simb[:], BIG),
              reads=[simb], writes=[d1b])
        em.op("vector", lambda v: v.tensor_mul(d1b[:], d1b[:], eqb[:]),
              reads=[d1b, eqb], writes=[d1b])
        em.op("vector", lambda v, i=i: v.tensor_sub(sxm_all[:, i * 256:(i + 1) * 256],
                                                    simb[:, 0:256], d1b[:, 0:256]),
              reads=[simb, d1b], writes=[sxm_all])
        em.op("vector", lambda v, i=i: v.tensor_sub(sym_all[:, i * 256:(i + 1) * 256],
                                                    simb[:, 256:512], d1b[:, 256:512]),
              reads=[simb, d1b], writes=[sym_all])
        for h in range(HEADS_P):
            em.op("vector", lambda v, i=i, h=h: v.max(
                out=v8x[:, h * 8:(h + 1) * 8],
                in_=sxm_all[:, i * 256 + h * 64: i * 256 + (h + 1) * 64]),
                reads=[sxm_all], writes=[v8x])
            em.op("vector", lambda v, i=i, h=h: v.max(
                out=v8y[:, h * 8:(h + 1) * 8],
                in_=sym_all[:, i * 256 + h * 64: i * 256 + (h + 1) * 64]),
                reads=[sym_all], writes=[v8y])
        em.op("vector", lambda v: v.tensor_tensor(
            out=p64[:].rearrange("p (h i j) -> p h i j", h=4, i=8),
            in0=v8x[:].rearrange("p (h i) -> p h i", h=4).unsqueeze(3).to_broadcast([128, 4, 8, 8]),
            in1=v8y[:].rearrange("p (h j) -> p h j", h=4).unsqueeze(2).to_broadcast([128, 4, 8, 8]),
            op=ALU.add), reads=[v8x, v8y], writes=[p64])
        for h in range(HEADS_P):
            em.op("vector", lambda v, h=h: v.max(out=v8p[:, h * 8:(h + 1) * 8],
                                                 in_=p64[:, h * 64:(h + 1) * 64]),
                  reads=[p64], writes=[v8p])
        for h in range(HEADS_P):
            em.op("vector", lambda v, i=i, h=h: v.tensor_copy(
                out=thr_all[:, i * 4 + h: i * 4 + h + 1], in_=v8p[:, h * 8 + 7: h * 8 + 8]),
                reads=[v8p], writes=[thr_all])
            em.op("vector", lambda v, i=i, h=h: v.tensor_scalar_mul(
                ngm_all[:, i * 4 + h: i * 4 + h + 1], v8p[:, h * 8: h * 8 + 1], -1.0),
                reads=[v8p], writes=[ngm_all])
            em.op("scalar", lambda s, i=i, h=h: s.activation(
                out=pexp[:, h * 8:(h + 1) * 8], in_=v8p[:, h * 8:(h + 1) * 8], func=AF.Exp,
                bias=ngm_all[:, i * 4 + h: i * 4 + h + 1]),
                reads=[v8p, ngm_all], writes=[pexp])
        em.op("vector", lambda v, i=i: v.reduce_sum(
            out=rcp_all[:, i * 4:(i + 1) * 4],
            in_=pexp[:].rearrange("p (h k) -> p h k", h=4), axis=AX.X),
            reads=[pexp], writes=[rcp_all])
        em.op("vector", lambda v, i=i: v.reciprocal(
            out=rcp_all[:, i * 4:(i + 1) * 4], in_=rcp_all[:, i * 4:(i + 1) * 4]),
            reads=[rcp_all], writes=[rcp_all])
        if dbg and i == 0:
            em.dma(dbg_sim, dbg_sim[:], simb, simb[:])
            em.dma(dbg_sxm, dbg_sxm[:], sxm_all, sxm_all[:, 0:256])
            em.dma(dbg_sym, dbg_sym[:], sym_all, sym_all[:, 0:256])
            em.dma(dbg_small, dbg_small[:, 0:4], thr_all, thr_all[:, 0:4])
            em.dma(dbg_small, dbg_small[:, 4:8], ngm_all, ngm_all[:, 0:4])
            em.dma(dbg_small, dbg_small[:, 8:12], rcp_all, rcp_all[:, 0:4])

    em.pop()
    if hook is not None:
        hook()

    em.push()
    deT_bf = [em.sb(f"deT{c}", (128, NEXP), BF16) for c in range(8)]
    for c in range(8):
        for half in range(2):
            em.dma(deT_bf[c], deT_bf[c][:, half * 2048:(half + 1) * 2048], None,
                   deT[c * 128:(c + 1) * 128, half * 2048:(half + 1) * 2048], engine=_weng(deT))
    proj_1 = em.sb("proj_b", (128, NEXP), BF16)
    proj_b = [proj_1, proj_1]
    zb = [em.sb(f"zb{j}", (128, NEXP), BF16) for j in range(2)]
    eb = [em.sb(f"eb{j}", (128, NEXP), BF16) for j in range(2)]
    mb_ = [em.sb(f"mb{j}", (128, NEXP), BF16) for j in range(2)]
    wacc = [em.sb(f"wacc{j}", (128, NEXP), BF16) for j in range(2)]
    wts_1 = em.sb("wts", (128, 32 * 128), BF16)
    wts = [wts_1, wts_1]
    psP = em.ps("psP", (128, 512), F32)
    psP2 = em.ps("psP2", (128, 512), F32)
    psW = em.ps("psW", (128, 512), BF16)
    psW2 = em.ps("psW2", (128, 512), BF16)
    for i in range(nt):
        ti = i % 2
        prj, wa, wt_ = proj_b[ti], wacc[ti], wts[ti]
        for ec in range(8):
            pp = psP if ec % 2 == 0 else psP2
            for c in range(8):
                em.op("tensor", lambda t, i=i, ec=ec, c=c, pp=pp: t.matmul(
                    pp[:], xn_fm[c][:, i * 128:(i + 1) * 128],
                    deT_bf[c][:, ec * 512:(ec + 1) * 512],
                    start=(c == 0), stop=(c == 7)),
                    reads=[xn_fm[c], deT_bf[c]], writes=[pp])
            em.op("scalar", lambda s, ec=ec, pp=pp, prj=prj: s.activation(
                out=prj[:, ec * 512:(ec + 1) * 512], in_=pp[:], func=AF.Identity),
                reads=[pp], writes=[prj])
        for hh in range(2):
            for j2 in range(2):
                h = hh * 2 + j2
                em.op("vector", lambda v, i=i, h=h, zz=zb[j2]: v.tensor_tensor(
                    out=zz[:].rearrange("p (i j) -> p i j", i=64),
                    in0=sxm_all[:, i * 256 + h * 64:i * 256 + (h + 1) * 64]
                        .unsqueeze(2).to_broadcast([128, 64, 64]),
                    in1=sym_all[:, i * 256 + h * 64:i * 256 + (h + 1) * 64]
                        .unsqueeze(1).to_broadcast([128, 64, 64]),
                    op=ALU.add), reads=[sxm_all, sym_all], writes=[zb[j2]])
            for j2 in range(2):
                h = hh * 2 + j2
                em.op("scalar", lambda s, i=i, h=h, zz=zb[j2], ee=eb[j2]: s.activation(
                    out=ee[:], in_=zz[:], func=AF.Exp,
                    bias=ngm_all[:, i * 4 + h:i * 4 + h + 1]),
                    reads=[zb[j2], ngm_all], writes=[eb[j2]])
            for j2 in range(2):
                h = hh * 2 + j2
                em.op("vector", lambda v, i=i, h=h, zz=zb[j2], ee=eb[j2], mm=mb_[j2]:
                      v.scalar_tensor_tensor(
                          out=mm[:], in0=zz[:], scalar=thr_all[:, i * 4 + h:i * 4 + h + 1],
                          in1=ee[:], op0=ALU.is_ge, op1=ALU.mult),
                      reads=[zb[j2], thr_all, eb[j2]], writes=[mb_[j2]])
                em.op("vector", lambda v, mm=mb_[j2], prj=prj: v.tensor_mul(mm[:], mm[:], prj[:]),
                      reads=[mb_[j2], prj], writes=[mb_[j2]])
            for j2 in range(2):
                h = hh * 2 + j2
                dst = wa if h == 0 else eb[j2]
                em.op("scalar", lambda s, i=i, h=h, mm=mb_[j2], ss=dst: s.activation(
                    out=ss[:], in_=mm[:], func=AF.Silu,
                    scale=rcp_all[:, i * 4 + h:i * 4 + h + 1]),
                    reads=[mb_[j2], rcp_all], writes=[dst])
            for j2 in range(2):
                h = hh * 2 + j2
                if h > 0:
                    em.op("vector", lambda v, ss=eb[j2], wa=wa: v.tensor_add(wa[:], wa[:], ss[:]),
                          reads=[wa, eb[j2]], writes=[wa])
        if dbg and i == 0:
            em.dma(dbg_proj, dbg_proj[:], prj, prj[:])
            em.dma(dbg_w, dbg_w[:], wa, wa[:])
        pw = psW if ti == 0 else psW2
        for c in range(32):
            emit_transpose128(em, wa, wa[:, c * 128:(c + 1) * 128], pw, pw[:, 0:128],
                              wt_, wt_[:, c * 128:(c + 1) * 128], id_t)
        em.dma(wt_dram, wt_dram[i].rearrange("c p t -> p c t"), wt_,
               wt_[:].rearrange("p (c t) -> p c t", c=32), engine="sync")

    em.pop()

    em.push()
    wup_bf = [em.sb(f"wup{c}", (128, IFF), BF16) for c in range(8)]
    for c in range(8):
        for half in range(2):
            em.dma(wup_bf[c], wup_bf[c][:, half * 2048:(half + 1) * 2048], None,
                   wupT[c * 128:(c + 1) * 128, half * 2048:(half + 1) * 2048], engine=_weng(wupT))
    ub = [em.sb(f"ub{j}", (128, ntok), BF16) for j in range(2)]
    psU = em.ps("psU", (128, 512), F32)
    psU2 = em.ps("psU2", (128, 512), F32)
    tqs2 = min(512, ntok)
    for ic in range(32):
        u_t = ub[ic % 2]
        for tq in range(ntok // tqs2):
            pp = psU if tq % 2 == 0 else psU2
            for c in range(8):
                em.op("tensor", lambda t, ic=ic, tq=tq, c=c, pp=pp: t.matmul(
                    pp[:, 0:tqs2], wup_bf[c][:, ic * 128:(ic + 1) * 128],
                    xn_fm[c][:, tq * tqs2:(tq + 1) * tqs2],
                    start=(c == 0), stop=(c == 7)),
                    reads=[wup_bf[c], xn_fm[c]], writes=[pp])
            em.op("scalar", lambda s, tq=tq, pp=pp, u_t=u_t: s.activation(
                out=u_t[:, tq * tqs2:(tq + 1) * tqs2], in_=pp[:, 0:tqs2], func=AF.Silu),
                reads=[pp], writes=[u_t])
        em.dma(u_dram, u_dram[ic], u_t, u_t[:], engine="sync")

    em.pop()
    em.pop()

    em.push()
    wdn_bf = [em.sb(f"wdn{c}", (128, 32 * 128), BF16) for c in range(8)]
    for ic in range(32):
        em.dma(wdn_bf[ic // 4], wdn_bf[ic // 4][:, (ic % 4) * 1024:(ic % 4 + 1) * 1024],
               None, wdnT[ic * 128:(ic + 1) * 128, :], engine=_weng(wdnT))
    ue_bf = [em.sb(f"ue{c}", (128, 32 * 128), BF16) for c in range(8)]
    for ec in range(32):
        em.dma(ue_bf[ec // 4], ue_bf[ec // 4][:, (ec % 4) * 1024:(ec % 4 + 1) * 1024],
               None, ue[ec * 128:(ec + 1) * 128, :], engine=_weng(ue))
    ustrip = [em.sb(f"ustrip{j}", (128, 4096), BF16) for j in range(2)]
    wtstrip = [em.sb(f"wtstrip{j}", (128, 4096), BF16) for j in range(2)]
    yb = [em.sb(f"yb{j}", (128, D), F32) for j in range(2)]
    psO = em.ps("psO", (128, 1024), F32)
    for i in range(nt):
        us = ustrip[i % 2]
        ws_ = wtstrip[i % 2]
        xb = xt[i % 2]
        ybuf = yb[i % 2]
        em.dma(us, us[:].rearrange("p (c t) -> p c t", c=32), u_dram,
               u_dram[:, :, i * 128:(i + 1) * 128].rearrange("c p t -> p c t"), engine="sync")
        em.dma(ws_, ws_[:].rearrange("p (c t) -> p c t", c=32), wt_dram,
               wt_dram[i].rearrange("c p t -> p c t"), engine="sync")
        em.dma(xb, xb[:], None, x[i * 128:(i + 1) * 128, :])
        for dh in range(2):
            for ic in range(32):
                em.op("tensor", lambda t, i=i, dh=dh, ic=ic, us=us: t.matmul(
                    psO[:, dh * 512:(dh + 1) * 512], us[:, ic * 128:(ic + 1) * 128],
                    wdn_bf[ic // 4][:, (ic % 4) * 1024 + dh * 512:(ic % 4) * 1024 + (dh + 1) * 512],
                    start=(ic == 0), stop=False),
                    reads=[us, wdn_bf[ic // 4]], writes=[psO])
            for ec in range(32):
                em.op("tensor", lambda t, i=i, dh=dh, ec=ec, ws_=ws_: t.matmul(
                    psO[:, dh * 512:(dh + 1) * 512], ws_[:, ec * 128:(ec + 1) * 128],
                    ue_bf[ec // 4][:, (ec % 4) * 1024 + dh * 512:(ec % 4) * 1024 + (dh + 1) * 512],
                    start=False, stop=(ec == 31)),
                    reads=[ws_, ue_bf[ec // 4]], writes=[psO])
        em.op("vector", lambda v, ybuf=ybuf, xb=xb: v.scalar_tensor_tensor(
            out=ybuf[:], in0=psO[:], scalar=0.5, in1=xb[:], op0=ALU.mult, op1=ALU.add),
            reads=[psO, xb], writes=[ybuf])
        em.dma(y, y[i * 128:(i + 1) * 128, :], ybuf, ybuf[:])

    em.pop()


def peer_dins(em, ntok=TOK):
    t = {
        "wqT": em.din("wqT", (D, 512), F32),
        "keysP": em.din("keysP", (2, HEADS_P, 64, 64), F32),
        "deT": em.din("deT", (D, NEXP), F32),
        "ue": em.din("ue", (NEXP, D), F32),
        "wupT": em.din("wupT", (D, IFF), F32),
        "wdnT": em.din("wdnT", (IFF, D), F32),
        "wt_dram": em.dtmp("wt_dram", (ntok // 128, 32, 128, 128), BF16),
        "u_dram": em.dtmp("u_dram", (32, 128, ntok), BF16),
    }
    return t


def build_peer(ntok=TOK, dbg=False):
    em = Em("peer")
    em.cconst(0.0)
    t = peer_dins(em, ntok)
    t["x"] = em.din("x", (ntok, D), F32)
    t["ident"] = em.din("ident", (128, 128), BF16)
    t["y"] = em.dout("y", (ntok, D), F32)
    id_t = em.sb("id_t", (128, 128), BF16)
    em.dma(id_t, id_t[:], None, t["ident"][:])
    t["id_t"] = id_t
    emit_peer(em, t, ntok, dbg=dbg)
    em.finish()
    return em


def peer_host_inputs(x_shard, W_up, W_down, W_q, keys, down_embed, up_embed):
    return {
        "x": np.ascontiguousarray(x_shard, dtype=np.float32),
        "wqT": np.ascontiguousarray(W_q.T, dtype=np.float32),
        "keysP": np.ascontiguousarray(keys.transpose(2, 0, 3, 1), dtype=np.float32),
        "deT": np.ascontiguousarray(down_embed.T, dtype=np.float32),
        "ue": np.ascontiguousarray(up_embed, dtype=np.float32),
        "wupT": np.ascontiguousarray(W_up.T, dtype=np.float32),
        "wdnT": np.ascontiguousarray(W_down.T, dtype=np.float32),
        "ident": np.eye(128, dtype=bfd),
    }


def emit_attn1(em, t, ntok=TOK):
    nt = ntok // 128
    x = t["x"]; wq = t["wq"]; wk = t["wk"]; wv = t["wv"]; id_t = t["id_t"]
    xn_o = t["xn"]; q_o = t["q"]; k_o = t["k"]; v_o = t["v"]
    em.push()
    xt = [em.sb(f"xt{i}", (128, D), F32) for i in range(2)]
    scr = em.sb("scr", (128, D), F32)
    stats = em.sb("stats", (128, 8), F32)
    xn_f32 = em.sb("xn_f32", (128, D), F32)
    xn_tm = em.sb("xn_tm", (128, D), BF16)
    xn_fm = [em.sb(f"xnfm{c}", (128, ntok), BF16) for c in range(8)]
    psT = em.ps("psT", (128, 512), BF16)
    for i in range(nt):
        xb = xt[i % 2]
        em.dma(xb, xb[:], None, x[i * 128:(i + 1) * 128, :])
        emit_ln(em, xb, xn_f32, scr, stats)
        em.dma(xn_o, xn_o[i * 128:(i + 1) * 128, :], xn_f32, xn_f32[:])
        em.op("vector", lambda v: v.tensor_copy(out=xn_tm[:], in_=xn_f32[:]),
              reads=[xn_f32], writes=[xn_tm])
        for c in range(8):
            emit_transpose128(em, xn_tm, xn_tm[:, c * 128:(c + 1) * 128],
                              psT, psT[:, 0:128],
                              xn_fm[c], xn_fm[c][:, i * 128:(i + 1) * 128], id_t)

    wq_bf = [em.sb(f"wqb{c}", (128, D), BF16) for c in range(8)]
    wk_bf = [em.sb(f"wkb{c}", (128, 256), BF16) for c in range(8)]
    wv_bf = [em.sb(f"wvb{c}", (128, 256), BF16) for c in range(8)]
    for c in range(8):
        em.dma(wq_bf[c], wq_bf[c][:], None, wq[c * 128:(c + 1) * 128, :], engine=_weng(wq))
        em.dma(wk_bf[c], wk_bf[c][:], None, wk[c * 128:(c + 1) * 128, :], engine=_weng(wk))
        em.dma(wv_bf[c], wv_bf[c][:], None, wv[c * 128:(c + 1) * 128, :], engine=_weng(wv))
    ob_a = [em.sb(f"ob{j}", (128, 512), BF16) for j in range(2)]
    psM = em.ps("psM", (128, 512), F32)
    psM2 = em.ps("psM2", (128, 512), F32)
    tqs = min(512, ntok)
    for qc in range(8):
        for th in range(ntok // tqs):
            pp = psM if th % 2 == 0 else psM2
            ob = ob_a[th % 2]
            for c in range(8):
                em.op("tensor", lambda t, qc=qc, th=th, c=c, pp=pp: t.matmul(
                    pp[:, 0:tqs], wq_bf[c][:, qc * 128:(qc + 1) * 128],
                    xn_fm[c][:, th * tqs:(th + 1) * tqs],
                    start=(c == 0), stop=(c == 7)),
                    reads=[wq_bf[c], xn_fm[c]], writes=[pp])
            em.op("scalar", lambda s, pp=pp, ob=ob: s.activation(out=ob[:, 0:tqs], in_=pp[:, 0:tqs],
                                                                 func=AF.Identity),
                  reads=[pp], writes=[ob])
            em.dma(q_o, q_o[qc * 128:(qc + 1) * 128, th * tqs:(th + 1) * tqs], ob, ob[:, 0:tqs])
    for kc in range(2):
        for th in range(ntok // tqs):
            pp = psM if th % 2 == 0 else psM2
            ob = ob_a[th % 2]
            for c in range(8):
                em.op("tensor", lambda t, kc=kc, th=th, c=c, pp=pp: t.matmul(
                    pp[:, 0:tqs], wk_bf[c][:, kc * 128:(kc + 1) * 128],
                    xn_fm[c][:, th * tqs:(th + 1) * tqs],
                    start=(c == 0), stop=(c == 7)),
                    reads=[wk_bf[c], xn_fm[c]], writes=[pp])
            em.op("scalar", lambda s, pp=pp, ob=ob: s.activation(out=ob[:, 0:tqs], in_=pp[:, 0:tqs],
                                                                 func=AF.Identity, scale=0.125),
                  reads=[pp], writes=[ob])
            em.dma(k_o, k_o[kc * 128:(kc + 1) * 128, th * tqs:(th + 1) * tqs], ob, ob[:, 0:tqs])
    for i in range(nt):
        pp = psM if i % 2 == 0 else psM2
        ob = ob_a[i % 2]
        for c in range(8):
            em.op("tensor", lambda t, i=i, c=c, pp=pp: t.matmul(
                pp[:, 0:256], xn_fm[c][:, i * 128:(i + 1) * 128], wv_bf[c][:],
                start=(c == 0), stop=(c == 7)),
                reads=[xn_fm[c], wv_bf[c]], writes=[pp])
        em.op("scalar", lambda s, pp=pp, ob=ob: s.activation(out=ob[:, 0:256], in_=pp[:, 0:256],
                                                             func=AF.Identity),
              reads=[pp], writes=[ob])
        em.dma(v_o, v_o[i * 128:(i + 1) * 128, :], ob, ob[:, 0:256])
    em.pop()


def build_attn1(ntok=TOK):
    em = Em("attn1")
    em.cconst(0.0)
    t = {
        "x": em.din("x", (ntok, D), F32),
        "wq": em.din("wq", (D, D), F32),
        "wk": em.din("wk", (D, 256), F32),
        "wv": em.din("wv", (D, 256), F32),
        "ident": em.din("ident", (128, 128), BF16),
        "xn": em.dout("xn", (ntok, D), F32),
        "q": em.dout("q", (D, ntok), BF16),
        "k": em.dout("k", (256, ntok), BF16),
        "v": em.dout("v", (ntok, 256), BF16),
    }
    id_t = em.sb("id_t", (128, 128), BF16)
    em.dma(id_t, id_t[:], None, t["ident"][:])
    t["id_t"] = id_t
    emit_attn1(em, t, ntok)
    em.finish()
    return em


def emit_attn2(em, t, ntok=TOK, sk=2 * TOK):
    nt = ntok // 128
    ntk = sk // 128
    nsc = max(1, ntok // 512)
    scs = min(512, ntok)
    x = t["x"]; xn_i = t["xn"]; q_i = t["q"]; k_i = t["k"]; v_i = t["v"]
    wpost = t["wpost"]; id_t = t["id_t"]; y = t["y"]
    bscr = t["bscr"]; rd_dram = t["rd_dram"]
    em.push()
    qf = [em.sb(f"qf{c}", (128, ntok), BF16) for c in range(8)]
    for c in range(8):
        em.dma(qf[c], qf[c][:], None, q_i[c * 128:(c + 1) * 128, :])
    kdup = [em.sb(f"kdup{g}", (128, sk), BF16) for g in range(AG)]
    for g in range(AG):
        em.dma(kdup[g], kdup[g][0:64, :], None, k_i[g * 64:(g + 1) * 64, :])
        em.dma(kdup[g], kdup[g][64:128, :], None, k_i[g * 64:(g + 1) * 64, :])
    vaug = em.sb("vaug", (128, ntk * AG * 65), BF16)
    for g in range(AG):
        em.dma(vaug, vaug[:].rearrange("p (tkg e) -> p tkg e", e=65)[:, g::AG, 0:64],
               None, v_i[:].rearrange("(tk p) e -> p tk e", p=128)[:, :, g * 64:(g + 1) * 64])
    em.op("vector", lambda v: v.memset(
        vaug[:].rearrange("p (tk g e) -> p tk g e", tk=ntk, g=AG)[:, :, :, 64:65], 1.0),
        reads=[], writes=[vaug])
    af = [em.sb(f"af{c}", (128, ntok), BF16) for c in range(8)]
    negB = em.sb("negB", (128, 1), F32)

    em.push()
    sq = em.sb("sq", (128, 512), F32)
    ones2 = em.sb("ones2", (128, 2), F32)
    em.op("vector", lambda v: v.memset(ones2[0:64, :], 0.0), reads=[], writes=[ones2])
    em.op("vector", lambda v: v.memset(ones2[0:64, 0:1], 1.0), reads=[], writes=[ones2])
    em.op("vector", lambda v: v.memset(ones2[64:128, 0:1], 0.0), reads=[], writes=[ones2])
    em.op("vector", lambda v: v.memset(ones2[64:128, 1:2], 1.0), reads=[], writes=[ones2])
    nmx = em.sb("nmx", (2, 64), F32)
    psN = em.ps("psN", (128, 512), F32)
    col = 0
    col_q = 0
    for (src_, nchunks) in ((qf, ntok // 512 if ntok >= 512 else 1), (kdup, sk // 512 if sk >= 512 else 1)):
        nbuf = len(src_) if src_ is qf else AG
        for c in range(nbuf):
            for th in range(nchunks):
                w_ = min(512, (ntok if src_ is qf else sk))
                em.op("scalar", lambda s, c=c, th=th, src=src_, w_=w_: s.activation(
                    out=sq[:, 0:w_], in_=src[c][:, th * w_:(th + 1) * w_], func=AF.Square),
                    reads=[src_[c]], writes=[sq])
                em.op("tensor", lambda t, w_=w_: t.matmul(
                    psN[:2, 0:w_], ones2[:], sq[:, 0:w_], start=True, stop=True),
                    reads=[ones2, sq], writes=[psN])
                em.op("vector", lambda v, col=col, w_=w_: v.reduce_max(
                    out=nmx[:, col:col + 1], in_=psN[:2, 0:w_], axis=AX.X),
                    reads=[psN], writes=[nmx])
                col += 1
        if src_ is qf:
            col_q = col
    ncq = col_q
    nred = em.sb("nred", (2, 2), F32)
    em.op("vector", lambda v: v.reduce_max(out=nred[:, 0:1], in_=nmx[:, 0:ncq], axis=AX.X),
          reads=[nmx], writes=[nred])
    em.op("vector", lambda v: v.reduce_max(out=nred[:, 1:2], in_=nmx[:, ncq:col], axis=AX.X),
          reads=[nmx], writes=[nred])
    em.dma(bscr, bscr[0:1, 0:4], nred, nred[:, 0:2])
    brow = em.sb("brow", (1, 8), F32)
    em.dma(brow, brow[0:1, 0:4], bscr, bscr[0:1, 0:4])
    em.op("vector", lambda v: v.reduce_max(out=brow[:, 4:5],
                                           in_=brow[:, 0:4].rearrange("p (a b) -> p b a", b=2)[:, 0:1, :],
                                           axis=AX.X),
          reads=[brow], writes=[brow])
    em.op("vector", lambda v: v.reduce_max(out=brow[:, 5:6],
                                           in_=brow[:, 0:4].rearrange("p (a b) -> p b a", b=2)[:, 1:2, :],
                                           axis=AX.X),
          reads=[brow], writes=[brow])
    em.op("vector", lambda v: v.tensor_mul(brow[:, 6:7], brow[:, 4:5], brow[:, 5:6]),
          reads=[brow], writes=[brow])
    em.op("scalar", lambda s: s.activation(out=brow[:, 6:7], in_=brow[:, 6:7], func=AF.Sqrt),
          reads=[brow], writes=[brow])
    em.op("vector", lambda v: v.tensor_scalar(out=brow[:, 7:8], in0=brow[:, 6:7],
                                              scalar1=-1.05, scalar2=-0.05,
                                              op0=ALU.mult, op1=ALU.add),
          reads=[brow], writes=[brow])
    em.dma(bscr, bscr[0:1, 4:5], brow, brow[:, 7:8])
    em.dma(negB, negB[:], bscr, bscr[0:1, 4:5].to_broadcast([128, 1]))
    em.pop()

    em.push()
    scs2 = min(1024, ntok)
    nsc2 = ntok // scs2
    hw = scs2 // 2
    den = em.sb("den", (1, 1024), F32)
    eb = [em.sb(f"ebb{j}", (128, scs2), BF16) for j in range(2)]
    psE = [em.ps(f"psE{j}", (128, scs2), F32) for j in range(2)]
    psA = [[em.ps(f"psA{j}h{hf}", (65, hw), F32) for hf in range(2)] for j in range(2)]
    for h in range(AH):
        g = h // 4
        base = (h % 2) * 64
        qb = qf[h // 2]
        for sc in range(nsc2):
            pa = psA[sc % 2]
            q0 = sc * scs2

            def qk(tk, half, pe, h=h, g=g, base=base, qb=qb, q0=q0):
                em.op("tensor", lambda t, tk=tk, half=half, pe=pe: t.matmul(
                    pe[:, half * hw:(half + 1) * hw],
                    kdup[g][base:base + 64, tk * 128:(tk + 1) * 128],
                    qb[base:base + 64, q0 + half * hw:q0 + (half + 1) * hw],
                    start=True, stop=True),
                    reads=[kdup[g], qb], writes=[pe])

            qk(0, 0, psE[0]); qk(0, 1, psE[0])
            for tk in range(ntk):
                pe = psE[tk % 2]
                ebb = eb[tk % 2]
                em.op("scalar", lambda s, pe=pe, ebb=ebb: s.activation(
                    out=ebb[:], in_=pe[:], func=AF.Exp, bias=negB[:]),
                    reads=[pe, negB], writes=[ebb])
                if tk + 1 < ntk:
                    qk(tk + 1, 0, psE[(tk + 1) % 2]); qk(tk + 1, 1, psE[(tk + 1) % 2])
                for half in range(2):
                    em.op("tensor", lambda t, g=g, tk=tk, half=half, ebb=ebb, pa=pa: t.matmul(
                        pa[half][:],
                        vaug[:, (tk * AG + g) * 65:(tk * AG + g + 1) * 65],
                        ebb[:, half * hw:(half + 1) * hw],
                        start=(tk == 0), stop=(tk == ntk - 1)),
                        reads=[vaug, ebb], writes=[pa[half]])
            for half in range(2):
                em.op("vector", lambda v, h=h, base=base, q0=q0, half=half, pa=pa: v.tensor_copy(
                    out=af[h // 2][base:base + 64, q0 + half * hw:q0 + (half + 1) * hw],
                    in_=pa[half][0:64, :]),
                    reads=[pa[half]], writes=[af[h // 2]])
                em.op("vector", lambda v, half=half, pa=pa: v.tensor_copy(
                    out=den[0:1, half * hw:(half + 1) * hw], in_=pa[half][64:65, :]),
                    reads=[pa[half]], writes=[den])
            em.dma(rd_dram, rd_dram[h, q0:q0 + scs2].unsqueeze(0), den, den[0:1, 0:scs2])

    em.pop()

    em.push()
    rb = em.sb("rb", (128, ntok), F32)
    for c in range(8):
        em.dma(rb, rb[0:64, :], rd_dram, rd_dram[2 * c].unsqueeze(0).to_broadcast([64, ntok]))
        em.dma(rb, rb[64:128, :], rd_dram, rd_dram[2 * c + 1].unsqueeze(0).to_broadcast([64, ntok]))
        em.op("vector", lambda v: v.reciprocal(out=rb[:], in_=rb[:]), reads=[rb], writes=[rb])
        em.op("vector", lambda v, c=c: v.tensor_mul(af[c][:], af[c][:], rb[:]),
              reads=[af[c], rb], writes=[af[c]])
    wp_bf = [em.sb(f"wpb{c}", (128, D), BF16) for c in range(8)]
    for c in range(8):
        em.dma(wp_bf[c], wp_bf[c][:], None, wpost[c * 128:(c + 1) * 128, :], engine=_weng(wpost))
    xb = [em.sb(f"xb{j}", (128, D), F32) for j in range(2)]
    xnb = [em.sb(f"xnb{j}", (128, D), F32) for j in range(2)]
    yb = [em.sb(f"yb{j}", (128, D), F32) for j in range(2)]
    psO = em.ps("psO", (128, 1024), F32)
    for i in range(nt):
        xbb = xb[i % 2]
        xnn = xnb[i % 2]
        ybb = yb[i % 2]
        em.dma(xbb, xbb[:], None, x[i * 128:(i + 1) * 128, :])
        em.dma(xnn, xnn[:], None, xn_i[i * 128:(i + 1) * 128, :])
        for dh in range(2):
            for c in range(8):
                em.op("tensor", lambda t, i=i, dh=dh, c=c: t.matmul(
                    psO[:, dh * 512:(dh + 1) * 512], af[c][:, i * 128:(i + 1) * 128],
                    wp_bf[c][:, dh * 512:(dh + 1) * 512],
                    start=(c == 0), stop=(c == 7)),
                    reads=[af[c], wp_bf[c]], writes=[psO])
        em.op("vector", lambda v, xnn=xnn: v.tensor_add(xnn[:], xnn[:], psO[:]),
              reads=[xnn, psO], writes=[xnn])
        em.op("vector", lambda v, ybb=ybb, xbb=xbb, xnn=xnn: v.tensor_add(ybb[:], xnn[:], xbb[:]),
              reads=[xnn, xbb], writes=[ybb])
        em.dma(y, y[i * 128:(i + 1) * 128, :], ybb, ybb[:])
    em.pop()
    em.pop()


def build_attn2(ntok=TOK, sk=2 * TOK):
    em = Em("attn2")
    em.cconst(0.0)
    t = {
        "x": em.din("x", (ntok, D), F32),
        "xn": em.din("xn", (ntok, D), F32),
        "q": em.din("q", (D, ntok), BF16),
        "k": em.din("k", (256, sk), BF16),
        "v": em.din("v", (sk, 256), BF16),
        "wpost": em.din("wpost", (D, D), F32),
        "ident": em.din("ident", (128, 128), BF16),
        "y": em.dout("y", (ntok, D), F32),
        "bscr": em.dtmp("bscr", (1, 16), F32),
        "rd_dram": em.dtmp("rd_dram", (AH, ntok), F32),
    }
    id_t = em.sb("id_t", (128, 128), BF16)
    em.dma(id_t, id_t[:], None, t["ident"][:])
    t["id_t"] = id_t
    emit_attn2(em, t, ntok, sk)
    em.finish()
    return em


def attn_host_inputs1(x_shard, Wq, Wk, Wv):
    return {
        "x": np.ascontiguousarray(x_shard, dtype=np.float32),
        "wq": np.ascontiguousarray(Wq, dtype=np.float32),
        "wk": np.ascontiguousarray(Wk, dtype=np.float32),
        "wv": np.ascontiguousarray(Wv, dtype=np.float32),
        "ident": np.eye(128, dtype=bfd),
    }


def emit_m1(em, t, ntok=TOK):
    nt = ntok // 128
    ntc = max(1, ntok // 512)
    tqs = min(512, ntok)
    x = t["x"]; xhalo = t["xhalo"]; hmask = t["hmask"]; ln_g = t["ln_g"]; ln_b = t["ln_b"]
    w_in = t["w_in"]; conv_w = t["conv_w"]; conv_b = t["conv_b"]; w_x = t["w_x"]
    w_dt = t["w_dt"]; b_dt = t["b_dt"]; id_t = t["id_t"]
    xn1_o = t["xn1"]; xc_o = t["xc"]; z_o = t["z"]; dt_o = t["dt"]; bc_o = t["bc"]
    em.push()
    g_bc = em.sb("g_bc", (128, D), F32)
    b_bc = em.sb("b_bc", (128, D), F32)
    em.dma(g_bc, g_bc[:], None, ln_g[:].to_broadcast([128, D]))
    em.dma(b_bc, b_bc[:], None, ln_b[:].to_broadcast([128, D]))

    xh_fm = [em.sb(f"xhfm{k}", (128, 3 + ntok), BF16) for k in range(16)]
    em.push()
    xn2_fm = [em.sb(f"xn2fm{c}", (128, ntok), BF16) for c in range(8)]
    xn2h_fm = em.sb("xn2h_fm", (128, 24), BF16)
    em.push()
    xt = [em.sb(f"xt{i}", (128, D), F32) for i in range(2)]
    scr = em.sb("scr", (128, D), F32)
    stats = em.sb("stats", (128, 8), F32)
    xnt = em.sb("xnt", (128, D), F32)
    xn2t = em.sb("xn2t", (128, D), F32)
    xn2b = em.sb("xn2b", (128, D), BF16)
    psT = em.ps("psT", (128, 512), BF16)
    for i in range(nt):
        xb = xt[i % 2]
        em.dma(xb, xb[:], None, x[i * 128:(i + 1) * 128, :])
        emit_ln(em, xb, xnt, scr, stats)
        em.op("vector", lambda v: v.tensor_mul(xn2t[:], xnt[:], g_bc[:]),
              reads=[xnt, g_bc], writes=[xn2t])
        em.op("vector", lambda v: v.tensor_add(xn2t[:], xn2t[:], b_bc[:]),
              reads=[xn2t, b_bc], writes=[xn2t])
        em.dma(xn1_o, xn1_o[i * 128:(i + 1) * 128, :], xnt, xnt[:])
        em.op("vector", lambda v: v.tensor_copy(out=xn2b[:], in_=xn2t[:]),
              reads=[xn2t], writes=[xn2b])
        for c in range(8):
            emit_transpose128(em, xn2b, xn2b[:, c * 128:(c + 1) * 128],
                              psT, psT[:, 0:128],
                              xn2_fm[c], xn2_fm[c][:, i * 128:(i + 1) * 128], id_t)
    hx = em.sb("hx", (3, D), F32)
    hm = em.sb("hm", (3, 1), F32)
    hstats = em.sb("hstats", (3, 8), F32)
    hscr = em.sb("hscr", (3, D), F32)
    hxn = em.sb("hxn", (3, D), F32)
    hxb = em.sb("hxb", (3, D), BF16)
    em.dma(hx, hx[:], None, xhalo[:])
    em.dma(hm, hm[:], None, hmask[:])
    emit_ln(em, hx, hxn, hscr, hstats, width=D)
    em.op("vector", lambda v: v.tensor_mul(hxn[:], hxn[:], g_bc[0:3, :]),
          reads=[hxn, g_bc], writes=[hxn])
    em.op("vector", lambda v: v.tensor_add(hxn[:], hxn[:], b_bc[0:3, :]),
          reads=[hxn, b_bc], writes=[hxn])
    em.op("vector", lambda v: v.tensor_scalar(out=hxb[:], in0=hxn[:], scalar1=hm[:],
                                              scalar2=None, op0=ALU.mult),
          reads=[hxn, hm], writes=[hxb])
    psH = em.ps("psH", (128, 128), BF16)
    for c in range(8):
        em.op("tensor", lambda t, c=c: t.transpose(psH[:, 0:3], hxb[:, c * 128:(c + 1) * 128],
                                                   id_t[0:3, 0:3]),
              reads=[hxb, id_t], writes=[psH])
        em.op("vector", lambda v, c=c: v.tensor_copy(out=xn2h_fm[:, c * 3:(c + 1) * 3],
                                                     in_=psH[:, 0:3]),
              reads=[psH], writes=[xn2h_fm])
    em.pop()

    em.push()
    w_in_bf = [em.sb(f"winb{c}", (128, 2 * DI), BF16) for c in range(8)]
    for c in range(8):
        for hf in range(2):
            em.dma(w_in_bf[c], w_in_bf[c][:, hf * DI:(hf + 1) * DI], None,
                   w_in[c * 128:(c + 1) * 128, hf * DI:(hf + 1) * DI], engine=_weng(w_in))
    zb = em.sb("zbm", (128, 512), F32)
    psZ = em.ps("psZ", (128, 512), F32)
    psZ2 = em.ps("psZ2", (128, 512), F32)
    for oc in range(32):
        is_xh = oc < 16
        for th in range(ntc):
            pp = psZ if th % 2 == 0 else psZ2
            for c in range(8):
                em.op("tensor", lambda t, oc=oc, th=th, c=c, pp=pp: t.matmul(
                    pp[:, 0:tqs], w_in_bf[c][:, oc * 128:(oc + 1) * 128],
                    xn2_fm[c][:, th * tqs:(th + 1) * tqs],
                    start=(c == 0), stop=(c == 7)),
                    reads=[w_in_bf[c], xn2_fm[c]], writes=[pp])
            if is_xh:
                em.op("vector", lambda v, oc=oc, th=th, pp=pp: v.tensor_copy(
                    out=xh_fm[oc][:, 3 + th * tqs: 3 + (th + 1) * tqs], in_=pp[:, 0:tqs]),
                    reads=[pp], writes=[xh_fm[oc]])
            else:
                em.op("vector", lambda v, pp=pp: v.tensor_copy(out=zb[:, 0:tqs], in_=pp[:, 0:tqs]),
                      reads=[pp], writes=[zb])
                em.dma(z_o, z_o[(oc - 16) * 128:(oc - 15) * 128, th * tqs:(th + 1) * tqs],
                       zb, zb[:, 0:tqs])
        if is_xh:
            pp = psZ
            for c in range(8):
                em.op("tensor", lambda t, oc=oc, c=c, pp=pp: t.matmul(
                    pp[:, 0:3], w_in_bf[c][:, oc * 128:(oc + 1) * 128],
                    xn2h_fm[:, c * 3:(c + 1) * 3],
                    start=(c == 0), stop=(c == 7)),
                    reads=[w_in_bf[c], xn2h_fm], writes=[pp])
            em.op("vector", lambda v, oc=oc, pp=pp: v.tensor_copy(
                out=xh_fm[oc][:, 0:3], in_=pp[:, 0:3]), reads=[pp], writes=[xh_fm[oc]])
    em.pop()
    em.pop()

    xc_bf = [em.sb(f"xcbf{k}", (128, ntok), BF16) for k in range(16)]
    em.push()
    cw = em.sb("cw", (128, 16 * DCONV), F32)
    cb = em.sb("cb", (128, 16), F32)
    for k in range(16):
        em.dma(cw, cw[:, k * DCONV:(k + 1) * DCONV], None, conv_w[k * 128:(k + 1) * 128, :])
    em.dma(cb, cb[:], None, conv_b[:])
    acc = em.sb("acc", (128, TOK), F32)
    xcf = em.sb("xcf", (128, TOK), F32)
    for k in range(16):
        em.op("vector", lambda v, k=k: v.tensor_scalar(
            out=acc[:, 0:ntok], in0=xh_fm[k][:, 0:ntok], scalar1=cw[:, k * DCONV:k * DCONV + 1],
            scalar2=None, op0=ALU.mult),
            reads=[xh_fm[k], cw], writes=[acc])
        for tap in range(1, DCONV):
            em.op("vector", lambda v, k=k, tap=tap: v.scalar_tensor_tensor(
                out=acc[:, 0:ntok], in0=xh_fm[k][:, tap:tap + ntok],
                scalar=cw[:, k * DCONV + tap:k * DCONV + tap + 1],
                in1=acc[:, 0:ntok], op0=ALU.mult, op1=ALU.add),
                reads=[xh_fm[k], cw, acc], writes=[acc])
        em.op("scalar", lambda s, k=k: s.activation(out=xcf[:, 0:ntok], in_=acc[:, 0:ntok],
                                                    func=AF.Silu, bias=cb[:, k:k + 1]),
              reads=[acc, cb], writes=[xcf])
        em.dma(xc_o, xc_o[k * 128:(k + 1) * 128, :], xcf, xcf[:, 0:ntok])
        em.op("vector", lambda v, k=k: v.tensor_copy(out=xc_bf[k][:], in_=xcf[:, 0:ntok]),
              reads=[xcf], writes=[xc_bf[k]])
    em.pop()

    em.push()
    wx_bf = em.sb("wx_bf", (128, 16 * 96), BF16)
    for k in range(16):
        em.dma(wx_bf, wx_bf[:, k * 96:(k + 1) * 96], None,
               w_x[k * 128:(k + 1) * 128, :], engine="gpsimd")
    dbc = em.sb("dbc", (96, ntok), F32)
    dbc_bf = em.sb("dbc_bf", (96, ntok), BF16)
    psD = em.ps("psD", (128, 512), F32)
    for th in range(ntc):
        for k in range(16):
            em.op("tensor", lambda t, th=th, k=k: t.matmul(
                psD[0:96, 0:tqs], wx_bf[:, k * 96:(k + 1) * 96],
                xc_bf[k][:, th * tqs:(th + 1) * tqs], start=(k == 0), stop=(k == 15)),
                reads=[wx_bf, xc_bf[k]], writes=[psD])
        em.op("vector", lambda v, th=th: v.tensor_copy(out=dbc[:, th * tqs:(th + 1) * tqs],
                                                       in_=psD[0:96, 0:tqs]),
              reads=[psD], writes=[dbc])
    em.dma(bc_o, bc_o[:], dbc, dbc[64:96, :])
    em.op("vector", lambda v: v.tensor_copy(out=dbc_bf[:], in_=dbc[:]),
          reads=[dbc], writes=[dbc_bf])
    wdt_bf = em.sb("wdt_bf", (64, DI), BF16)
    em.dma(wdt_bf, wdt_bf[:], None, w_dt[:], engine="gpsimd")
    bdt_c = em.sb("bdt_c", (128, 16), F32)
    em.dma(bdt_c, bdt_c[:], None, b_dt[:])
    ut = em.sb("ut", (128, 512), F32)
    t1 = em.sb("t1m", (128, 512), F32)
    t2 = em.sb("t2m", (128, 512), F32)
    dtt = em.sb("dtt", (128, 512), F32)
    one_c = em.cconst(1.0)
    for dc in range(16):
        for th in range(ntc):
            em.op("tensor", lambda t, dc=dc, th=th: t.matmul(
                psD[:, 0:tqs], wdt_bf[:, dc * 128:(dc + 1) * 128],
                dbc_bf[0:64, th * tqs:(th + 1) * tqs], start=True, stop=True),
                reads=[wdt_bf, dbc_bf], writes=[psD])
            em.op("scalar", lambda s, dc=dc: s.activation(out=ut[:, 0:tqs], in_=psD[:, 0:tqs],
                                                          func=AF.Identity,
                                                          bias=bdt_c[:, dc:dc + 1]),
                  reads=[psD, bdt_c], writes=[ut])
            em.op("scalar", lambda s: s.activation(out=t1[:, 0:tqs], in_=ut[:, 0:tqs], func=AF.Abs),
                  reads=[ut], writes=[t1])
            em.op("scalar", lambda s: s.activation(out=t1[:, 0:tqs], in_=t1[:, 0:tqs], func=AF.Exp,
                                                   scale=-1.0),
                  reads=[t1], writes=[t1])
            em.op("scalar", lambda s: s.activation(out=t1[:, 0:tqs], in_=t1[:, 0:tqs], func=AF.Ln,
                                                   bias=one_c),
                  reads=[t1], writes=[t1])
            em.op("scalar", lambda s: s.activation(out=t2[:, 0:tqs], in_=ut[:, 0:tqs], func=AF.Relu),
                  reads=[ut], writes=[t2])
            em.op("vector", lambda v: v.tensor_add(dtt[:, 0:tqs], t1[:, 0:tqs], t2[:, 0:tqs]),
                  reads=[t1, t2], writes=[dtt])
            em.dma(dt_o, dt_o[dc * 128:(dc + 1) * 128, th * tqs:(th + 1) * tqs], dtt, dtt[:, 0:tqs])
    em.pop()
    em.pop()


def build_m1(ntok=TOK):
    em = Em("m1")
    em.cconst(0.0)
    t = {
        "x": em.din("x", (ntok, D), F32),
        "xhalo": em.din("xhalo", (3, D), F32),
        "hmask": em.din("hmask", (3, 1), F32),
        "ln_g": em.din("ln_g", (1, D), F32),
        "ln_b": em.din("ln_b", (1, D), F32),
        "w_in": em.din("w_in", (D, 2 * DI), F32),
        "conv_w": em.din("conv_w", (DI, DCONV), F32),
        "conv_b": em.din("conv_b", (128, 16), F32),
        "w_x": em.din("w_x", (DI, 96), F32),
        "w_dt": em.din("w_dt", (DTRANK, DI), F32),
        "b_dt": em.din("b_dt", (128, 16), F32),
        "ident": em.din("ident", (128, 128), BF16),
        "xn1": em.dout("xn1", (ntok, D), F32),
        "xc": em.dout("xc", (DI, ntok), F32),
        "z": em.dout("z", (DI, ntok), F32),
        "dt": em.dout("dt", (DI, ntok), F32),
        "bc": em.dout("bc", (32, ntok), F32),
    }
    id_t = em.sb("id_t", (128, 128), BF16)
    em.dma(id_t, id_t[:], None, t["ident"][:])
    t["id_t"] = id_t
    emit_m1(em, t, ntok)
    em.finish()
    return em


def emit_m2(em, t, nch=1024, slen=2 * TOK):
    ndb = nch // 128
    CB = min(512, slen)
    CB2 = min(1024, slen)
    ncb2 = slen // CB2
    xc_i = t["xc"]; dt_i = t["dt"]; b_i = t["bm"]; c_i = t["cm"]; alog = t["alog"]
    y_o = t["ym"]; bsel = t["bsel"]
    em.push()
    ab = em.sb("ab", (128, ndb * DSTATE), F32)
    for db in range(ndb):
        em.dma(ab, ab[:, db * DSTATE:(db + 1) * DSTATE], None,
               alog[db * 128:(db + 1) * 128, :])
    em.op("scalar", lambda s: s.activation(out=ab[:], in_=ab[:], func=AF.Exp, scale=1.0),
          reads=[ab], writes=[ab])
    em.op("vector", lambda v: v.tensor_scalar_mul(ab[:], ab[:], -1.0), reads=[ab], writes=[ab])
    zcol = em.sb("zcol", (128, 1), F32)
    em.op("vector", lambda v: v.memset(zcol[:], 0.0), reads=[], writes=[zcol])

    bselb = em.sb("bselb", (48, DSTATE * 128), BF16)
    em.dma(bselb, bselb[0:16, :], None, bsel[0:16, :])
    em.dma(bselb, bselb[32:48, :], None, bsel[32:48, :])
    brow = em.sb("brow", (48, slen), BF16)
    em.dma(brow, brow[0:16, :], None, b_i[:], engine="gpsimd")
    em.dma(brow, brow[32:48, :], None, c_i[:], engine="gpsimd")

    dtb = [em.sb(f"dtb{j}", (128, slen), F32) for j in range(2)]
    wxc = em.sb("wxc", (128, slen), F32)
    wb = [em.sb(f"wb{j}", (128, slen), BF16) for j in range(2)]
    bbc = [em.sb(f"bbc{j}", (128, slen), BF16) for j in range(2)]
    cbc = [em.sb(f"cbc{j}", (128, slen), BF16) for j in range(2)]
    da = [em.sb(f"da{j}", (128, slen), BF16) for j in range(2)]
    dbu = em.sb("dbu", (128, slen), BF16)
    hbb = em.sb("hb", (128, slen), BF16)
    ynb = em.sb("yn", (128, slen), BF16)
    yacc = [em.sb(f"yacc{j}", (128, slen), F32) for j in range(2)]
    psB = [em.ps(f"psB{j}", (128, CB2), F32) for j in range(2)]
    psC = [em.ps(f"psC{j}", (128, CB2), F32) for j in range(2)]
    for db in range(ndb):
        jd = db % 2
        em.dma(dtb[jd], dtb[jd][:], None, dt_i[db * 128:(db + 1) * 128, :])
        em.dma(wxc, wxc[:], None, xc_i[db * 128:(db + 1) * 128, :])
        em.op("vector", lambda v, jd=jd: v.tensor_mul(wb[jd][:], wxc[:], dtb[jd][:]),
              reads=[wxc, dtb[jd]], writes=[wb[jd]])
        for n in range(DSTATE):
            j = n % 2
            for cc in range(ncb2):
                pb, pc = psB[cc % 2], psC[cc % 2]
                for h in range(CB2 // CB):
                    em.op("tensor", lambda t_, n=n, cc=cc, h=h, pb=pb: t_.matmul(
                        pb[:, h * CB:(h + 1) * CB], bselb[0:16, n * 128:(n + 1) * 128],
                        brow[0:16, cc * CB2 + h * CB: cc * CB2 + (h + 1) * CB],
                        start=True, stop=True),
                        reads=[bselb, brow], writes=[pb])
                    em.op("tensor", lambda t_, n=n, cc=cc, h=h, pc=pc: t_.matmul(
                        pc[:, h * CB:(h + 1) * CB], bselb[32:48, n * 128:(n + 1) * 128],
                        brow[32:48, cc * CB2 + h * CB: cc * CB2 + (h + 1) * CB],
                        start=True, stop=True),
                        reads=[bselb, brow], writes=[pc])
                em.op("scalar", lambda s, cc=cc, j=j, pb=pb: s.activation(
                    out=bbc[j][:, cc * CB2:(cc + 1) * CB2], in_=pb[:], func=AF.Identity),
                    reads=[pb], writes=[bbc[j]])
                em.op("scalar", lambda s, cc=cc, j=j, pc=pc: s.activation(
                    out=cbc[j][:, cc * CB2:(cc + 1) * CB2], in_=pc[:], func=AF.Identity),
                    reads=[pc], writes=[cbc[j]])
            em.op("scalar", lambda s, db=db, n=n, j=j, jd=jd: s.activation(
                out=da[j][:], in_=dtb[jd][:], func=AF.Exp,
                scale=ab[:, db * DSTATE + n:db * DSTATE + n + 1]),
                reads=[dtb[jd], ab], writes=[da[j]])
            em.op("vector", lambda v, j=j, jd=jd: v.tensor_mul(
                dbu[:], wb[jd][:], bbc[j][:]), reads=[wb[jd], bbc[j]], writes=[dbu])
            em.op("vector", lambda v, j=j: v.tensor_tensor_scan(
                out=hbb[:], data0=da[j][:], data1=dbu[:], initial=zcol[:],
                op0=ALU.mult, op1=ALU.add),
                reads=[da[j], dbu, zcol], writes=[hbb])
            if n == 0:
                em.op("vector", lambda v, j=j, jd=jd: v.tensor_mul(
                    yacc[jd][:], hbb[:], cbc[j][:]), reads=[hbb, cbc[j]], writes=[yacc[jd]])
            else:
                em.op("vector", lambda v, j=j: v.tensor_mul(
                    ynb[:], hbb[:], cbc[j][:]), reads=[hbb, cbc[j]], writes=[ynb])
                em.op("vector", lambda v, jd=jd: v.tensor_add(
                    yacc[jd][:], yacc[jd][:], ynb[:]), reads=[yacc[jd], ynb], writes=[yacc[jd]])
        em.dma(y_o, y_o[db * 128:(db + 1) * 128, :], yacc[jd], yacc[jd][:])
    em.pop()


def mk_bsel():
    b = np.zeros((48, DSTATE * 128), dtype=bfd)
    for n in range(DSTATE):
        b[n, n * 128:(n + 1) * 128] = 1.0
        b[32 + n, n * 128:(n + 1) * 128] = 1.0
    return b


def build_m2(nch=1024, slen=2 * TOK):
    em = Em("m2")
    em.cconst(0.0)
    t = {
        "xc": em.din("xc", (nch, slen), F32),
        "dt": em.din("dt", (nch, slen), F32),
        "bm": em.din("bm", (DSTATE, slen), F32),
        "cm": em.din("cm", (DSTATE, slen), F32),
        "alog": em.din("alog", (nch, DSTATE), F32),
        "bsel": em.din("bsel", (48, DSTATE * 128), BF16),
        "ym": em.dout("y", (nch, slen), F32),
    }
    emit_m2(em, t, nch, slen)
    em.finish()
    return em


def emit_m3(em, t, ntok=TOK):
    nt = ntok // 128
    x = t["x"]; xn1_i = t["xn1"]; y_i = t["ym"]; xc_i = t["xc"]; z_i = t["zm"]
    dvec = t["dvec"]; w_out = t["w_out"]; y_o = t["y"]
    em.push()
    dcol = em.sb("dcol", (128, 16), F32)
    em.dma(dcol, dcol[:], None, dvec[:])
    wout_bf = em.sb("wout_bf", (128, 16 * D), BF16)
    for k in range(16):
        em.dma(wout_bf, wout_bf[:, k * D:(k + 1) * D], None,
               w_out[k * 128:(k + 1) * 128, :], engine=_weng(w_out))
    ygb = [em.sb(f"ygb{k}", (128, ntok), BF16) for k in range(16)]
    ybf = em.sb("ybf", (128, TOK), F32)
    xcb = em.sb("xcb", (128, TOK), F32)
    zbf = em.sb("zbf", (128, TOK), F32)
    for k in range(16):
        em.dma(ybf, ybf[:, 0:ntok], None, y_i[k * 128:(k + 1) * 128, :])
        em.dma(xcb, xcb[:, 0:ntok], None, xc_i[k * 128:(k + 1) * 128, :])
        em.dma(zbf, zbf[:, 0:ntok], None, z_i[k * 128:(k + 1) * 128, :])
        em.op("vector", lambda v, k=k: v.scalar_tensor_tensor(
            out=ybf[:, 0:ntok], in0=xcb[:, 0:ntok], scalar=dcol[:, k:k + 1],
            in1=ybf[:, 0:ntok], op0=ALU.mult, op1=ALU.add),
            reads=[xcb, dcol, ybf], writes=[ybf])
        em.op("scalar", lambda s: s.activation(out=zbf[:, 0:ntok], in_=zbf[:, 0:ntok],
                                               func=AF.Silu),
              reads=[zbf], writes=[zbf])
        em.op("vector", lambda v, k=k: v.tensor_mul(ygb[k][:], ybf[:, 0:ntok], zbf[:, 0:ntok]),
              reads=[ybf, zbf], writes=[ygb[k]])
    xb = [em.sb(f"xb{j}", (128, D), F32) for j in range(2)]
    xnb = [em.sb(f"xnb{j}", (128, D), F32) for j in range(2)]
    yb = [em.sb(f"yb{j}", (128, D), F32) for j in range(2)]
    psO = em.ps("psO", (128, 1024), F32)
    for i in range(nt):
        xbb, xnn, ybb = xb[i % 2], xnb[i % 2], yb[i % 2]
        em.dma(xbb, xbb[:], None, x[i * 128:(i + 1) * 128, :])
        em.dma(xnn, xnn[:], None, xn1_i[i * 128:(i + 1) * 128, :])
        for dh in range(2):
            for k in range(16):
                em.op("tensor", lambda t, i=i, dh=dh, k=k: t.matmul(
                    psO[:, dh * 512:(dh + 1) * 512], ygb[k][:, i * 128:(i + 1) * 128],
                    wout_bf[:, k * D + dh * 512:k * D + (dh + 1) * 512],
                    start=(k == 0), stop=(k == 15)),
                    reads=[ygb[k], wout_bf], writes=[psO])
        em.op("vector", lambda v, xnn=xnn: v.tensor_add(xnn[:], xnn[:], psO[:]),
              reads=[xnn, psO], writes=[xnn])
        em.op("vector", lambda v, ybb=ybb, xbb=xbb, xnn=xnn: v.tensor_add(ybb[:], xnn[:], xbb[:]),
              reads=[xnn, xbb], writes=[ybb])
        em.dma(y_o, y_o[i * 128:(i + 1) * 128, :], ybb, ybb[:])
    em.pop()


def build_m3(ntok=TOK):
    em = Em("m3")
    em.cconst(0.0)
    t = {
        "x": em.din("x", (ntok, D), F32),
        "xn1": em.din("xn1", (ntok, D), F32),
        "ym": em.din("ym", (DI, ntok), F32),
        "xc": em.din("xc", (DI, ntok), F32),
        "zm": em.din("zm", (DI, ntok), F32),
        "dvec": em.din("dvec", (128, 16), F32),
        "w_out": em.din("w_out", (DI, D), F32),
        "y": em.dout("y", (ntok, D), F32),
    }
    emit_m3(em, t, ntok)
    em.finish()
    return em


_EMS = {}


def _get(name, builder):
    if name not in _EMS:
        _EMS[name] = builder()
    return _EMS[name]


def _run8(em, maps):
    if em.const_vals:
        ca = em.consts_input()
        for m in maps:
            m["consts_arr"] = ca
    res = run_bass_kernel_spmd(em.nc, maps, core_ids=list(range(NCORES)))
    return res.results


def _peer_call(x8, W_up, W_down, W_q, keys, de, ue):
    em = _get("peer", lambda: build_peer(ntok=TOK))
    base = peer_host_inputs(np.zeros((1,)), W_up, W_down, W_q, keys, de, ue)
    del base["x"]
    maps = [dict(base, x=np.ascontiguousarray(x8[c], dtype=np.float32)) for c in range(NCORES)]
    res = _run8(em, maps)
    return np.stack([r["y"] for r in res])


def _qffn_call(x8, w_int8, ws, bias):
    em = _get("qffn", lambda: build_qffn(ntok=TOK))
    base = qffn_host_inputs(np.zeros((1,)), w_int8, ws, bias)
    del base["x"]
    maps = [dict(base, x=np.ascontiguousarray(x8[c], dtype=np.float32)) for c in range(NCORES)]
    res = _run8(em, maps)
    return np.stack([r["y"] for r in res])


def _mamba_call(x8, ln_g, ln_b, W_in, conv_w, conv_b, W_x, W_dt, b_dt, A_log, Dv, W_out):
    def cvt16(v):
        return np.ascontiguousarray(np.asarray(v, np.float32).reshape(16, 128).T)
    em1 = _get("m1", lambda: build_m1(ntok=TOK))
    maps = []
    for c in range(NCORES):
        if c % 2 == 0:
            halo = np.zeros((3, D), np.float32)
            hm = np.zeros((3, 1), np.float32)
        else:
            halo = np.ascontiguousarray(x8[c - 1][TOK - 3:TOK], dtype=np.float32)
            hm = np.ones((3, 1), np.float32)
        maps.append({
            "x": np.ascontiguousarray(x8[c], dtype=np.float32), "xhalo": halo, "hmask": hm,
            "ln_g": np.asarray(ln_g, np.float32).reshape(1, D),
            "ln_b": np.asarray(ln_b, np.float32).reshape(1, D),
            "w_in": np.ascontiguousarray(W_in, dtype=np.float32),
            "conv_w": np.ascontiguousarray(conv_w, dtype=np.float32),
            "conv_b": cvt16(conv_b),
            "w_x": np.ascontiguousarray(W_x, dtype=np.float32),
            "w_dt": np.ascontiguousarray(W_dt, dtype=np.float32),
            "b_dt": cvt16(b_dt),
            "ident": np.eye(128, dtype=bfd),
        })
    r1 = _run8(em1, maps)
    em2 = _get("m2", lambda: build_m2(nch=1024, slen=2 * TOK))
    alog = np.ascontiguousarray(A_log, dtype=np.float32)
    maps2 = []
    for c in range(NCORES):
        b, half = c // 2, c % 2
        xc_full = np.concatenate([r1[2 * b]["xc"], r1[2 * b + 1]["xc"]], axis=1)
        dt_full = np.concatenate([r1[2 * b]["dt"], r1[2 * b + 1]["dt"]], axis=1)
        bc_full = np.concatenate([r1[2 * b]["bc"], r1[2 * b + 1]["bc"]], axis=1)
        maps2.append({
            "xc": np.ascontiguousarray(xc_full[half * 1024:(half + 1) * 1024]),
            "dt": np.ascontiguousarray(dt_full[half * 1024:(half + 1) * 1024]),
            "bm": np.ascontiguousarray(bc_full[0:16]),
            "cm": np.ascontiguousarray(bc_full[16:32]),
            "alog": np.ascontiguousarray(alog[half * 1024:(half + 1) * 1024]),
        })
    r2 = _run8(em2, maps2)
    em3 = _get("m3", lambda: build_m3(ntok=TOK))
    maps3 = []
    for c in range(NCORES):
        b, half = c // 2, c % 2
        y_full = np.concatenate([r2[2 * b]["y"], r2[2 * b + 1]["y"]], axis=0)
        maps3.append({
            "x": np.ascontiguousarray(x8[c], dtype=np.float32),
            "xn1": r1[c]["xn1"],
            "ym": np.ascontiguousarray(y_full[:, half * TOK:(half + 1) * TOK]),
            "xc": r1[c]["xc"], "zm": r1[c]["z"],
            "dvec": cvt16(Dv),
            "w_out": np.ascontiguousarray(W_out, dtype=np.float32),
        })
    r3 = _run8(em3, maps3)
    return np.stack([r["y"] for r in r3])


def _attn_call(x8, Wq, Wk, Wv, Wpost):
    em1 = _get("attn1", lambda: build_attn1(ntok=TOK))
    base = attn_host_inputs1(np.zeros((1,)), Wq, Wk, Wv)
    del base["x"]
    maps = [dict(base, x=np.ascontiguousarray(x8[c], dtype=np.float32)) for c in range(NCORES)]
    r1 = _run8(em1, maps)
    em2 = _get("attn2", lambda: build_attn2(ntok=TOK, sk=2 * TOK))
    maps2 = []
    wpost_c = np.ascontiguousarray(Wpost, dtype=np.float32)
    for c in range(NCORES):
        b = c // 2
        k_full = np.concatenate([r1[2 * b]["k"], r1[2 * b + 1]["k"]], axis=1)
        v_full = np.concatenate([r1[2 * b]["v"], r1[2 * b + 1]["v"]], axis=0)
        maps2.append({
            "x": np.ascontiguousarray(x8[c], dtype=np.float32),
            "xn": r1[c]["xn"], "q": r1[c]["q"],
            "k": np.ascontiguousarray(k_full), "v": np.ascontiguousarray(v_full),
            "wpost": wpost_c, "ident": np.eye(128, dtype=bfd),
        })
    r2 = _run8(em2, maps2)
    return np.stack([r["y"] for r in r2])


def build_fused():
    em = Em("fused", rd=False)
    em.cconst(0.0)
    ntok, sk = TOK, 2 * TOK
    p = em.preg

    x0 = em.din("x0", (ntok, D), F32)
    ident = em.din("ident", (128, 128), BF16)
    y_out = em.dout("yout", (ntok, D), F32)
    id_t = em.sb("id_t", (128, 128), BF16)
    em.dma(id_t, id_t[:], None, ident[:])

    t_peer = peer_dins(em, ntok)
    t_peer["id_t"] = id_t
    t_qffn = {
        "w_int8T": em.din("w_int8T", (D, D), I32),
        "wscale_col": em.din("wscale_col", (128, 1), F32),
        "bias_row": em.din("bias_row", (1, D), F32),
        "ident": ident, "id_t": id_t,
    }
    t_m1 = {
        "hmask": em.din("hmask", (3, 1), F32),
        "ln_g": em.din("ln_g", (1, D), F32),
        "ln_b": em.din("ln_b", (1, D), F32),
        "w_in": em.din("w_in", (D, 2 * DI), F32),
        "conv_w": em.din("conv_w", (DI, DCONV), F32),
        "conv_b": em.din("conv_b", (128, 16), F32),
        "w_x": em.din("w_x", (DI, 96), F32),
        "w_dt": em.din("w_dt", (DTRANK, DI), F32),
        "b_dt": em.din("b_dt", (128, 16), F32),
        "id_t": id_t,
    }
    alog = em.din("alog", (1024, DSTATE), F32)
    bsel_d = em.din("bsel", (48, DSTATE * 128), BF16)
    t_attn = {
        "wq": em.din("wq", (D, D), F32),
        "wk": em.din("wk", (D, 256), F32),
        "wv": em.din("wv", (D, 256), F32),
        "wpost": em.din("wpost", (D, D), F32),
    }
    dvec = em.din("dvec", (128, 16), F32)
    w_out_m = em.din("w_out", (DI, D), F32)

    x1 = em.dtmp("x1", (ntok, D), F32)
    x2 = em.dtmp("x2", (ntok, D), F32)
    x3 = em.dtmp("x3", (ntok, D), F32)
    x4 = em.dtmp("x4", (ntok, D), F32)
    x5 = em.dtmp("x5", (ntok, D), F32)
    xn1_loc = em.dtmp("xn1_loc", (ntok, D), F32)
    z_loc = em.dtmp("z_loc", (DI, ntok), F32)
    xn_a_loc = em.dtmp("xn_a_loc", (ntok, D), F32)
    q_loc = em.dtmp("q_loc", (D, ntok), BF16)
    bscr = em.dtmp("bscr2", (1, 16), F32)
    rd_dram = em.dtmp("rd_dram2", (AH, ntok), F32)

    halo_sh = em.dshared("halo_sh", (2, 3, D), F32)
    xc_sh = em.dshared("xc_sh", (DI, sk), F32)
    dt_sh = em.dshared("dt_sh", (DI, sk), F32)
    bc_sh = em.dshared("bc_sh", (32, sk), F32)
    y_sh = em.dshared("y_sh", (DI, sk), F32)
    k_sh = em.dshared("k_sh", (256, sk), BF16)
    v_sh = em.dshared("v_sh", (sk, 256), BF16)

    mir = {}
    plan = []
    for nm, b, rows, cols in [
            ("deT", t_peer["deT"], D, NEXP),
            ("wupT", t_peer["wupT"], D, IFF), ("wdnT", t_peer["wdnT"], IFF, D)]:
        mir[nm] = em.dtmp(nm + "_bfm", (rows, cols), BF16)
        plan.append((b, mir[nm], rows, cols))
    t_peer.update(deT=mir["deT"], wupT=mir["wupT"], wdnT=mir["wdnT"])

    em.push()
    emit_peer(em, dict(t_peer, x=x0, y=x1), ntok,
              hook=lambda: emit_precast(em, plan))
    em.pop()
    em.fence()
    em.push()
    hb = em.sb("halo_b", (3, D), F32)
    em.dma(hb, hb[:], x1, x1[ntok - 3:ntok, :])
    em.dma(halo_sh, halo_sh[bass.ts(p, 1), :, :].squeeze(0), hb, hb[:])
    em.pop()
    em.barrier(protects=[halo_sh])

    em.push()
    emit_m1(em, dict(t_m1, x=x1,
                     xhalo=BufView(halo_sh, halo_sh.t[0]),
                     xn1=xn1_loc,
                     xc=BufView(xc_sh, xc_sh.t[:, bass.ts(p, ntok)]),
                     z=z_loc,
                     dt=BufView(dt_sh, dt_sh.t[:, bass.ts(p, ntok)]),
                     bc=BufView(bc_sh, bc_sh.t[:, bass.ts(p, ntok)])), ntok)
    em.pop()
    em.fence()
    em.barrier(protects=[xc_sh, dt_sh, bc_sh])
    em.push()
    emit_m2(em, dict(xc=BufView(xc_sh, xc_sh.t[bass.ts(p, 1024), :]),
                     dt=BufView(dt_sh, dt_sh.t[bass.ts(p, 1024), :]),
                     bm=BufView(bc_sh, bc_sh.t[0:16, :]),
                     cm=BufView(bc_sh, bc_sh.t[16:32, :]),
                     alog=alog, bsel=bsel_d,
                     ym=BufView(y_sh, y_sh.t[bass.ts(p, 1024), :])), 1024, sk)
    em.pop()
    em.fence()
    em.barrier(protects=[y_sh])
    em.push()
    emit_m3(em, dict(x=x1, xn1=xn1_loc,
                     ym=BufView(y_sh, y_sh.t[:, bass.ts(p, ntok)]),
                     xc=BufView(xc_sh, xc_sh.t[:, bass.ts(p, ntok)]),
                     zm=z_loc, dvec=dvec, w_out=w_out_m, y=x2), ntok)
    em.pop()
    em.fence()

    em.push()
    emit_peer(em, dict(t_peer, x=x2, y=x3), ntok)
    em.pop()
    em.fence()

    em.push()
    emit_qffn(em, dict(t_qffn, x=x3, y=x4), ntok)
    em.pop()
    em.fence()

    em.push()
    emit_attn1(em, dict(t_attn, id_t=id_t, x=x4, xn=xn_a_loc, q=q_loc,
                        k=BufView(k_sh, k_sh.t[:, bass.ts(p, ntok)]),
                        v=BufView(v_sh, v_sh.t[bass.ts(p, ntok), :])), ntok)
    em.pop()
    em.fence()
    em.barrier(protects=[k_sh, v_sh])
    em.push()
    emit_attn2(em, dict(t_attn, id_t=id_t, x=x4, xn=xn_a_loc, q=q_loc,
                        k=k_sh, v=v_sh, y=x5, bscr=bscr, rd_dram=rd_dram), ntok, sk)
    em.pop()
    em.fence()

    em.push()
    emit_qffn(em, dict(t_qffn, x=x5, y=y_out), ntok)
    em.pop()
    em.finish()
    return em


def kernel(L0_attn_x, L0_attn_ln_g, L0_attn_ln_b, L0_attn_W_in, L0_attn_conv_w,
           L0_attn_conv_b, L0_attn_W_x, L0_attn_W_dt, L0_attn_b_dt, L0_attn_A_log,
           L0_attn_D, L0_attn_W_out, L0_ffn_W_up, L0_ffn_W_down, L0_ffn_W_q,
           L0_ffn_keys, L0_ffn_down_embed, L0_ffn_up_embed, L1_attn_Wq, L1_attn_Wk,
           L1_attn_Wv, L1_attn_Wpost, L1_ffn_w_int8, L1_ffn_weight_scale, L1_ffn_bias):
    def cvt16(v):
        return np.ascontiguousarray(np.asarray(v, np.float32).reshape(16, 128).T)

    xf = np.asarray(L0_attn_x, dtype=np.float32).reshape(NCORES * TOK, D)
    em = _get("fused", build_fused)
    base = {
        "ident": np.eye(128, dtype=bfd),
        "wqT": np.ascontiguousarray(np.asarray(L0_ffn_W_q, np.float32).T),
        "keysP": np.ascontiguousarray(np.asarray(L0_ffn_keys, np.float32).transpose(2, 0, 3, 1)),
        "deT": np.ascontiguousarray(np.asarray(L0_ffn_down_embed, np.float32).T),
        "ue": np.ascontiguousarray(np.asarray(L0_ffn_up_embed, np.float32)),
        "wupT": np.ascontiguousarray(np.asarray(L0_ffn_W_up, np.float32).T),
        "wdnT": np.ascontiguousarray(np.asarray(L0_ffn_W_down, np.float32).T),
        "w_int8T": np.ascontiguousarray(np.asarray(L1_ffn_w_int8, np.int32).T),
        "wscale_col": np.ascontiguousarray(np.broadcast_to(
            np.asarray(L1_ffn_weight_scale, np.float32).reshape(1, 1), (128, 1))),
        "bias_row": np.ascontiguousarray(np.asarray(L1_ffn_bias, np.float32).reshape(1, D)),
        "ln_g": np.asarray(L0_attn_ln_g, np.float32).reshape(1, D),
        "ln_b": np.asarray(L0_attn_ln_b, np.float32).reshape(1, D),
        "w_in": np.ascontiguousarray(np.asarray(L0_attn_W_in, np.float32)),
        "conv_w": np.ascontiguousarray(np.asarray(L0_attn_conv_w, np.float32)),
        "conv_b": cvt16(L0_attn_conv_b),
        "w_x": np.ascontiguousarray(np.asarray(L0_attn_W_x, np.float32)),
        "w_dt": np.ascontiguousarray(np.asarray(L0_attn_W_dt, np.float32)),
        "b_dt": cvt16(L0_attn_b_dt),
        "wq": np.ascontiguousarray(np.asarray(L1_attn_Wq, np.float32)),
        "wk": np.ascontiguousarray(np.asarray(L1_attn_Wk, np.float32)),
        "wv": np.ascontiguousarray(np.asarray(L1_attn_Wv, np.float32)),
        "wpost": np.ascontiguousarray(np.asarray(L1_attn_Wpost, np.float32)),
        "dvec": cvt16(L0_attn_D),
        "w_out": np.ascontiguousarray(np.asarray(L0_attn_W_out, np.float32)),
        "bsel": mk_bsel(),
        "consts_arr": em.consts_input(),
    }
    alog_f = np.ascontiguousarray(np.asarray(L0_attn_A_log, np.float32))
    nonce = int(np.random.randint(1, 2 ** 30))
    barseq = (nonce + np.arange(16, dtype=np.int64)).astype(np.int32).reshape(1, 16)
    maps = []
    for c in range(NCORES):
        par = c % 2
        m = dict(base)
        m["x0"] = np.ascontiguousarray(xf[c * TOK:(c + 1) * TOK])
        m["par"] = np.array([[par, 1 - par]], np.int32)
        m["barseq"] = barseq
        m["hmask"] = (np.zeros((3, 1), np.float32) if par == 0
                      else np.ones((3, 1), np.float32))
        m["alog"] = np.ascontiguousarray(alog_f[par * 1024:(par + 1) * 1024])
        maps.append(m)
    res = run_bass_kernel_spmd(em.nc, maps, core_ids=list(range(NCORES)))
    out = np.stack([r["yout"] for r in res.results])
    return np.ascontiguousarray(out.reshape(4, 4096, D))
